# version 09
# baseline (speedup 1.0000x reference)
import math
from contextlib import ExitStack

import numpy as np
import concourse.bass as bass
import concourse.mybir as mybir

F32 = mybir.dt.float32
BF16 = mybir.dt.bfloat16
AF = mybir.ActivationFunctionType
ALU = mybir.AluOpType
AX = mybir.AxisListType

D = 1024
H = 16
DH = 64
HID = 4096
NMEM = 4
TOPK = 64
LN_EPS = 1e-5
ABS_EPS = 1e-6
NEG_BIG = -3.3895314e38
VK_SHORT = -30.0
QK_SCALE = 10.0

ENGS = ("sync", "tensor", "vector", "scalar", "gpsimd")
DMA_ENGS = ("sync", "gpsimd", "scalar", "vector")


class Prog:

    def __init__(self, nc, ctx):
        self.nc = nc
        self.ctx = ctx
        self.items = {e: [] for e in ENGS}
        self.esem = {e: ctx.enter_context(nc.semaphore(f"sem_{e}")) for e in ENGS}
        self.ecnt = {e: 0 for e in ENGS}
        self.dsem = {}
        self.last_w = {}
        self.readers = {}
        self.waited = {e: {} for e in ENGS}
        self.gdep = {}

    def _dma_sem(self, name, eng):
        if name not in self.dsem:
            self.dsem[name] = [self.ctx.enter_context(self.nc.semaphore(f"d_{name}")), 0, eng]
        return self.dsem[name]

    def _wait(self, eng, dep):
        kind, a, b = dep
        if kind == "d":
            rec = self.dsem[a]
            sem, val = rec[0], rec[1]
        else:
            sem, val = a, b
            if eng == "tensor" and sem is self.esem["tensor"]:
                return
        if self.waited[eng].get(sem, -1) >= val:
            return
        self.items[eng].append(("w", sem, val))
        self.waited[eng][sem] = val

    def retire_all(self):
        for m in (self.last_w, self.readers):
            for deps in m.values():
                for dep in deps:
                    kind, a, b = dep
                    if kind == "d":
                        rec = self.dsem[a]
                        sem, val = rec[0], rec[1]
                    else:
                        sem, val = a, b
                    if self.gdep.get(sem, -1) < val:
                        self.gdep[sem] = val
        self.last_w.clear()
        self.readers.clear()

    def op(self, eng, emit, reads=(), writes=(), dma=None):
        for sem, val in self.gdep.items():
            self._wait(eng, ("e", sem, val))
        for b in reads:
            for dep in self.last_w.get(b, ()):
                self._wait(eng, dep)
        for b in writes:
            for dep in self.last_w.get(b, ()):
                self._wait(eng, dep)
            for dep in self.readers.get(b, ()):
                self._wait(eng, dep)
        if dma is not None:
            assert len(writes) >= 1
            dma = "b_" + writes[0]
            rec = self._dma_sem(dma, eng)
            rec[1] += 16
            sem, n = rec[0], 16
            done = ("d", dma, rec[1])
        else:
            sem, n = self.esem[eng], 1
            self.ecnt[eng] += 1
            done = ("e", sem, self.ecnt[eng])
        self.items[eng].append(("op", emit, sem, n))
        for b in reads:
            self.readers.setdefault(b, []).append(done)
        for b in writes:
            self.last_w[b] = [done]
            self.readers[b] = []
        return done

    def barrier_deps(self):
        deps = [(self.esem[e], self.ecnt[e]) for e in ENGS if self.ecnt[e]]
        deps += [(rec[0], rec[1]) for rec in self.dsem.values() if rec[1]]
        return deps

    def flush(self, block):
        for e in ENGS:
            items = self.items[e]
            if not items:
                continue
            self.items[e] = []

            def emit_all(eng, items=items):
                for it in items:
                    if it[0] == "w":
                        eng.wait_ge(it[1], it[2])
                    else:
                        _, emitfn, sem, n = it
                        emitfn(eng).then_inc(sem, n)
            getattr(block, e)(emit_all)


class Cfg:
    def __init__(self, seq=4096, t_loc=2048, tok_blocks=None, debug_out=None):
        self.seq = seq
        self.t_loc = t_loc
        self.t_full = seq
        self.tok_blocks = tok_blocks or [(0, t_loc)]
        assert sum(l for _, l in self.tok_blocks) == t_loc
        self.no_exchange = (t_loc == seq)
        self.pair_blocks = None
        self.debug_out = debug_out
        self.n_ttiles = t_loc // 128
        self.n_jc = 1 + seq // 128
        self.jcols = 128 + seq

    def g_tok(self, l):
        off = 0
        for gs, ln in self.tok_blocks:
            if l < off + ln:
                return gs + (l - off)
            off += ln
        raise ValueError(l)

    def rank_blocks(self, r):
        blocks = self.pair_blocks[r]
        out, off = [], 0
        for gs, ln in blocks:
            out.append((gs, off, ln))
            off += ln
        return out

    def pos_local_chunk(self, m):
        if self.no_exchange:
            return m
        nloc = self.t_loc // 128
        return (m // 2) if m % 2 == 0 else nloc + m // 2

    def pos_global_chunk(self, m):
        if self.no_exchange:
            return m
        own = [gs // 128 for gs, _ in self.tok_blocks]
        all_ = set(range(self.seq // 128))
        part = sorted(all_ - set(own))
        return own[m // 2] if m % 2 == 0 else part[m // 2]

    def sched(self):
        NG = self.t_loc // 128
        out = []
        for k in range(NG):
            if self.no_exchange:
                ntc = k + 1
                mpos = {k + 1}
            else:
                ntc = 2 * k + 2
                mpos = {2 * k + 1, 2 * k + 2}
            out.append((ntc, mpos))
        return out

    def host_consts(self):
        import ml_dtypes
        ident = np.eye(128, dtype=np.float32)
        dmask = np.zeros((128, 128), dtype=np.float32)
        dmask[:, NMEM:] = NEG_BIG
        NG = self.t_loc // 128
        ivec = np.zeros((128, NG), np.float32)
        ss = np.zeros((128, 16), np.float32)
        for k in range(NG):
            g0 = self.g_tok(k * 128)
            ivec[:, k] = g0 + np.arange(128) // 16
        for b in range(16):
            g0 = self.g_tok(0)
            ss[:, b] = ((g0 + 8 * b + np.arange(128) // 16 + 5) < TOPK).astype(np.float32)
        njc = self.n_jc
        negbase = np.zeros((128, njc), np.float32)
        for m in range(njc - 1):
            negbase[:, m + 1] = -128.0 * self.pos_global_chunk(m)
        return {"ident_c": ident.astype(ml_dtypes.bfloat16),
                "dmask_c": dmask.astype(ml_dtypes.bfloat16),
                "shortsel_c": ss.astype(np.uint32),
                "ivec_c": ivec,
                "negbase_c": negbase,
                "colio_c": np.broadcast_to(np.arange(128, dtype=np.float32)[None, :],
                                           (128, 128)).copy()}


class Kern:

    def __init__(self, cfg: Cfg):
        self.cfg = cfg
        nc = self.nc = bass.Bass("TRN2", target_bir_lowering=False, debug=False)
        T, SEQ, JCOL = cfg.t_loc, cfg.seq, cfg.jcols
        self.din = {}

        def dt_in(name, shape, dtype=F32):
            self.din[name] = nc.dram_tensor(name, shape, dtype, kind="ExternalInput").ap()
            return self.din[name]

        self.x_in = dt_in("x_in", [cfg.t_full, D])
        self.W_in = {w: dt_in(w, [D, D]) for w in ("Wq", "Wk", "Wv", "Wg", "Wo")}
        self.bg_in = dt_in("bg", [D])
        self.Wpre_in = dt_in("Wpre", [H, H])
        self.Wpost_in = dt_in("Wpost", [H, H])
        self.memk_in = dt_in("memk", [H, NMEM, DH])
        self.memv_in = dt_in("memv", [H, NMEM, DH])
        self.hs_in = dt_in("hs", [H])
        self.Wfc_in = dt_in("Wfc", [HID, D])
        self.Wproj_in = dt_in("Wproj", [D, HID])
        self.lnw_in = dt_in("lnw", [HID])
        self.lnb_in = dt_in("lnb", [HID])
        self.ident_in = dt_in("ident_c", [128, 128], BF16)
        self.dmask_in = dt_in("dmask_c", [128, 128], BF16)
        self.shortsel_in = dt_in("shortsel_c", [128, 16], mybir.dt.uint32)
        self.ivec_in = dt_in("ivec_c", [128, cfg.t_loc // 128], F32)
        self.negbase_in = dt_in("negbase_c", [128, cfg.n_jc], F32)
        self.colio_in = dt_in("colio_c", [128, 128], F32)

        self.y_out = nc.dram_tensor("y_out", [T, D], F32, kind="ExternalOutput").ap()

        dram = lambda name, shape, dtype: nc.dram_tensor(name, shape, dtype, kind="Internal").ap()
        self.x1_d = dram("x1_d", [cfg.t_full, D], F32)
        self.x2_d = dram("x2_d", [T, D], F32)
        self.wfcqT_d = dram("wfcqT_d", [D, HID], BF16)
        self.wprojqT_d = dram("wprojqT_d", [HID, D], BF16)
        self.qnT_d = dram("qnT_d", [D, T], BF16)
        self.sgT_d = dram("sgT_d", [D, T], BF16)
        self.knT_own_d = dram("knT_own_d", [D, cfg.t_full], BF16)
        self.v_own_d = dram("v_own_d", [cfg.t_full, D], BF16)
        self.kT_d = dram("kT_d", [D, JCOL], BF16)
        self.v_d = dram("v_d", [JCOL, D], BF16)

        self.ctx = ExitStack()
        sbuf = self.sbuf = lambda name, shape, dtype: self.ctx.enter_context(
            nc.sbuf_tensor(name, shape, dtype))

        self.ident = sbuf("ident", [128, 128], BF16)
        self.ident_f = sbuf("ident_f", [128, 128], F32)
        self.dmask = sbuf("dmask", [128, 128], BF16)
        self.ones128_b = sbuf("ones128_b", [128, 1], BF16)
        self.ones128_f = sbuf("ones128_f", [128, 1], F32)
        self.onesrow_b = sbuf("onesrow_b", [1, 128], BF16)
        self.onesrow_f = sbuf("onesrow_f", [1, 128], F32)
        self.onesh_b = sbuf("onesh_b", [128, 2], BF16)
        self.onesh_bq = sbuf("onesh_bq", [128, 2], BF16)
        self.cpool = sbuf("cpool", [128, 4], F32)
        self.lnw_sb = sbuf("lnw_sb", [128, 32], F32)
        self.lnb_sb = sbuf("lnb_sb", [128, 32], F32)
        self.wpre_t = sbuf("wpre_t", [128, 128], BF16)
        self.wpost_t = sbuf("wpost_t", [128, 128], BF16)

        self.putil_ctx = ExitStack()
        self.putil = self.putil_ctx.enter_context(nc.psum_tensor("putil", [128, 128], F32))
        self.P = Prog(nc, self.ctx)
        self.block = self.ctx.enter_context(nc.Block())

    def load_vec_T(self, vec_in, n, dst, key, psum_ap=None, alloc=None):
        P, nc = self.P, self.nc
        nch = n // 128
        stk = key + "_stg"
        stg = (alloc or self.sbuf)(key + "_stg", [nch, 128], F32)
        pst = (psum_ap if psum_ap is not None else self.putil)[:, 0:nch]
        P.op("sync", lambda e: e.dma_start(stg[:], vec_in.rearrange("(c p) -> c p", p=128)),
             writes=[stk], dma="g")
        P.op("tensor", lambda e: e.transpose(pst, stg[:], self.ident_f[0:nch, 0:nch]),
             reads=[stk, "ident_f"], writes=["putil"])
        P.op("vector", lambda e: e.tensor_copy(dst[:], pst), reads=["putil"], writes=[key])

    def stage_consts(self):
        P, nc = self.P, self.nc
        P.op("sync", lambda e: e.dma_start(self.ident[:], self.ident_in), writes=["ident"], dma="g")
        P.op("sync", lambda e: e.dma_start(self.dmask[:], self.dmask_in),
             writes=["dmask"], dma="g")
        P.op("vector", lambda e: e.tensor_copy(self.ident_f[:], self.ident[:]),
             reads=["ident"], writes=["ident_f"])
        self.load_vec_T(self.lnw_in, 4096, self.lnw_sb, "lnw_sb")
        self.load_vec_T(self.lnb_in, 4096, self.lnb_sb, "lnb_sb")
        P.op("vector", lambda e: e.memset(self.ones128_b[:], 1.0), writes=["ones128_b"])
        P.op("vector", lambda e: e.memset(self.ones128_f[:], 1.0), writes=["ones128_f"])
        P.op("vector", lambda e: e.memset(self.onesrow_b[:], 1.0), writes=["onesrow_b"])
        P.op("vector", lambda e: e.memset(self.onesrow_f[:], 1.0), writes=["onesrow_f"])
        P.op("vector", lambda e: e.memset(self.cpool[:, 0:1], float(ABS_EPS)), writes=["cpool"])
        P.op("vector", lambda e: e.memset(self.cpool[:, 1:2], float(LN_EPS)), writes=["cpool"])
        P.op("vector", lambda e: e.memset(self.cpool[:, 2:3], float(np.log(QK_SCALE))), writes=["cpool"])
        P.op("vector", lambda e: e.memset(self.cpool[:, 3:4], float(VK_SHORT)), writes=["cpool"])
        P.op("vector", lambda e: e.memset(self.onesh_b[:], 0.0), writes=["onesh_b"])
        P.op("vector", lambda e: e.memset(self.onesh_b[0:64, 0:1], 1.0), writes=["onesh_b"])
        P.op("vector", lambda e: e.memset(self.onesh_b[64:128, 1:2], 1.0), writes=["onesh_b"])
        P.op("vector", lambda e: e.memset(self.onesh_bq[:], 0.0), writes=["onesh_bq"])
        P.op("vector", lambda e: e.memset(self.onesh_bq[0:64, 0:1], 0.1), writes=["onesh_bq"])
        P.op("vector", lambda e: e.memset(self.onesh_bq[64:128, 1:2], 0.1), writes=["onesh_bq"])

        wsm = self.sbuf("wsm", [16, 64], F32)
        wsm_b = self.sbuf("wsm_b", [16, 32], BF16)
        wsmT = self.wsmT = self.sbuf("wsmT", [16, 32], BF16)
        pm16 = self.putil[0:16, 0:16].bitcast(BF16)
        P.op("sync", lambda e: e.dma_start(wsm[:, 0:16], self.Wpre_in), writes=["wsm"], dma="g")
        P.op("sync", lambda e: e.dma_start(wsm[:, 16:32], self.Wpost_in), writes=["wsm"], dma="g")
        P.op("sync", lambda e: e.dma_start(wsm[0:16, 32:33], self.hs_in[:, None]), writes=["wsm"], dma="g")
        P.op("vector", lambda e: e.tensor_scalar(wsm[0:16, 48:64], wsm[0:16, 16:32],
                                                 wsm[0:16, 32:33], None, ALU.mult),
             reads=["wsm"], writes=["wsm48"])
        P.op("vector", lambda e: e.tensor_copy(wsm_b[:, 0:16], wsm[0:16, 0:16]),
             reads=["wsm"], writes=["wsm_b"])
        P.op("vector", lambda e: e.tensor_copy(wsm_b[:, 16:32], wsm[0:16, 48:64]),
             reads=["wsm48"], writes=["wsm_b"])
        P.op("tensor", lambda e: e.transpose(pm16[:, 0:16], wsm_b[:, 0:16], self.ident[0:16, 0:16]),
             reads=["wsm_b", "ident"], writes=["putil"])
        P.op("tensor", lambda e: e.transpose(pm16[:, 16:32], wsm_b[:, 16:32], self.ident[0:16, 0:16]),
             reads=["wsm_b", "ident"], writes=["putil"])
        P.op("vector", lambda e: e.tensor_copy(wsmT[:], pm16[:]), reads=["putil"], writes=["wsmT"])
        P.op("vector", lambda e: e.memset(self.wpre_t[:], 0.0), writes=["wpre_t"])
        P.op("vector", lambda e: e.memset(self.wpost_t[:], 0.0), writes=["wpost_t"])
        for io in range(8):
            P.op("gpsimd", lambda e, io=io: e.dma_start(
                    self.wpre_t[16 * io:16 * io + 16, 16 * io:16 * io + 16], wsmT[:, 0:16]),
                 reads=["wsmT"], writes=["wpre_t"], dma="g")
            P.op("gpsimd", lambda e, io=io: e.dma_start(
                    self.wpost_t[16 * io:16 * io + 16, 16 * io:16 * io + 16], wsmT[:, 16:32]),
                 reads=["wsmT"], writes=["wpost_t"], dma="g")

    def stage_quant(self, wfcqT_sb, wprojqT_sb, sub="all"):
        P, nc = self.P, self.nc
        self.qsub = sub
        qctx = ExitStack()
        qsbuf = lambda name, shape, dtype: qctx.enter_context(nc.sbuf_tensor(name, shape, dtype))
        qstg = [qsbuf(f"qstg{i}", [128, HID], F32) for i in range(2)]
        qsm = qsbuf("qsm", [128, 40], F32)
        qmask = qsbuf("qmask", [128, HID], F32)
        qbf = qsbuf("qbf", [128, HID], BF16)
        pq = self.putil[:, 64:72]
        pqt = qctx.enter_context(nc.psum_tensor("pqt", [128, 4096], BF16))

        def do_quant(w_in, n_rows, n_cols, tag, out_cb):
            nrc = n_rows // 128
            for c in range(nrc):
                s = qstg[c % 2]
                sk = f"qs{c % 2}"
                P.op("sync", lambda e, c=c, s=s: e.dma_start(
                        s[:, 0:n_cols], w_in[c * 128:(c + 1) * 128, :]), writes=[sk], dma="g")
                P.op("vector", lambda e, c=c, s=s: e.tensor_reduce(
                        qsm[:, c:c + 1], s[:, 0:n_cols], AX.X, ALU.add, apply_absolute_value=True),
                     reads=[sk], writes=["qsm"])
            P.op("vector", lambda e, nrc=nrc: e.tensor_reduce(
                    qsm[:, 36:37], qsm[:, 0:nrc], AX.X, ALU.add), reads=["qsm"], writes=["qsm36"])
            P.op("tensor", lambda e: e.matmul(pq[0:1, 0:1], qsm[:, 36:37], self.ones128_f[:],
                                              start=True, stop=True),
                 reads=["qsm36", "ones128_f"], writes=["putil"])
            P.op("scalar", lambda e, n=n_rows * n_cols: e.activation(
                    qsm[0:1, 37:38], pq[0:1, 0:1], AF.Identity, bias=self.cpool[0:1, 0:1], scale=1.0 / n),
                 reads=["putil", "cpool"], writes=["qsm37"])
            P.op("tensor", lambda e: e.matmul(pq[:, 1:2], self.onesrow_f[:], qsm[0:1, 37:38],
                                              start=True, stop=True),
                 reads=["qsm37", "onesrow_f"], writes=["putil"])
            if self.qsub == "absum":
                return
            P.op("vector", lambda e: e.tensor_copy(qsm[:, 38:39], pq[:, 1:2]),
                 reads=["putil"], writes=["gam0"])
            P.op("vector", lambda e: e.reciprocal(qsm[:, 39:40], qsm[:, 38:39]),
                 reads=["gam0"], writes=["gam"])
            for c in range(nrc):
                s = qstg[c % 2]
                sk = f"qs{c % 2}"
                P.op("sync", lambda e, c=c, s=s: e.dma_start(
                        s[:, 0:n_cols], w_in[c * 128:(c + 1) * 128, :]), writes=[sk], dma="g")
                P.op("vector", lambda e, s=s, n_cols=n_cols: e.tensor_scalar(
                        s[:, 0:n_cols], s[:, 0:n_cols], qsm[:, 39:40], None, ALU.mult),
                     reads=[sk, "gam"], writes=[sk])
                P.op("vector", lambda e, s=s, n_cols=n_cols: e.tensor_scalar(
                        qmask[:, 0:n_cols], s[:, 0:n_cols], 0.5, None, ALU.is_gt),
                     reads=[sk], writes=["qmask"])
                P.op("vector", lambda e, s=s, n_cols=n_cols: e.tensor_scalar(
                        s[:, 0:n_cols], s[:, 0:n_cols], -0.5, None, ALU.is_lt),
                     reads=[sk], writes=[sk])
                P.op("vector", lambda e, s=s, n_cols=n_cols: e.tensor_sub(
                        qbf[:, 0:n_cols], qmask[:, 0:n_cols], s[:, 0:n_cols]),
                     reads=["qmask", sk], writes=["qbf"])
                if self.qsub != "noq":
                    out_cb(c)

        def fc_cb(c):
            for dmc in range(8):
                pslot = dmc % 4
                P.op("tensor", lambda e, dmc=dmc, pslot=pslot: e.transpose(
                        pqt[:, pslot * 1024:pslot * 1024 + 128],
                        qbf[:, dmc * 128:(dmc + 1) * 128], self.ident[:]),
                     reads=["qbf", "ident"], writes=[f"pqt{pslot}"])
                P.op("vector", lambda e, c=c, dmc=dmc, pslot=pslot: e.tensor_copy(
                        wfcqT_sb[:, dmc * HID + c * 128: dmc * HID + (c + 1) * 128],
                        pqt[:, pslot * 1024:pslot * 1024 + 128]),
                     reads=[f"pqt{pslot}"], writes=["wfcqT_sb"])
        do_quant(self.Wfc_in, HID, D, "fc", fc_cb)
        for dmc in range(8):
            P.op("gpsimd", lambda e, dmc=dmc: e.dma_start(
                    self.wfcqT_d[dmc * 128:(dmc + 1) * 128, :],
                    wfcqT_sb[:, dmc * HID:(dmc + 1) * HID]),
                 reads=["wfcqT_sb"], writes=["wfcqT_d"], dma="g")

        def pj_cb(c):
            for hc in range(32):
                pslot = hc % 4
                P.op("tensor", lambda e, hc=hc, pslot=pslot: e.transpose(
                        pqt[:, pslot * 1024:pslot * 1024 + 128],
                        qbf[:, hc * 128:(hc + 1) * 128], self.ident[:]),
                     reads=["qbf", "ident"], writes=[f"pqt{pslot}"])
                P.op("vector", lambda e, c=c, hc=hc, pslot=pslot: e.tensor_copy(
                        wprojqT_sb[:, hc * D + c * 128: hc * D + (c + 1) * 128],
                        pqt[:, pslot * 1024:pslot * 1024 + 128]),
                     reads=[f"pqt{pslot}"], writes=["wprojqT_sb"])
        do_quant(self.Wproj_in, D, HID, "pj", pj_cb)
        for hc in range(32):
            P.op("gpsimd", lambda e, hc=hc: e.dma_start(
                    self.wprojqT_d[hc * 128:(hc + 1) * 128, :],
                    wprojqT_sb[:, hc * D:(hc + 1) * D]),
                 reads=["wprojqT_sb"], writes=["wprojqT_d"], dma="g")
        P.retire_all()
        P.flush(self.block)
        qctx.close()

    def ln_tile(self, x_ap, xf, fslice, stat, scol, keys, out_bf, oslice):
        P = self.P
        xk, stk, outk = keys
        P.op("sync", lambda e: e.dma_start(xf[:, fslice], x_ap), writes=[xk], dma="g")
        P.op("vector", lambda e: e.tensor_scalar(
                out_bf[:, oslice], xf[:, fslice], -1.0 / D, None, ALU.mult,
                op1=ALU.add, accum_out=stat[:, scol:scol + 1]),
             reads=[xk], writes=[stk, outk])
        P.op("vector", lambda e: e.tensor_scalar(
                xf[:, fslice], xf[:, fslice], stat[:, scol:scol + 1], None, ALU.add),
             reads=[xk, stk], writes=[xk])
        P.op("scalar", lambda e: e.activation(
                out_bf[:, oslice], xf[:, fslice], AF.Square, accum_out=stat[:, scol + 1:scol + 2]),
             reads=[xk], writes=[stk, outk])
        P.op("scalar", lambda e: e.activation(
                stat[:, scol + 1:scol + 2], stat[:, scol + 1:scol + 2], AF.Ln,
                scale=1.0 / D, bias=self.cpool[:, 1:2]),
             reads=[stk, "cpool"], writes=[stk])
        P.op("scalar", lambda e: e.activation(
                stat[:, scol + 1:scol + 2], stat[:, scol + 1:scol + 2], AF.Exp, scale=-0.5),
             reads=[stk], writes=[stk])
        P.op("vector", lambda e: e.tensor_scalar(
                out_bf[:, oslice], xf[:, fslice], stat[:, scol + 1:scol + 2], None, ALU.mult),
             reads=[xk, stk], writes=[outk])

    def stage_ffn(self, x_src_d, x_dst_d, wfcqT_sb, wprojqT_sb, tag, scale_weights=True,
                  n_tok=None):
        P, nc, cfg = self.P, self.nc, self.cfg
        T = n_tok if n_tok is not None else cfg.t_loc
        TB = min(256, T)
        NTB = TB // 128
        NBLK = T // TB
        with ExitStack() as fstk:
            en = fstk.enter_context
            xf = en(nc.sbuf_tensor(f"xf_{tag}", [128, D], F32))
            xr = en(nc.sbuf_tensor(f"xr_{tag}", [128, D], F32))
            stat = en(nc.sbuf_tensor(f"st_{tag}", [128, 8], F32))
            lnxb = en(nc.sbuf_tensor(f"lnx_{tag}", [128, D], BF16))
            lnxT = en(nc.sbuf_tensor(f"lnxT_{tag}", [128, 8 * TB], BF16))
            gh = en(nc.sbuf_tensor(f"gh_{tag}", [128, 32 * TB], BF16))
            gh2 = en(nc.sbuf_tensor(f"gh2_{tag}", [128, TB], BF16))
            tstat = en(nc.sbuf_tensor(f"tstat_{tag}", [1, 2 * TB], F32))
            bcf = en(nc.sbuf_tensor(f"bcf_{tag}", [128, 2 * TB], F32))
            BC = en(nc.sbuf_tensor(f"BC_{tag}", [128, 18], F32))
            bhl = en(nc.sbuf_tensor(f"bhl_{tag}", [128, 2], BF16))
            yTs = en(nc.sbuf_tensor(f"yTs_{tag}", [128, TB], F32))
            yU = en(nc.sbuf_tensor(f"yU_{tag}", [128, TB], F32))
            ytok = en(nc.sbuf_tensor(f"ytok_{tag}", [128, NTB * D], F32))
            pfA = en(nc.psum_tensor(f"pfA_{tag}", [128, 512], F32))
            pfB = en(nc.psum_tensor(f"pfB_{tag}", [128, 512], F32))
            pstA = en(nc.psum_tensor(f"pstA_{tag}", [1, 512], F32))
            pstB = en(nc.psum_tensor(f"pstB_{tag}", [1, 512], F32))
            pxT = en(nc.psum_tensor(f"pxT_{tag}", [128, 1024], BF16))
            ptr = en(nc.psum_tensor(f"ptr_{tag}", [128, 512], F32))
            pyt = en(nc.psum_tensor(f"py_{tag}", [128, 512], F32))
            pBC = en(nc.psum_tensor(f"pBC_{tag}", [128, 512], F32))

            for dmc in range(8):
                for hc in range(32):
                    P.op("vector", lambda e, hc=hc: e.tensor_copy(bhl[:, 0:1], self.lnb_sb[:, hc:hc + 1]),
                         reads=["lnb_sb"], writes=[f"{tag}bhl"])
                    P.op("vector", lambda e, hc=hc: e.tensor_sub(stat[:, 4:5], self.lnb_sb[:, hc:hc + 1],
                                                                 bhl[:, 0:1]),
                         reads=["lnb_sb", f"{tag}bhl"], writes=[f"{tag}blo"])
                    P.op("vector", lambda e: e.tensor_copy(bhl[:, 1:2], stat[:, 4:5]),
                         reads=[f"{tag}blo"], writes=[f"{tag}bhl"])
                    P.op("tensor", lambda e, hc=hc, dmc=dmc: e.matmul(
                            pBC[:, 0:2], wprojqT_sb[:, hc * D + dmc * 128: hc * D + (dmc + 1) * 128],
                            bhl[:], start=(hc == 0), stop=(hc == 31)),
                         reads=["wprojqT_sb", f"{tag}bhl"], writes=[f"{tag}pBC"])
                P.op("vector", lambda e, dmc=dmc: e.tensor_copy(BC[:, 16:18], pBC[:, 0:2]),
                     reads=[f"{tag}pBC"], writes=[f"{tag}BCs"])
                P.op("vector", lambda e, dmc=dmc: e.tensor_add(BC[:, 8 + dmc:9 + dmc],
                                                               BC[:, 16:17], BC[:, 17:18]),
                     reads=[f"{tag}BCs"], writes=[f"{tag}BC"])
            if scale_weights:
                for hc in range(32):
                    P.op("vector", lambda e, hc=hc: e.tensor_scalar(
                            wprojqT_sb[:, hc * D:(hc + 1) * D], wprojqT_sb[:, hc * D:(hc + 1) * D],
                            self.lnw_sb[:, hc:hc + 1], None, ALU.mult),
                         reads=["wprojqT_sb", "lnw_sb"], writes=["wprojqT_sb"])
            for dmc in range(8):
                for hc in range(32):
                    P.op("tensor", lambda e, hc=hc, dmc=dmc: e.matmul(
                            pBC[:, 0:1], wprojqT_sb[:, hc * D + dmc * 128: hc * D + (dmc + 1) * 128],
                            self.ones128_b[:], start=(hc == 0), stop=(hc == 31)),
                         reads=["wprojqT_sb", "ones128_b"], writes=[f"{tag}pBC"])
                P.op("vector", lambda e, dmc=dmc: e.tensor_copy(BC[:, dmc:dmc + 1], pBC[:, 0:1]),
                     reads=[f"{tag}pBC"], writes=[f"{tag}BC"])

            for blk in range(NBLK):
                t0 = blk * TB
                for tt in range(NTB):
                    tok = t0 + tt * 128
                    self.ln_tile(x_src_d[tok:tok + 128, :], xf, slice(0, D), stat, 0,
                                 (f"{tag}xf", f"{tag}stat", f"{tag}lnxb"), lnxb, slice(0, D))
                    for dmc in range(8):
                        P.op("tensor", lambda e, dmc=dmc: e.transpose(
                                pxT[:, 0:128],
                                lnxb[:, dmc * 128:dmc * 128 + 128], self.ident[:]),
                             reads=[f"{tag}lnxb", "ident"], writes=[f"{tag}pxT"])
                        P.op("vector", lambda e, tt=tt, dmc=dmc: e.tensor_copy(
                                lnxT[:, dmc * TB + tt * 128: dmc * TB + (tt + 1) * 128],
                                pxT[:, 0:128]),
                             reads=[f"{tag}pxT"], writes=[f"{tag}lnxT"])

                for s in range(32):
                    pcurt, pk = (pfA, f"{tag}pfA") if s % 2 == 0 else (pfB, f"{tag}pfB")
                    pcur = pcurt[:, 0:TB]
                    for dmc in range(8):
                        P.op("tensor", lambda e, s=s, dmc=dmc, pcur=pcur: e.matmul(
                                pcur, wfcqT_sb[:, dmc * HID + s * 128: dmc * HID + (s + 1) * 128],
                                lnxT[:, dmc * TB:(dmc + 1) * TB],
                                start=(dmc == 0), stop=(dmc == 7)),
                             reads=["wfcqT_sb", f"{tag}lnxT"], writes=[pk])
                    P.op("scalar", lambda e, s=s, pcur=pcur: e.activation(
                            gh[:, s * TB:(s + 1) * TB], pcur, AF.Gelu),
                         reads=[pk], writes=[f"{tag}gh{s}"])
                    P.op("scalar", lambda e, s=s: e.activation(
                            gh2[:], gh[:, s * TB:(s + 1) * TB], AF.Square),
                         reads=[f"{tag}gh{s}"], writes=[f"{tag}gh2"])
                    P.op("tensor", lambda e, s=s: e.matmul(
                            pstA[0:1, 0:TB], self.ones128_b[:], gh[:, s * TB:(s + 1) * TB],
                            start=(s == 0), stop=(s == 31)),
                         reads=[f"{tag}gh{s}", "ones128_b"], writes=[f"{tag}pstA"])
                    P.op("tensor", lambda e, s=s: e.matmul(
                            pstB[0:1, 0:TB], self.ones128_b[:], gh2[:],
                            start=(s == 0), stop=(s == 31)),
                         reads=[f"{tag}gh2", "ones128_b"], writes=[f"{tag}pstB"])

                P.op("scalar", lambda e: e.activation(tstat[0:1, 0:TB], pstA[0:1, 0:TB],
                                                      AF.Identity, scale=1.0 / HID),
                     reads=[f"{tag}pstA"], writes=[f"{tag}mu"])
                P.op("scalar", lambda e: e.activation(tstat[0:1, TB:2 * TB], pstB[0:1, 0:TB],
                                                      AF.Identity, scale=1.0 / HID),
                     reads=[f"{tag}pstB"], writes=[f"{tag}ex2"])
                P.op("vector", lambda e: e.tensor_mul(yU[0:1, 0:TB], tstat[0:1, 0:TB],
                                                      tstat[0:1, 0:TB]),
                     reads=[f"{tag}mu"], writes=[f"{tag}mu2"])
                P.op("vector", lambda e: e.tensor_sub(tstat[0:1, TB:2 * TB], tstat[0:1, TB:2 * TB],
                                                      yU[0:1, 0:TB]),
                     reads=[f"{tag}ex2", f"{tag}mu2"], writes=[f"{tag}var"])
                P.op("scalar", lambda e: e.activation(tstat[0:1, TB:2 * TB], tstat[0:1, TB:2 * TB],
                                                      AF.Ln, bias=self.cpool[0:1, 1:2]),
                     reads=[f"{tag}var", "cpool"], writes=[f"{tag}var"])
                P.op("scalar", lambda e: e.activation(tstat[0:1, TB:2 * TB], tstat[0:1, TB:2 * TB],
                                                      AF.Exp, scale=-0.5),
                     reads=[f"{tag}var"], writes=[f"{tag}var"])
                P.op("vector", lambda e: e.tensor_scalar(
                        tstat[0:1, 0:TB], tstat[0:1, 0:TB], -1.0, None, ALU.mult),
                     reads=[f"{tag}mu"], writes=[f"{tag}mu"])
                P.op("vector", lambda e: e.tensor_mul(tstat[0:1, 0:TB], tstat[0:1, 0:TB],
                                                      tstat[0:1, TB:2 * TB]),
                     reads=[f"{tag}mu", f"{tag}var"], writes=[f"{tag}mrs"])
                P.op("tensor", lambda e: e.matmul(ptr[:, 0:TB], self.onesrow_f[:],
                                                  tstat[0:1, TB:2 * TB], start=True, stop=True),
                     reads=[f"{tag}var", "onesrow_f"], writes=[f"{tag}ptr"])
                P.op("vector", lambda e: e.tensor_copy(bcf[:, 0:TB], ptr[:, 0:TB]),
                     reads=[f"{tag}ptr"], writes=[f"{tag}bcA"])
                P.op("tensor", lambda e: e.matmul(ptr[:, 0:TB], self.onesrow_f[:],
                                                  tstat[0:1, 0:TB], start=True, stop=True),
                     reads=[f"{tag}mrs", "onesrow_f"], writes=[f"{tag}ptr"])
                P.op("vector", lambda e: e.tensor_copy(bcf[:, TB:2 * TB], ptr[:, 0:TB]),
                     reads=[f"{tag}ptr"], writes=[f"{tag}bcB"])

                for dmc in range(8):
                    for s in range(32):
                        P.op("tensor", lambda e, s=s, dmc=dmc: e.matmul(
                                pyt[:, 0:TB], wprojqT_sb[:, s * D + dmc * 128: s * D + (dmc + 1) * 128],
                                gh[:, s * TB:(s + 1) * TB],
                                start=(s == 0), stop=(s == 31)),
                             reads=["wprojqT_sb", f"{tag}gh{s}"], writes=[f"{tag}py"])
                    P.op("vector", lambda e, dmc=dmc: e.scalar_tensor_tensor(
                            yTs[:], pyt[:, 0:TB], 1.0, bcf[:, 0:TB], ALU.mult, ALU.mult),
                         reads=[f"{tag}py", f"{tag}bcA"], writes=[f"{tag}yTs"])
                    P.op("vector", lambda e, dmc=dmc: e.tensor_scalar(
                            yU[:], bcf[:, TB:2 * TB], BC[:, dmc:dmc + 1], BC[:, 8 + dmc:9 + dmc],
                            ALU.mult, op1=ALU.add),
                         reads=[f"{tag}bcB", f"{tag}BC"], writes=[f"{tag}yU"])
                    P.op("vector", lambda e, dmc=dmc: e.tensor_add(yTs[:], yTs[:], yU[:]),
                         reads=[f"{tag}yTs", f"{tag}yU"], writes=[f"{tag}yTs"])
                    for tt in range(NTB):
                        P.op("tensor", lambda e, tt=tt: e.transpose(
                                ptr[:, 0:128],
                                yTs[:, tt * 128:(tt + 1) * 128], self.ident_f[:]),
                             reads=[f"{tag}yTs", "ident_f"], writes=[f"{tag}ptr"])
                        P.op("vector", lambda e, tt=tt, dmc=dmc: e.tensor_copy(
                                ytok[:, tt * D + dmc * 128: tt * D + dmc * 128 + 128],
                                ptr[:, 0:128]),
                             reads=[f"{tag}ptr"], writes=[f"{tag}ytok{tt}"])
                for tt in range(NTB):
                    tok = t0 + tt * 128
                    P.op("sync", lambda e, tok=tok: e.dma_start(xr[:], x_src_d[tok:tok + 128, :]),
                         writes=[f"{tag}xr"], dma="g")
                    P.op("vector", lambda e, tt=tt: e.scalar_tensor_tensor(
                            xr[:], ytok[:, tt * D:(tt + 1) * D], 0.5, xr[:],
                            ALU.mult, ALU.add),
                         reads=[f"{tag}ytok{tt}", f"{tag}xr"], writes=[f"{tag}xr"])
                    P.op("gpsimd", lambda e, tok=tok: e.dma_start(x_dst_d[tok:tok + 128, :], xr[:]),
                         reads=[f"{tag}xr"], writes=["x_dst" + tag], dma="g")
            P.retire_all()
            P.flush(self.block)


    def stage_proj(self):
        P, nc, cfg = self.P, self.nc, self.cfg
        T = cfg.t_full
        TB = min(512, cfg.t_loc)
        NBLK = T // TB
        NBLK_OWN = cfg.t_loc // TB
        with ExitStack() as stk:
            en = stk.enter_context
            wsb = {}
            for w in ("Wq", "Wk", "Wv", "Wg"):
                wsb[w] = en(nc.sbuf_tensor(f"wsb_{w}", [128, 8 * D], BF16))
            stg = en(nc.sbuf_tensor("pj_stg", [128, D], F32))
            bgsb = en(nc.sbuf_tensor("pj_bg", [128, 8], F32))
            xf = en(nc.sbuf_tensor("pj_xf", [128, D], F32))
            stat = en(nc.sbuf_tensor("pj_stat", [128, 8], F32))
            lnxb = en(nc.sbuf_tensor("pj_lnx", [128, D], BF16))
            lnxT = en(nc.sbuf_tensor("pj_lnxT", [128, 8 * TB], BF16))
            sq = en(nc.sbuf_tensor("pj_sq", [128, 512], BF16))
            qraw = en(nc.sbuf_tensor("pj_qraw", [128, 512], F32))
            nrm = en(nc.sbuf_tensor("pj_nrm", [2, 1024], F32))
            onesh_f = en(nc.sbuf_tensor("pj_onesh_f", [2, 128], F32))
            outb = en(nc.sbuf_tensor("pj_outb", [128, 512], BF16))
            mk = en(nc.sbuf_tensor("pj_mk", [64, 64], F32))
            mkb = en(nc.sbuf_tensor("pj_mkb", [64, 64], BF16))
            mkn = en(nc.sbuf_tensor("pj_mkn", [64, 1], F32))
            zpad = en(nc.sbuf_tensor("pj_zpad", [128, 1024], BF16))
            ppj = en(nc.psum_tensor("ppjA", [128, 512], F32))
            ppj2 = en(nc.psum_tensor("ppjB", [128, 512], F32))
            pxT = en(nc.psum_tensor("pj_pxT", [128, 1024], BF16))
            pnrm = en(nc.psum_tensor("pj_pnrm", [2, 512], F32))
            pnb = en(nc.psum_tensor("pj_pnb", [128, 512], F32))
            pmk = en(nc.psum_tensor("pj_pmk", [64, 1024], BF16))

            oshT = en(nc.sbuf_tensor("pj_oshT", [128, 2], F32))
            P.op("vector", lambda e: e.memset(oshT[:], 0.0), writes=["oshT"])
            P.op("vector", lambda e: e.memset(oshT[0:64, 0:1], 1.0), writes=["oshT"])
            P.op("vector", lambda e: e.memset(oshT[64:128, 1:2], 1.0), writes=["oshT"])
            P.op("tensor", lambda e: e.transpose(pnrm[0:2, 0:128], oshT[:], self.ident_f[:]),
                 reads=["oshT", "ident_f"], writes=["pj_pnrm"])
            P.op("vector", lambda e: e.tensor_copy(onesh_f[:], pnrm[0:2, 0:128]),
                 reads=["pj_pnrm"], writes=["onesh_f"])
            P.op("vector", lambda e: e.memset(zpad[:], 0.0), writes=["zpad"])
            P.op("vector", lambda e: e.memset(self.cpool[:, 0:1], float(ABS_EPS)), writes=["cpool"])
            self.load_vec_T(self.bg_in, D, bgsb, "pj_bg", psum_ap=pnb, alloc=lambda n_, s_, d_: en(nc.sbuf_tensor(n_, s_, d_)))
            for w in ("Wq", "Wk", "Wv", "Wg"):
                for dmc in range(8):
                    P.op("sync", lambda e, w=w, dmc=dmc: e.dma_start(
                            stg[:], self.W_in[w][dmc * 128:(dmc + 1) * 128, :]),
                         writes=["pj_stg"], dma="g")
                    P.op("vector", lambda e, w=w, dmc=dmc: e.tensor_copy(
                            wsb[w][:, dmc * D:(dmc + 1) * D], stg[:]),
                         reads=["pj_stg"], writes=[f"wsb_{w}"])

            P.op("sync", lambda e: e.dma_start(mk[:], self.memk_in.rearrange("g s d -> (g s) d")),
                 writes=["pj_mk"], dma="g")
            P.op("scalar", lambda e: e.activation(mkb[:], mk[:], AF.Square,
                                                  accum_out=mkn[:]),
                 reads=["pj_mk"], writes=["pj_mkn", "pj_mkb"])
            P.op("scalar", lambda e: e.activation(mkn[:], mkn[:], AF.Ln, bias=self.cpool[0:64, 0:1],
                                                  scale=1.0),
                 reads=["pj_mkn", "cpool"], writes=["pj_mkn"])
            P.op("scalar", lambda e: e.activation(mkn[:], mkn[:], AF.Exp, scale=-0.5),
                 reads=["pj_mkn"], writes=["pj_mkn"])
            P.op("vector", lambda e: e.tensor_scalar(mkb[:], mk[:], mkn[:], None, ALU.mult),
                 reads=["pj_mk", "pj_mkn"], writes=["pj_mkb"])
            P.op("tensor", lambda e: e.transpose(pmk[:, 0:64], mkb[:], self.ident[0:64, 0:64]),
                 reads=["pj_mkb", "ident"], writes=["pj_pmk"])
            P.op("vector", lambda e: e.tensor_copy(mkb[:], pmk[:, 0:64]), reads=["pj_pmk"], writes=["pj_mkb"])
            P.op("gpsimd", lambda e: e.dma_start(
                    self.kT_d.rearrange("(g d) c -> d g c", d=DH)[:, :, 0:NMEM],
                    mkb[:].rearrange("d (g s) -> d g s", s=NMEM)),
                 reads=["pj_mkb"], writes=["kT_d"], dma="g")
            P.op("gpsimd", lambda e: e.dma_start(
                    self.v_d[0:NMEM, :].rearrange("s (g d) -> s g d", d=DH),
                    self.memv_in.rearrange("g s d -> s g d")),
                 writes=["v_d"], dma="g")
            for c in range(8):
                P.op("gpsimd", lambda e, c=c: e.dma_start(
                        self.kT_d[c * 128:(c + 1) * 128, NMEM:128], zpad[:, 0:128 - NMEM]),
                     reads=["zpad"], writes=["kT_d"], dma="g")
            P.op("gpsimd", lambda e: e.dma_start(self.v_d[NMEM:128, :], zpad[0:128 - NMEM, :]),
                 reads=["zpad"], writes=["v_d"], dma="g")

            def norm_head_slice(hds, src_key, out_dram, qscale):
                P.op("scalar", lambda e: e.activation(sq[:, 0:TB], ppj[:, 0:TB], AF.Square),
                     reads=["ppjA"], writes=["pj_sq"])
                P.op("scalar", lambda e: e.activation(qraw[:, 0:TB], ppj[:, 0:TB], AF.Identity),
                     reads=["ppjA"], writes=["pj_qraw"])
                P.op("tensor", lambda e: e.matmul(pnrm[0:2, 0:TB], self.onesh_b[:], sq[:, 0:TB],
                                                  start=True, stop=True),
                     reads=["onesh_b", "pj_sq"], writes=["pj_pnrm"])
                P.op("vector", lambda e: e.tensor_scalar(nrm[:, 0:TB], pnrm[0:2, 0:TB], 1e-24, None,
                                                         ALU.max),
                     reads=["pj_pnrm"], writes=["pj_nrm"])
                P.op("scalar", lambda e: e.activation(nrm[:, 0:TB], nrm[:, 0:TB], AF.Ln),
                     reads=["pj_nrm"], writes=["pj_nrm"])
                if qscale != 1.0:
                    P.op("scalar", lambda e: e.activation(
                            nrm[:, 0:TB], nrm[:, 0:TB], AF.Exp, scale=-0.5,
                            bias=self.cpool[0:2, 2:3]),
                         reads=["pj_nrm", "cpool"], writes=["pj_nrm"])
                else:
                    P.op("scalar", lambda e: e.activation(
                            nrm[:, 0:TB], nrm[:, 0:TB], AF.Exp, scale=-0.5),
                         reads=["pj_nrm"], writes=["pj_nrm"])
                P.op("tensor", lambda e: e.matmul(pnb[:, 0:TB], onesh_f[:], nrm[:, 0:TB],
                                                  start=True, stop=True),
                     reads=["onesh_f", "pj_nrm"], writes=["pj_pnb"])
                P.op("vector", lambda e: e.tensor_mul(outb[:, 0:TB], qraw[:, 0:TB], pnb[:, 0:TB]),
                     reads=["pj_qraw", "pj_pnb"], writes=["pj_outb"])
                P.op("gpsimd", lambda e: e.dma_start(out_dram, outb[:, 0:TB]),
                     reads=["pj_outb"], writes=[src_key + "_d"], dma="g")

            for blk in range(NBLK):
                t0 = blk * TB
                for tt in range(TB // 128):
                    tok = t0 + tt * 128
                    self.ln_tile(self.x1_d[tok:tok + 128, :], xf, slice(0, D), stat, 0,
                                 ("pj_xf", "pj_stat", "pj_lnxb"), lnxb, slice(0, D))
                    for dmc in range(8):
                        P.op("tensor", lambda e, dmc=dmc: e.transpose(
                                pxT[:, 0:128],
                                lnxb[:, dmc * 128:dmc * 128 + 128], self.ident[:]),
                             reads=["pj_lnxb", "ident"], writes=["pj_pxT"])
                        P.op("vector", lambda e, tt=tt, dmc=dmc: e.tensor_copy(
                                lnxT[:, dmc * TB + tt * 128: dmc * TB + (tt + 1) * 128],
                                pxT[:, 0:128]),
                             reads=["pj_pxT"], writes=["pj_lnxT"])

                for w, out_d, qs in (("Wq", self.qnT_d, QK_SCALE), ("Wk", self.knT_own_d, 1.0)):
                    if w == "Wq" and blk >= NBLK_OWN:
                        continue
                    for hds in range(8):
                        for dmc in range(8):
                            P.op("tensor", lambda e, w=w, hds=hds, dmc=dmc: e.matmul(
                                    ppj[:, 0:TB], wsb[w][:, dmc * D + hds * 128: dmc * D + (hds + 1) * 128],
                                    lnxT[:, dmc * TB:(dmc + 1) * TB],
                                    start=(dmc == 0), stop=(dmc == 7)),
                                 reads=[f"wsb_{w}", "pj_lnxT"], writes=["ppjA"])
                        norm_head_slice(hds, w, out_d[hds * 128:(hds + 1) * 128, t0:t0 + TB], qs)
                for tt in range(TB // 128):
                    tok = t0 + tt * 128
                    for vh in range(2):
                        for dmc in range(8):
                            P.op("tensor", lambda e, tt=tt, dmc=dmc, vh=vh: e.matmul(
                                    ppj2[:], lnxT[:, dmc * TB + tt * 128: dmc * TB + (tt + 1) * 128],
                                    wsb["Wv"][:, dmc * D + vh * 512: dmc * D + vh * 512 + 512],
                                    start=(dmc == 0), stop=(dmc == 7)),
                                 reads=["wsb_Wv", "pj_lnxT"], writes=["ppjB"])
                        P.op("vector", lambda e, vh=vh: e.tensor_copy(outb[:, 0:512], ppj2[:]),
                             reads=["ppjB"], writes=["pj_outb"])
                        P.op("gpsimd", lambda e, tok=tok, vh=vh: e.dma_start(
                                self.v_own_d[tok:tok + 128, vh * 512:vh * 512 + 512], outb[:, 0:512]),
                             reads=["pj_outb"], writes=["v_own_d"], dma="g")
                for hds in range(8):
                    if blk >= NBLK_OWN:
                        break
                    for dmc in range(8):
                        P.op("tensor", lambda e, hds=hds, dmc=dmc: e.matmul(
                                ppj[:, 0:TB], wsb["Wg"][:, dmc * D + hds * 128: dmc * D + (hds + 1) * 128],
                                lnxT[:, dmc * TB:(dmc + 1) * TB],
                                start=(dmc == 0), stop=(dmc == 7)),
                             reads=["wsb_Wg", "pj_lnxT"], writes=["ppjA"])
                    P.op("scalar", lambda e, hds=hds: e.activation(
                            outb[:, 0:TB], ppj[:, 0:TB], AF.Sigmoid, bias=bgsb[:, hds:hds + 1]),
                         reads=["ppjA", "pj_bg"], writes=["pj_outb"])
                    P.op("gpsimd", lambda e, hds=hds, t0=t0: e.dma_start(
                            self.sgT_d[hds * 128:(hds + 1) * 128, t0:t0 + TB], outb[:, 0:TB]),
                         reads=["pj_outb"], writes=["sgT_d"], dma="g")
            P.retire_all()
            P.flush(self.block)

    def stage_kv_assemble(self):
        P, nc, cfg = self.P, self.nc, self.cfg
        for m in range(cfg.seq // 128):
            lc = cfg.pos_local_chunk(m)
            P.op("gpsimd", lambda e, m=m, lc=lc: e.dma_start(
                    self.kT_d[:, 128 + m * 128:128 + (m + 1) * 128],
                    self.knT_own_d[:, lc * 128:(lc + 1) * 128]),
                 reads=["Wk_d"], writes=["kT_d"], dma="g")
            P.op("gpsimd", lambda e, m=m, lc=lc: e.dma_start(
                    self.v_d[128 + m * 128:128 + (m + 1) * 128, :],
                    self.v_own_d[lc * 128:(lc + 1) * 128, :]),
                 reads=["v_own_d"], writes=["v_d"], dma="g")

    def stage_attn(self, asub="full"):
        P, nc, cfg = self.P, self.nc, self.cfg
        self.asub = asub
        T = cfg.t_loc
        JCOL = cfg.jcols
        NG = T // 128
        with ExitStack() as stk:
            en = stk.enter_context
            db = [en(nc.sbuf_tensor(f"db{b}", [128, JCOL], BF16)) for b in range(16)]
            esb = en(nc.sbuf_tensor("esb", [128, JCOL], BF16))
            sc0 = en(nc.sbuf_tensor("sc0", [128, JCOL], BF16))
            sc1 = esb
            qn_sb = en(nc.sbuf_tensor("qn_sb", [64, 16 * 128], BF16))
            kc_sb = [en(nc.sbuf_tensor(f"kc{i}", [64, 16 * 128], BF16)) for i in range(2)]
            vc_sb = [en(nc.sbuf_tensor("vc0", [128, 1024], BF16))] * 2
            z_sb = en(nc.sbuf_tensor("z_sb", [128, 2048], BF16))
            mr_sb = [en(nc.sbuf_tensor(f"mr{i}", [128, 128], BF16)) for i in range(2)]
            pt2_sb = [en(nc.sbuf_tensor("pt20", [128, 2048], BF16))] * 2
            m8t = en(nc.sbuf_tensor("m8t", [128, 8], BF16))
            vks = en(nc.sbuf_tensor("vks", [128, 64], F32))
            wscl = [en(nc.sbuf_tensor(f"wscl{b}", [128, 128], BF16)) for b in range(16)]
            wpost2 = en(nc.sbuf_tensor("wpost2", [128, 128], BF16))
            out2sb = en(nc.sbuf_tensor("out2sb", [128, 1024], F32))
            sg_sb = en(nc.sbuf_tensor("sg_sb", [128, 1024], BF16))
            gt_sb = sg_sb
            wo_sb = en(nc.sbuf_tensor("wo_sb", [128, 8 * D], BF16))
            yTs = en(nc.sbuf_tensor("at_yTs", [128, 128], F32))
            ytok = out2sb
            xr = en(nc.sbuf_tensor("at_xr", [128, D], F32))
            shortsel = en(nc.sbuf_tensor("shortsel", [128, 16], mybir.dt.uint32))
            ivec_sb = en(nc.sbuf_tensor("ivec_sb", [128, NG], F32))
            negbase_sb = en(nc.sbuf_tensor("negbase_sb", [128, cfg.n_jc], F32))
            colio_sb = en(nc.sbuf_tensor("colio_sb", [128, 128], F32))
            ivs = en(nc.sbuf_tensor("ivs", [128, 2], F32))
            msk = en(nc.sbuf_tensor("msk", [128, 128], BF16))
            pbig = en(nc.psum_tensor("pbig", [128, 512], F32))
            pgat = en(nc.psum_tensor("pgat", [128, 1024], BF16))
            ppre = en(nc.psum_tensor("ppre", [128, 512], F32))
            ppost = en(nc.psum_tensor("ppost", [128, 512], F32))
            pout2 = en(nc.psum_tensor("pout2", [128, 1024], F32))
            pwo = en(nc.psum_tensor("pwo", [128, 512], F32))
            ptw = en(nc.psum_tensor("ptw", [128, 512], F32))

            P.op("sync", lambda e: e.dma_start(shortsel[:], self.shortsel_in),
                 writes=["shortsel"], dma="g")
            P.op("sync", lambda e: e.dma_start(ivec_sb[:], self.ivec_in),
                 writes=["ivec_sb"], dma="g")
            P.op("sync", lambda e: e.dma_start(negbase_sb[:], self.negbase_in),
                 writes=["negbase_sb"], dma="g")
            P.op("sync", lambda e: e.dma_start(colio_sb[:], self.colio_in),
                 writes=["colio_sb"], dma="g")
            P.op("vector", lambda e: e.tensor_copy(
                    wpost2[:].rearrange("p (h i) -> p h i", i=8),
                    self.wpre_t[:].rearrange("p (i h) -> p h i", h=16) if False else
                    self.wpost_t[:].rearrange("p (i h) -> p h i", h=16)),
                 reads=["wpost_t"], writes=["wpost2"])
            for hdc in range(8):
                P.op("sync", lambda e, hdc=hdc: e.dma_start(
                        xr[:], self.W_in["Wo"][hdc * 128:(hdc + 1) * 128, :]),
                     writes=["at_xr"], dma="g")
                P.op("vector", lambda e, hdc=hdc: e.tensor_copy(
                        wo_sb[:, hdc * D:(hdc + 1) * D], xr[:]),
                     reads=["at_xr"], writes=["wo_sb"])

            kT_v = self.kT_d.rearrange("(g p) j -> p g j", p=64)
            qnT_v = self.qnT_d.rearrange("(g p) t -> p g t", p=64)

            sched = cfg.sched()
            for grp in range(NG):
                ntc, mpos = sched[grp]
                jcs = list(range(0, ntc + 1))
                valid = (ntc + 1) * 128

                for hds in range(4):
                    P.op("sync", lambda e, hds=hds, grp=grp: e.dma_start(
                            qn_sb[:].rearrange("p (g t) -> p g t", g=16)[:, hds * 4:(hds + 1) * 4, :],
                            qnT_v[:, hds * 4:(hds + 1) * 4, grp * 128:(grp + 1) * 128]),
                         reads=["Wq_d"], writes=["qn_sb"], dma="g")

                for jc in jcs:
                    kb = kc_sb[jc % 2]
                    kk = f"kc{jc % 2}"
                    P.op("sync", lambda e, jc=jc, kb=kb: e.dma_start(
                            kb[:].rearrange("p (g j) -> p g j", g=16),
                            kT_v[:, :, jc * 128:(jc + 1) * 128]),
                         reads=["kT_d"], writes=[kk], dma="g")
                    for gq in range(4):
                        for gl in range(4):
                            g = gq * 4 + gl
                            P.op("tensor", lambda e, g=g, gl=gl, kb=kb: e.matmul(
                                    pbig[:, gl * 128:(gl + 1) * 128],
                                    kb[0:64, g * 128:(g + 1) * 128],
                                    qn_sb[0:64, g * 128:(g + 1) * 128],
                                    start=True, stop=True),
                                 reads=[kk, "qn_sb"], writes=["pbig"])
                        P.op("scalar", lambda e, gq=gq: e.activation(
                                z_sb[:].rearrange("p (i g) -> p g i", g=16)[:, gq * 4:(gq + 1) * 4, :],
                                pbig[:], AF.Identity),
                             reads=["pbig"], writes=["z_sb"])
                    for b in range(16):
                        if self.asub == "A1":
                            break
                        sl = b % 2
                        P.op("tensor", lambda e, b=b: e.transpose(
                                pgat[:, 0:128],
                                z_sb[:, b * 128:(b + 1) * 128], self.ident[:]),
                             reads=["z_sb", "ident"], writes=["pgat"])
                        P.op("vector", lambda e, b=b, sl=sl: e.tensor_copy(
                                mr_sb[sl][:], pgat[:, 0:128]),
                             reads=["pgat"], writes=[f"mr{sl}"])
                        if self.asub == "A2":
                            continue
                        P.op("tensor", lambda e, b=b, sl=sl: e.matmul(
                                ppre[:, 0:128], self.wpre_t[:], mr_sb[sl][:],
                                start=True, stop=True),
                             reads=["wpre_t", f"mr{sl}"], writes=["ppre"])
                        if jc == 0:
                            P.op("vector", lambda e: e.tensor_tensor(
                                    ppre[:, 0:128], ppre[:, 0:128],
                                    self.dmask[:], ALU.add),
                                 reads=["ppre", "dmask"], writes=["ppre"])
                        if jc in mpos:
                            off = float(8 * b)
                            P.op("vector", lambda e, grp=grp, off=off, jc=jc: e.tensor_scalar(
                                    ivs[:, 0:1], ivec_sb[:, grp:grp + 1],
                                    negbase_sb[:, jc:jc + 1], off, ALU.add, op1=ALU.add),
                                 reads=["ivec_sb", "negbase_sb"], writes=["ivs"])
                            P.op("vector", lambda e: e.tensor_scalar(
                                    msk[:], colio_sb[:], ivs[:, 0:1], None, ALU.is_gt),
                                 reads=["colio_sb", "ivs"], writes=["msk"])
                            P.op("vector", lambda e: e.scalar_tensor_tensor(
                                    ppre[:, 0:128], msk[:], NEG_BIG,
                                    ppre[:, 0:128], ALU.mult, ALU.add),
                                 reads=["msk", "ppre"], writes=["ppre"])
                        P.op("scalar", lambda e, b=b, jc=jc: e.activation(
                                db[b][:, jc * 128:(jc + 1) * 128],
                                ppre[:, 0:128], AF.Identity),
                             reads=["ppre"], writes=[f"db{b}"])

                if self.asub.startswith("A"):
                    continue
                for b in range(16):
                    dcur, kcur = db[b], f"db{b}"
                    for r in range(8):
                        P.op("vector", lambda e, dcur=dcur, valid=valid: e.max(
                                m8t[:], dcur[:, 0:valid]),
                             reads=[kcur], writes=["m8t"])
                        if r < 7:
                            dnew, knew = (sc0, "sc0") if r % 2 == 0 else (sc1, "esb")
                            P.op("vector", lambda e, dcur=dcur, dnew=dnew, valid=valid:
                                    e.match_replace(dnew[:, 0:valid], m8t[:],
                                                    dcur[:, 0:valid], NEG_BIG),
                                 reads=["m8t", kcur], writes=[knew])
                            dcur, kcur = dnew, knew
                    P.op("vector", lambda e, b=b: e.tensor_copy(vks[:, b:b + 1], m8t[:, 7:8]),
                         reads=["m8t"], writes=["vks"])
                    if grp == 0:
                        P.op("vector", lambda e, b=b: e.copy_predicated(
                                vks[:, b:b + 1], shortsel[:, b:b + 1],
                                self.cpool[:, 3:4]),
                             reads=["shortsel", "cpool", "vks"], writes=["vks"])
                    P.op("vector", lambda e, b=b: e.tensor_scalar(
                            vks[:, 16 + b:17 + b], vks[:, b:b + 1], -1.0, None, ALU.mult),
                         reads=["vks"], writes=["vks"])
                    P.op("scalar", lambda e, b=b, valid=valid: e.activation(
                            esb[:, 0:valid], db[b][:, 0:valid], AF.Exp,
                            bias=vks[:, 16 + b:17 + b]),
                         reads=[f"db{b}", "vks"], writes=["esb"])
                    P.op("vector", lambda e, b=b, valid=valid: e.scalar_tensor_tensor(
                            db[b][:, 0:valid], db[b][:, 0:valid], vks[:, b:b + 1],
                            esb[:, 0:valid], ALU.is_ge, ALU.mult,
                            accum_out=vks[:, 32 + b:33 + b]),
                         reads=[f"db{b}", "vks", "esb"], writes=[f"db{b}", "vks_den"])
                for b in range(16):
                    P.op("vector", lambda e, b=b: e.reciprocal(vks[:, 48 + b:49 + b],
                                                               vks[:, 32 + b:33 + b]),
                         reads=["vks_den"], writes=["vks_r"])
                    P.op("vector", lambda e, b=b: e.tensor_scalar(
                            wscl[b][:], wpost2[:], vks[:, 48 + b:49 + b], None, ALU.mult),
                         reads=["wpost2", "vks_r"], writes=[f"wscl{b}"])

                for jc in jcs:
                    vb = vc_sb[jc % 2]
                    vkk = "vc0"
                    P.op("sync", lambda e, jc=jc, vb=vb: e.dma_start(
                            vb[:], self.v_d[jc * 128:(jc + 1) * 128, :]),
                         reads=["v_d"], writes=[vkk], dma="g")
                    p2 = pt2_sb[jc % 2]
                    p2k = "pt20"
                    for b in range(16):
                        P.op("tensor", lambda e, b=b, jc=jc: e.matmul(
                                ppost[:, 0:128],
                                db[b][:, jc * 128:(jc + 1) * 128], wscl[b][:],
                                start=True, stop=True),
                             reads=[f"db{b}", f"wscl{b}"], writes=["ppost"])
                        P.op("vector", lambda e, b=b, p2=p2: e.tensor_copy(
                                p2[:].rearrange("p (h bi) -> p h bi", bi=128)[:, :, b * 8:(b + 1) * 8],
                                ppost[:, 0:128].rearrange(
                                    "p (h i) -> p h i", i=8)),
                             reads=["ppost"], writes=[p2k])
                    for h in range(16):
                        ph = (h % 2) * 64
                        P.op("tensor", lambda e, h=h, ph=ph, vb=vb, p2=p2: e.matmul(
                                pout2[ph:ph + 64, (h // 2) * 128:(h // 2) * 128 + 128],
                                vb[:, h * 64:(h + 1) * 64],
                                p2[:, h * 128:(h + 1) * 128],
                                start=True, stop=True,
                                tile_position=(0, ph)),
                             reads=[vkk, p2k], writes=["pout2"])
                    if jc == 0:
                        P.op("vector", lambda e: e.tensor_copy(out2sb[:], pout2[:]),
                             reads=["pout2"], writes=["out2sb"])
                    else:
                        P.op("vector", lambda e: e.tensor_add(out2sb[:], out2sb[:], pout2[:]),
                             reads=["pout2", "out2sb"], writes=["out2sb"])

                for hds in range(8):
                    P.op("sync", lambda e, hds=hds, grp=grp: e.dma_start(
                            sg_sb[:, hds * 128:(hds + 1) * 128],
                            self.sgT_d[hds * 128:(hds + 1) * 128, grp * 128:(grp + 1) * 128]),
                         reads=["sgT_d"], writes=["sg_sb"], dma="g")
                P.op("vector", lambda e: e.tensor_mul(gt_sb[:], out2sb[:], sg_sb[:]),
                     reads=["out2sb", "sg_sb"], writes=["sg_sb"])
                for dms in range(8):
                    for hdc in range(8):
                        P.op("tensor", lambda e, dms=dms, hdc=hdc: e.matmul(
                                pwo[:, 0:128],
                                wo_sb[:, hdc * D + dms * 128: hdc * D + (dms + 1) * 128],
                                gt_sb[:, hdc * 128:(hdc + 1) * 128],
                                start=(hdc == 0), stop=(hdc == 7)),
                             reads=["wo_sb", "sg_sb"], writes=["pwo"])
                    P.op("vector", lambda e, dms=dms: e.tensor_copy(yTs[:], pwo[:, 0:128]),
                         reads=["pwo"], writes=["at_yTs"])
                    P.op("tensor", lambda e, dms=dms: e.transpose(
                            ptw[:, 0:128], yTs[:], self.ident_f[:]),
                         reads=["at_yTs", "ident_f"], writes=["ptw"])
                    P.op("vector", lambda e, dms=dms: e.tensor_copy(
                            ytok[:, dms * 128:(dms + 1) * 128], ptw[:, 0:128]),
                         reads=["ptw"], writes=["out2sb"])
                P.op("sync", lambda e, grp=grp: e.dma_start(
                        xr[:], self.x1_d[grp * 128:(grp + 1) * 128, :]),
                     reads=["x_dstf1"], writes=["at_xr"], dma="g")
                P.op("vector", lambda e: e.tensor_add(xr[:], xr[:], ytok[:]),
                     reads=["at_xr", "out2sb"], writes=["at_xr"])
                P.op("gpsimd", lambda e, grp=grp: e.dma_start(
                        self.x2_d[grp * 128:(grp + 1) * 128, :], xr[:]),
                     reads=["at_xr"], writes=["x2_d"], dma="g")
            P.retire_all()
            P.flush(self.block)


    def finish(self):
        deps = self.P.barrier_deps()

        def fin(eng, deps=deps):
            for sem, val in deps:
                eng.wait_ge(sem, val)
        self.P.flush(self.block)
        self.block.vector(fin)
        self.ctx.close()
        return self.nc


def build_x1(cfg: Cfg):
    k = Kern(cfg)
    P, nc = k.P, k.nc
    k.stage_consts()
    with nc.sbuf_tensor("wfcqT_sb", [128, 8 * HID], BF16) as wfcqT_sb, \
         nc.sbuf_tensor("wprojqT_sb", [128, 32 * D], BF16) as wprojqT_sb:
        k.stage_quant(wfcqT_sb, wprojqT_sb)
        k.putil_ctx.close()
        k.stage_ffn(k.x_in, k.x1_d, wfcqT_sb, wprojqT_sb, "f1", n_tok=cfg.t_full)
        P.flush(k.block)
    cp = k.sbuf("cp", [128, D], F32)
    for t in range(cfg.n_ttiles):
        P.op("sync", lambda e, t=t: e.dma_start(cp[:], k.x1_d[t * 128:(t + 1) * 128, :]),
             reads=["x_dstf1"], writes=["cp"], dma="g")
        P.op("gpsimd", lambda e, t=t: e.dma_start(k.y_out[t * 128:(t + 1) * 128, :], cp[:]),
             reads=["cp"], writes=["y_out"], dma="g")
    return k.finish(), k.din

def build_full(cfg: Cfg):
    k = Kern(cfg)
    P, nc = k.P, k.nc
    k.stage_consts()
    with nc.sbuf_tensor("wfcqT_sb", [128, 8 * HID], BF16) as wfcqT_sb, \
         nc.sbuf_tensor("wprojqT_sb", [128, 32 * D], BF16) as wprojqT_sb:
        k.stage_quant(wfcqT_sb, wprojqT_sb)
        k.P.retire_all()
        k.P.flush(k.block)
        k.putil_ctx.close()
        k.stage_ffn(k.x_in, k.x1_d, wfcqT_sb, wprojqT_sb, "f1", n_tok=cfg.t_full)
    k.stage_proj()
    k.stage_kv_assemble()
    k.stage_attn()
    with nc.sbuf_tensor("wfcqT_sb2", [128, 8 * HID], BF16) as w1, \
         nc.sbuf_tensor("wprojqT_sb2", [128, 32 * D], BF16) as w2:
        for dmc in range(8):
            P.op("sync", lambda e, dmc=dmc: e.dma_start(
                    w1[:, dmc * HID:(dmc + 1) * HID], k.wfcqT_d[dmc * 128:(dmc + 1) * 128, :]),
                 reads=["wfcqT_d"], writes=["wfcqT_sb"], dma="g")
        for hc in range(32):
            P.op("sync", lambda e, hc=hc: e.dma_start(
                    w2[:, hc * D:(hc + 1) * D], k.wprojqT_d[hc * 128:(hc + 1) * 128, :]),
                 reads=["wprojqT_d"], writes=["wprojqT_sb"], dma="g")
        k.stage_ffn(k.x2_d, k.y_out, w1, w2, "f2")
    return k.finish(), k.din


def build_quant_dbg(cfg: Cfg, upto="quant", sub="all"):
    k = Kern(cfg)
    P, nc = k.P, k.nc
    k.stage_consts()
    if upto == "consts":
        cp = k.sbuf("cpdbg", [128, D], F32)
        P.op("vector", lambda e: e.memset(cp[:], 1.0), writes=["cpdbg"])
        for t in range(cfg.t_loc // 128):
            P.op("gpsimd", lambda e, t=t: e.dma_start(k.y_out[t * 128:(t + 1) * 128, :], cp[:]),
                 reads=["cpdbg"], writes=["y_out"], dma="g")
        return k.finish(), k.din
    with nc.sbuf_tensor("wfcqT_sb", [128, 8 * HID], BF16) as wfcqT_sb, \
         nc.sbuf_tensor("wprojqT_sb", [128, 32 * D], BF16) as wprojqT_sb:
        k.stage_quant(wfcqT_sb, wprojqT_sb, sub=sub)
        P.flush(k.block)
    if sub != "all":
        cp = k.sbuf("cpdbg", [128, D], F32)
        P.op("vector", lambda e: e.memset(cp[:], 2.0), writes=["cpdbg"])
        for t in range(cfg.t_loc // 128):
            P.op("gpsimd", lambda e, t=t: e.dma_start(k.y_out[t * 128:(t + 1) * 128, :], cp[:]),
                 reads=["cpdbg"], writes=["y_out"], dma="g")
        return k.finish(), k.din
    cp = k.sbuf("cpdbg", [128, D], F32)
    cpb = k.sbuf("cpdbgb", [128, D], BF16)
    for t in range(8):
        P.op("sync", lambda e, t=t: e.dma_start(cpb[:], k.wfcqT_d[t * 128:(t + 1) * 128, 0:D]),
             reads=["wfcqT_d"], writes=["cpdbgb"], dma="g")
        P.op("vector", lambda e, t=t: e.tensor_copy(cp[:], cpb[:]),
             reads=["cpdbgb"], writes=["cpdbg"])
        P.op("gpsimd", lambda e, t=t: e.dma_start(k.y_out[t * 128:(t + 1) * 128, :], cp[:]),
             reads=["cpdbg"], writes=["y_out"], dma="g")
    return k.finish(), k.din


def build_dbg(cfg: Cfg, upto="proj", asub="full"):
    k = Kern(cfg)
    P, nc = k.P, k.nc
    k.stage_consts()
    with nc.sbuf_tensor("wfcqT_sb", [128, 8 * HID], BF16) as wfcqT_sb, \
         nc.sbuf_tensor("wprojqT_sb", [128, 32 * D], BF16) as wprojqT_sb:
        k.stage_quant(wfcqT_sb, wprojqT_sb)
        k.P.retire_all()
        k.P.flush(k.block)
        k.putil_ctx.close()
        k.stage_ffn(k.x_in, k.x1_d, wfcqT_sb, wprojqT_sb, "f1", n_tok=cfg.t_full)
    k.stage_proj()
    if upto == "proj":
        cp = k.sbuf("cpdbg", [128, D], F32)
        cpb = k.sbuf("cpdbgb", [128, D], BF16)
        for t in range(8):
            P.op("sync", lambda e, t=t: e.dma_start(cpb[:], k.qnT_d[t * 128:(t + 1) * 128, 0:D]),
                 reads=["Wq_d"], writes=["cpdbgb"], dma="g")
            P.op("vector", lambda e, t=t: e.tensor_copy(cp[:], cpb[:]),
                 reads=["cpdbgb"], writes=["cpdbg"])
            P.op("gpsimd", lambda e, t=t: e.dma_start(k.y_out[t * 128:(t + 1) * 128, :], cp[:]),
                 reads=["cpdbg"], writes=["y_out"], dma="g")
        return k.finish(), k.din
    k.stage_kv_assemble()
    k.stage_attn(asub=asub)
    if asub != "full":
        cp = k.sbuf("cpdbg", [128, D], F32)
        P.op("vector", lambda e: e.memset(cp[:], 5.0), writes=["cpdbg"])
        for t in range(cfg.t_loc // 128):
            P.op("gpsimd", lambda e, t=t: e.dma_start(k.y_out[t * 128:(t + 1) * 128, :], cp[:]),
                 reads=["cpdbg"], writes=["y_out"], dma="g")
        return k.finish(), k.din
    cp = k.sbuf("cpdbg", [128, D], F32)
    for t in range(cfg.t_loc // 128):
        P.op("sync", lambda e, t=t: e.dma_start(cp[:], k.x2_d[t * 128:(t + 1) * 128, :]),
             reads=["x2_d"], writes=["cpdbg"], dma="g")
        P.op("gpsimd", lambda e, t=t: e.dma_start(k.y_out[t * 128:(t + 1) * 128, :], cp[:]),
             reads=["cpdbg"], writes=["y_out"], dma="g")
    return k.finish(), k.din


from concourse.bass_utils import run_bass_kernel_spmd


def kernel(**inputs):
    SEQ, TLOC, NCORES = 4096, 2048, 8
    base = {
        "Wq": inputs["L0_attn_Wq"], "Wk": inputs["L0_attn_Wk"], "Wv": inputs["L0_attn_Wv"],
        "Wg": inputs["L0_attn_Wg"], "Wo": inputs["L0_attn_Wo"],
        "bg": inputs["L0_attn_bg"],
        "Wpre": inputs["L0_attn_W_pre_th"], "Wpost": inputs["L0_attn_W_post_th"],
        "memk": inputs["L0_attn_mem_k"], "memv": inputs["L0_attn_mem_v"],
        "hs": np.ascontiguousarray(inputs["L0_attn_head_scale_params"]).reshape(16),
        "Wfc": inputs["L0_ffn_W_fc"], "Wproj": inputs["L0_ffn_W_proj"],
        "lnw": inputs["L0_ffn_ln_w"], "lnb": inputs["L0_ffn_ln_b"],
    }
    base = {k: np.ascontiguousarray(v, dtype=np.float32) for k, v in base.items()}
    x = np.ascontiguousarray(inputs["L0_attn_x"], dtype=np.float32)

    in_maps = []
    tokmaps = []
    cfg0 = None
    for c in range(NCORES):
        beta, eta = c // 2, c % 2
        blocks = [(cc * 128, 128) for cc in range(SEQ // 128) if cc % 2 == eta]
        pblocks = [(cc * 128, 128) for cc in range(SEQ // 128) if cc % 2 != eta]
        cfg = Cfg(seq=SEQ, t_loc=TLOC, tok_blocks=blocks)
        if cfg0 is None:
            cfg0 = cfg
        tokmap = np.concatenate([np.arange(gs, gs + ln) for gs, ln in blocks])
        ptokmap = np.concatenate([np.arange(gs, gs + ln) for gs, ln in pblocks])
        tokmaps.append((beta, tokmap))
        m = dict(base)
        m.update(cfg.host_consts())
        m["x_in"] = np.ascontiguousarray(
            np.concatenate([x[beta][tokmap], x[beta][ptokmap]], axis=0))
        in_maps.append(m)

    nc, _ = build_full(cfg0)
    res = run_bass_kernel_spmd(nc, in_maps, core_ids=list(range(NCORES)))
    out = np.zeros((4, SEQ, 1024), dtype=np.float32)
    for c in range(NCORES):
        beta, tokmap = tokmaps[c]
        out[beta][tokmap] = res.results[c]["y_out"]
    return out


# version 12
# speedup vs baseline: 1.1313x; 1.1313x over previous
import math
from contextlib import ExitStack

import numpy as np
import concourse.bass as bass
import concourse.mybir as mybir

F32 = mybir.dt.float32
BF16 = mybir.dt.bfloat16
AF = mybir.ActivationFunctionType
ALU = mybir.AluOpType
AX = mybir.AxisListType

D = 1024
H = 16
DH = 64
HID = 4096
NMEM = 4
TOPK = 64
LN_EPS = 1e-5
ABS_EPS = 1e-6
NEG_BIG = -3.3895314e38
VK_SHORT = -30.0
QK_SCALE = 10.0

ENGS = ("sync", "tensor", "vector", "scalar", "gpsimd")
DMA_ENGS = ("sync", "gpsimd", "scalar", "vector")


class Prog:

    def __init__(self, nc, ctx):
        self.nc = nc
        self.ctx = ctx
        self.items = {e: [] for e in ENGS}
        self.esem = {e: ctx.enter_context(nc.semaphore(f"sem_{e}")) for e in ENGS}
        self.ecnt = {e: 0 for e in ENGS}
        self.dsem = {}
        self.last_w = {}
        self.readers = {}
        self.waited = {e: {} for e in ENGS}
        self.gdep = {}

    def _dma_sem(self, name, eng):
        if name not in self.dsem:
            self.dsem[name] = [self.ctx.enter_context(self.nc.semaphore(f"d_{name}")), 0, eng]
        return self.dsem[name]

    def _wait(self, eng, dep):
        kind, a, b = dep
        if kind == "d":
            rec = self.dsem[a]
            sem, val = rec[0], rec[1]
        else:
            sem, val = a, b
            if eng == "tensor" and sem is self.esem["tensor"]:
                return
        if self.waited[eng].get(sem, -1) >= val:
            return
        self.items[eng].append(("w", sem, val))
        self.waited[eng][sem] = val

    def retire_all(self):
        for m in (self.last_w, self.readers):
            for deps in m.values():
                for dep in deps:
                    kind, a, b = dep
                    if kind == "d":
                        rec = self.dsem[a]
                        sem, val = rec[0], rec[1]
                    else:
                        sem, val = a, b
                    if self.gdep.get(sem, -1) < val:
                        self.gdep[sem] = val
        self.last_w.clear()
        self.readers.clear()

    def op(self, eng, emit, reads=(), writes=(), dma=None):
        for sem, val in self.gdep.items():
            self._wait(eng, ("e", sem, val))
        for b in reads:
            for dep in self.last_w.get(b, ()):
                self._wait(eng, dep)
        for b in writes:
            for dep in self.last_w.get(b, ()):
                self._wait(eng, dep)
            for dep in self.readers.get(b, ()):
                self._wait(eng, dep)
        if dma is not None:
            assert len(writes) >= 1
            dma = "b_" + writes[0]
            rec = self._dma_sem(dma, eng)
            rec[1] += 16
            sem, n = rec[0], 16
            done = ("d", dma, rec[1])
        else:
            sem, n = self.esem[eng], 1
            self.ecnt[eng] += 1
            done = ("e", sem, self.ecnt[eng])
        self.items[eng].append(("op", emit, sem, n))
        for b in reads:
            self.readers.setdefault(b, []).append(done)
        for b in writes:
            self.last_w[b] = [done]
            self.readers[b] = []
        return done

    def barrier_deps(self):
        deps = [(self.esem[e], self.ecnt[e]) for e in ENGS if self.ecnt[e]]
        deps += [(rec[0], rec[1]) for rec in self.dsem.values() if rec[1]]
        return deps

    def flush(self, block):
        for e in ENGS:
            items = self.items[e]
            if not items:
                continue
            self.items[e] = []

            def emit_all(eng, items=items):
                for it in items:
                    if it[0] == "w":
                        eng.wait_ge(it[1], it[2])
                    else:
                        _, emitfn, sem, n = it
                        emitfn(eng).then_inc(sem, n)
            getattr(block, e)(emit_all)


class Cfg:
    def __init__(self, seq=4096, t_loc=2048, tok_blocks=None, debug_out=None):
        self.seq = seq
        self.t_loc = t_loc
        self.t_full = seq
        self.tok_blocks = tok_blocks or [(0, t_loc)]
        assert sum(l for _, l in self.tok_blocks) == t_loc
        self.no_exchange = (t_loc == seq)
        self.pair_blocks = None
        self.debug_out = debug_out
        self.n_ttiles = t_loc // 128
        self.n_jc = 1 + seq // 128
        self.jcols = 128 + seq

    def g_tok(self, l):
        off = 0
        for gs, ln in self.tok_blocks:
            if l < off + ln:
                return gs + (l - off)
            off += ln
        raise ValueError(l)

    def rank_blocks(self, r):
        blocks = self.pair_blocks[r]
        out, off = [], 0
        for gs, ln in blocks:
            out.append((gs, off, ln))
            off += ln
        return out

    def pos_local_chunk(self, m):
        if self.no_exchange:
            return m
        nloc = self.t_loc // 128
        return (m // 2) if m % 2 == 0 else nloc + m // 2

    def pos_global_chunk(self, m):
        if self.no_exchange:
            return m
        own = [gs // 128 for gs, _ in self.tok_blocks]
        all_ = set(range(self.seq // 128))
        part = sorted(all_ - set(own))
        return own[m // 2] if m % 2 == 0 else part[m // 2]

    def sched(self):
        NG = self.t_loc // 128
        out = []
        for k in range(NG):
            if self.no_exchange:
                ntc = k + 1
                mpos = {k + 1}
            else:
                ntc = 2 * k + 2
                mpos = {2 * k + 1, 2 * k + 2}
            out.append((ntc, mpos))
        return out

    def host_consts(self):
        import ml_dtypes
        ident = np.eye(128, dtype=np.float32)
        dmask = np.zeros((128, 128), dtype=np.float32)
        dmask[:, NMEM:] = NEG_BIG
        NG = self.t_loc // 128
        ivec = np.zeros((128, NG), np.float32)
        ss = np.zeros((128, 16), np.float32)
        for k in range(NG):
            g0 = self.g_tok(k * 128)
            ivec[:, k] = g0 + np.arange(128) // 16
        for b in range(16):
            g0 = self.g_tok(0)
            ss[:, b] = ((g0 + 8 * b + np.arange(128) // 16 + 5) < TOPK).astype(np.float32)
        njc = self.n_jc
        negbase = np.zeros((128, njc), np.float32)
        for m in range(njc - 1):
            negbase[:, m + 1] = -128.0 * self.pos_global_chunk(m)
        return {"ident_c": ident.astype(ml_dtypes.bfloat16),
                "dmask_c": dmask.astype(ml_dtypes.bfloat16),
                "shortsel_c": ss.astype(np.uint32),
                "ivec_c": ivec,
                "negbase_c": negbase,
                "colio_c": np.broadcast_to(np.arange(128, dtype=np.float32)[None, :],
                                           (128, 128)).copy()}


class Kern:

    def __init__(self, cfg: Cfg):
        self.cfg = cfg
        nc = self.nc = bass.Bass("TRN2", target_bir_lowering=False, debug=False)
        T, SEQ, JCOL = cfg.t_loc, cfg.seq, cfg.jcols
        self.din = {}

        def dt_in(name, shape, dtype=F32):
            self.din[name] = nc.dram_tensor(name, shape, dtype, kind="ExternalInput").ap()
            return self.din[name]

        self.x_in = dt_in("x_in", [cfg.t_full, D])
        self.W_in = {w: dt_in(w, [D, D]) for w in ("Wq", "Wk", "Wv", "Wg", "Wo")}
        self.bg_in = dt_in("bg", [D])
        self.Wpre_in = dt_in("Wpre", [H, H])
        self.Wpost_in = dt_in("Wpost", [H, H])
        self.memk_in = dt_in("memk", [H, NMEM, DH])
        self.memv_in = dt_in("memv", [H, NMEM, DH])
        self.hs_in = dt_in("hs", [H])
        self.Wfc_in = dt_in("Wfc", [HID, D])
        self.Wproj_in = dt_in("Wproj", [D, HID])
        self.lnw_in = dt_in("lnw", [HID])
        self.lnb_in = dt_in("lnb", [HID])
        self.ident_in = dt_in("ident_c", [128, 128], BF16)
        self.dmask_in = dt_in("dmask_c", [128, 128], BF16)
        self.shortsel_in = dt_in("shortsel_c", [128, 16], mybir.dt.uint32)
        self.ivec_in = dt_in("ivec_c", [128, cfg.t_loc // 128], F32)
        self.negbase_in = dt_in("negbase_c", [128, cfg.n_jc], F32)
        self.colio_in = dt_in("colio_c", [128, 128], F32)

        self.y_out = nc.dram_tensor("y_out", [T, D], F32, kind="ExternalOutput").ap()

        dram = lambda name, shape, dtype: nc.dram_tensor(name, shape, dtype, kind="Internal").ap()
        self.x1_d = dram("x1_d", [cfg.t_full, D], F32)
        self.x2_d = dram("x2_d", [T, D], F32)
        self.wfcqT_d = dram("wfcqT_d", [D, HID], BF16)
        self.wprojqT_d = dram("wprojqT_d", [HID, D], BF16)
        self.qnT_d = dram("qnT_d", [D, T], BF16)
        self.sgT_d = dram("sgT_d", [D, T], BF16)
        self.knT_own_d = dram("knT_own_d", [D, cfg.t_full], BF16)
        self.v_own_d = dram("v_own_d", [cfg.t_full, D], BF16)
        self.kT_d = dram("kT_d", [D, JCOL], BF16)
        self.v_d = dram("v_d", [JCOL, D], BF16)

        self.ctx = ExitStack()
        sbuf = self.sbuf = lambda name, shape, dtype: self.ctx.enter_context(
            nc.sbuf_tensor(name, shape, dtype))

        self.ident = sbuf("ident", [128, 128], BF16)
        self.ident_f = sbuf("ident_f", [128, 128], F32)
        self.dmask = sbuf("dmask", [128, 128], BF16)
        self.ones128_b = sbuf("ones128_b", [128, 1], BF16)
        self.ones128_f = sbuf("ones128_f", [128, 1], F32)
        self.onesrow_b = sbuf("onesrow_b", [1, 128], BF16)
        self.onesrow_f = sbuf("onesrow_f", [1, 128], F32)
        self.onesh_b = sbuf("onesh_b", [128, 2], BF16)
        self.onesh_bq = sbuf("onesh_bq", [128, 2], BF16)
        self.cpool = sbuf("cpool", [128, 4], F32)
        self.lnw_sb = sbuf("lnw_sb", [128, 32], F32)
        self.lnb_sb = sbuf("lnb_sb", [128, 32], F32)
        self.wpre_t = sbuf("wpre_t", [128, 128], BF16)
        self.wpost_t = sbuf("wpost_t", [128, 128], BF16)

        self.putil_ctx = ExitStack()
        self.putil = self.putil_ctx.enter_context(nc.psum_tensor("putil", [128, 128], F32))
        self.P = Prog(nc, self.ctx)
        self.block = self.ctx.enter_context(nc.Block())

    def load_vec_T(self, vec_in, n, dst, key, psum_ap=None, alloc=None):
        P, nc = self.P, self.nc
        nch = n // 128
        stk = key + "_stg"
        stg = (alloc or self.sbuf)(key + "_stg", [nch, 128], F32)
        pst = (psum_ap if psum_ap is not None else self.putil)[:, 0:nch]
        P.op("sync", lambda e: e.dma_start(stg[:], vec_in.rearrange("(c p) -> c p", p=128)),
             writes=[stk], dma="g")
        P.op("tensor", lambda e: e.transpose(pst, stg[:], self.ident_f[0:nch, 0:nch]),
             reads=[stk, "ident_f"], writes=["putil"])
        P.op("vector", lambda e: e.tensor_copy(dst[:], pst), reads=["putil"], writes=[key])

    def stage_consts(self):
        P, nc = self.P, self.nc
        P.op("sync", lambda e: e.dma_start(self.ident[:], self.ident_in), writes=["ident"], dma="g")
        P.op("sync", lambda e: e.dma_start(self.dmask[:], self.dmask_in),
             writes=["dmask"], dma="g")
        P.op("vector", lambda e: e.tensor_copy(self.ident_f[:], self.ident[:]),
             reads=["ident"], writes=["ident_f"])
        self.load_vec_T(self.lnw_in, 4096, self.lnw_sb, "lnw_sb")
        self.load_vec_T(self.lnb_in, 4096, self.lnb_sb, "lnb_sb")
        P.op("vector", lambda e: e.memset(self.ones128_b[:], 1.0), writes=["ones128_b"])
        P.op("vector", lambda e: e.memset(self.ones128_f[:], 1.0), writes=["ones128_f"])
        P.op("vector", lambda e: e.memset(self.onesrow_b[:], 1.0), writes=["onesrow_b"])
        P.op("vector", lambda e: e.memset(self.onesrow_f[:], 1.0), writes=["onesrow_f"])
        P.op("vector", lambda e: e.memset(self.cpool[:, 0:1], float(ABS_EPS)), writes=["cpool"])
        P.op("vector", lambda e: e.memset(self.cpool[:, 1:2], float(LN_EPS)), writes=["cpool"])
        P.op("vector", lambda e: e.memset(self.cpool[:, 2:3], float(np.log(QK_SCALE))), writes=["cpool"])
        P.op("vector", lambda e: e.memset(self.cpool[:, 3:4], float(VK_SHORT)), writes=["cpool"])
        P.op("vector", lambda e: e.memset(self.onesh_b[:], 0.0), writes=["onesh_b"])
        P.op("vector", lambda e: e.memset(self.onesh_b[0:64, 0:1], 1.0), writes=["onesh_b"])
        P.op("vector", lambda e: e.memset(self.onesh_b[64:128, 1:2], 1.0), writes=["onesh_b"])
        P.op("vector", lambda e: e.memset(self.onesh_bq[:], 0.0), writes=["onesh_bq"])
        P.op("vector", lambda e: e.memset(self.onesh_bq[0:64, 0:1], 0.1), writes=["onesh_bq"])
        P.op("vector", lambda e: e.memset(self.onesh_bq[64:128, 1:2], 0.1), writes=["onesh_bq"])

        wsm = self.sbuf("wsm", [16, 64], F32)
        wsm_b = self.sbuf("wsm_b", [16, 32], BF16)
        wsmT = self.wsmT = self.sbuf("wsmT", [16, 32], BF16)
        pm16 = self.putil[0:16, 0:16].bitcast(BF16)
        P.op("sync", lambda e: e.dma_start(wsm[:, 0:16], self.Wpre_in), writes=["wsm"], dma="g")
        P.op("sync", lambda e: e.dma_start(wsm[:, 16:32], self.Wpost_in), writes=["wsm"], dma="g")
        P.op("sync", lambda e: e.dma_start(wsm[0:16, 32:33], self.hs_in[:, None]), writes=["wsm"], dma="g")
        P.op("vector", lambda e: e.tensor_scalar(wsm[0:16, 48:64], wsm[0:16, 16:32],
                                                 wsm[0:16, 32:33], None, ALU.mult),
             reads=["wsm"], writes=["wsm48"])
        P.op("vector", lambda e: e.tensor_copy(wsm_b[:, 0:16], wsm[0:16, 0:16]),
             reads=["wsm"], writes=["wsm_b"])
        P.op("vector", lambda e: e.tensor_copy(wsm_b[:, 16:32], wsm[0:16, 48:64]),
             reads=["wsm48"], writes=["wsm_b"])
        P.op("tensor", lambda e: e.transpose(pm16[:, 0:16], wsm_b[:, 0:16], self.ident[0:16, 0:16]),
             reads=["wsm_b", "ident"], writes=["putil"])
        P.op("tensor", lambda e: e.transpose(pm16[:, 16:32], wsm_b[:, 16:32], self.ident[0:16, 0:16]),
             reads=["wsm_b", "ident"], writes=["putil"])
        P.op("vector", lambda e: e.tensor_copy(wsmT[:], pm16[:]), reads=["putil"], writes=["wsmT"])
        P.op("vector", lambda e: e.memset(self.wpre_t[:], 0.0), writes=["wpre_t"])
        P.op("vector", lambda e: e.memset(self.wpost_t[:], 0.0), writes=["wpost_t"])
        for io in range(8):
            P.op("gpsimd", lambda e, io=io: e.dma_start(
                    self.wpre_t[16 * io:16 * io + 16, 16 * io:16 * io + 16], wsmT[:, 0:16]),
                 reads=["wsmT"], writes=["wpre_t"], dma="g")
            P.op("gpsimd", lambda e, io=io: e.dma_start(
                    self.wpost_t[16 * io:16 * io + 16, 16 * io:16 * io + 16], wsmT[:, 16:32]),
                 reads=["wsmT"], writes=["wpost_t"], dma="g")

    def stage_quant(self, wfcqT_sb, wprojqT_sb, sub="all"):
        P, nc = self.P, self.nc
        self.qsub = sub
        qctx = ExitStack()
        qsbuf = lambda name, shape, dtype: qctx.enter_context(nc.sbuf_tensor(name, shape, dtype))
        qstg = [qsbuf(f"qstg{i}", [128, HID], F32) for i in range(2)]
        qsm = qsbuf("qsm", [128, 40], F32)
        qmask = qsbuf("qmask", [128, HID], F32)
        qbf = qsbuf("qbf", [128, HID], BF16)
        pq = self.putil[:, 64:72]
        pqt = qctx.enter_context(nc.psum_tensor("pqt", [128, 4096], BF16))

        def do_quant(w_in, n_rows, n_cols, tag, out_cb):
            nrc = n_rows // 128
            for c in range(nrc):
                s = qstg[c % 2]
                sk = f"qs{c % 2}"
                P.op("sync", lambda e, c=c, s=s: e.dma_start(
                        s[:, 0:n_cols], w_in[c * 128:(c + 1) * 128, :]), writes=[sk], dma="g")
                P.op("vector", lambda e, c=c, s=s: e.tensor_reduce(
                        qsm[:, c:c + 1], s[:, 0:n_cols], AX.X, ALU.add, apply_absolute_value=True),
                     reads=[sk], writes=["qsm"])
            P.op("vector", lambda e, nrc=nrc: e.tensor_reduce(
                    qsm[:, 36:37], qsm[:, 0:nrc], AX.X, ALU.add), reads=["qsm"], writes=["qsm36"])
            P.op("tensor", lambda e: e.matmul(pq[0:1, 0:1], qsm[:, 36:37], self.ones128_f[:],
                                              start=True, stop=True),
                 reads=["qsm36", "ones128_f"], writes=["putil"])
            P.op("scalar", lambda e, n=n_rows * n_cols: e.activation(
                    qsm[0:1, 37:38], pq[0:1, 0:1], AF.Identity, bias=self.cpool[0:1, 0:1], scale=1.0 / n),
                 reads=["putil", "cpool"], writes=["qsm37"])
            P.op("tensor", lambda e: e.matmul(pq[:, 1:2], self.onesrow_f[:], qsm[0:1, 37:38],
                                              start=True, stop=True),
                 reads=["qsm37", "onesrow_f"], writes=["putil"])
            if self.qsub == "absum":
                return
            P.op("vector", lambda e: e.tensor_copy(qsm[:, 38:39], pq[:, 1:2]),
                 reads=["putil"], writes=["gam0"])
            P.op("vector", lambda e: e.reciprocal(qsm[:, 39:40], qsm[:, 38:39]),
                 reads=["gam0"], writes=["gam"])
            for c in range(nrc):
                s = qstg[c % 2]
                sk = f"qs{c % 2}"
                P.op("sync", lambda e, c=c, s=s: e.dma_start(
                        s[:, 0:n_cols], w_in[c * 128:(c + 1) * 128, :]), writes=[sk], dma="g")
                P.op("vector", lambda e, s=s, n_cols=n_cols: e.tensor_scalar(
                        s[:, 0:n_cols], s[:, 0:n_cols], qsm[:, 39:40], None, ALU.mult),
                     reads=[sk, "gam"], writes=[sk])
                P.op("vector", lambda e, s=s, n_cols=n_cols: e.tensor_scalar(
                        qmask[:, 0:n_cols], s[:, 0:n_cols], 0.5, None, ALU.is_gt),
                     reads=[sk], writes=["qmask"])
                P.op("vector", lambda e, s=s, n_cols=n_cols: e.tensor_scalar(
                        s[:, 0:n_cols], s[:, 0:n_cols], -0.5, None, ALU.is_lt),
                     reads=[sk], writes=[sk])
                P.op("vector", lambda e, s=s, n_cols=n_cols: e.tensor_sub(
                        qbf[:, 0:n_cols], qmask[:, 0:n_cols], s[:, 0:n_cols]),
                     reads=["qmask", sk], writes=["qbf"])
                if self.qsub != "noq":
                    out_cb(c)

        def fc_cb(c):
            for dmc in range(8):
                pslot = dmc % 4
                P.op("tensor", lambda e, dmc=dmc, pslot=pslot: e.transpose(
                        pqt[:, pslot * 1024:pslot * 1024 + 128],
                        qbf[:, dmc * 128:(dmc + 1) * 128], self.ident[:]),
                     reads=["qbf", "ident"], writes=[f"pqt{pslot}"])
                P.op("vector", lambda e, c=c, dmc=dmc, pslot=pslot: e.tensor_copy(
                        wfcqT_sb[:, dmc * HID + c * 128: dmc * HID + (c + 1) * 128],
                        pqt[:, pslot * 1024:pslot * 1024 + 128]),
                     reads=[f"pqt{pslot}"], writes=["wfcqT_sb"])
        do_quant(self.Wfc_in, HID, D, "fc", fc_cb)
        for dmc in range(8):
            P.op("gpsimd", lambda e, dmc=dmc: e.dma_start(
                    self.wfcqT_d[dmc * 128:(dmc + 1) * 128, :],
                    wfcqT_sb[:, dmc * HID:(dmc + 1) * HID]),
                 reads=["wfcqT_sb"], writes=["wfcqT_d"], dma="g")

        def pj_cb(c):
            for hc in range(32):
                pslot = hc % 4
                P.op("tensor", lambda e, hc=hc, pslot=pslot: e.transpose(
                        pqt[:, pslot * 1024:pslot * 1024 + 128],
                        qbf[:, hc * 128:(hc + 1) * 128], self.ident[:]),
                     reads=["qbf", "ident"], writes=[f"pqt{pslot}"])
                P.op("vector", lambda e, c=c, hc=hc, pslot=pslot: e.tensor_copy(
                        wprojqT_sb[:, hc * D + c * 128: hc * D + (c + 1) * 128],
                        pqt[:, pslot * 1024:pslot * 1024 + 128]),
                     reads=[f"pqt{pslot}"], writes=["wprojqT_sb"])
        do_quant(self.Wproj_in, D, HID, "pj", pj_cb)
        for hc in range(32):
            P.op("gpsimd", lambda e, hc=hc: e.dma_start(
                    self.wprojqT_d[hc * 128:(hc + 1) * 128, :],
                    wprojqT_sb[:, hc * D:(hc + 1) * D]),
                 reads=["wprojqT_sb"], writes=["wprojqT_d"], dma="g")
        P.retire_all()
        P.flush(self.block)
        qctx.close()

    def ln_tile(self, x_ap, xf, fslice, stat, scol, keys, out_bf, oslice):
        P = self.P
        xk, stk, outk = keys
        P.op("sync", lambda e: e.dma_start(xf[:, fslice], x_ap), writes=[xk], dma="g")
        P.op("vector", lambda e: e.tensor_scalar(
                out_bf[:, oslice], xf[:, fslice], -1.0 / D, None, ALU.mult,
                op1=ALU.add, accum_out=stat[:, scol:scol + 1]),
             reads=[xk], writes=[stk, outk])
        P.op("vector", lambda e: e.tensor_scalar(
                xf[:, fslice], xf[:, fslice], stat[:, scol:scol + 1], None, ALU.add),
             reads=[xk, stk], writes=[xk])
        P.op("scalar", lambda e: e.activation(
                out_bf[:, oslice], xf[:, fslice], AF.Square, accum_out=stat[:, scol + 1:scol + 2]),
             reads=[xk], writes=[stk, outk])
        P.op("scalar", lambda e: e.activation(
                stat[:, scol + 1:scol + 2], stat[:, scol + 1:scol + 2], AF.Ln,
                scale=1.0 / D, bias=self.cpool[:, 1:2]),
             reads=[stk, "cpool"], writes=[stk])
        P.op("scalar", lambda e: e.activation(
                stat[:, scol + 1:scol + 2], stat[:, scol + 1:scol + 2], AF.Exp, scale=-0.5),
             reads=[stk], writes=[stk])
        P.op("vector", lambda e: e.tensor_scalar(
                out_bf[:, oslice], xf[:, fslice], stat[:, scol + 1:scol + 2], None, ALU.mult),
             reads=[xk, stk], writes=[outk])

    def stage_ffn(self, x_src_d, x_dst_d, wfcqT_sb, wprojqT_sb, tag, scale_weights=True,
                  n_tok=None):
        P, nc, cfg = self.P, self.nc, self.cfg
        T = n_tok if n_tok is not None else cfg.t_loc
        TB = min(256, T)
        NTB = TB // 128
        NBLK = T // TB
        with ExitStack() as fstk:
            en = fstk.enter_context
            xf = en(nc.sbuf_tensor(f"xf_{tag}", [128, D], F32))
            xr = en(nc.sbuf_tensor(f"xr_{tag}", [128, D], F32))
            stat = en(nc.sbuf_tensor(f"st_{tag}", [128, 8], F32))
            lnxb = en(nc.sbuf_tensor(f"lnx_{tag}", [128, D], BF16))
            lnxT = en(nc.sbuf_tensor(f"lnxT_{tag}", [128, 8 * TB], BF16))
            gh = en(nc.sbuf_tensor(f"gh_{tag}", [128, 32 * TB], BF16))
            gh2 = en(nc.sbuf_tensor(f"gh2_{tag}", [128, TB], BF16))
            tstat = en(nc.sbuf_tensor(f"tstat_{tag}", [1, 2 * TB], F32))
            bcf = en(nc.sbuf_tensor(f"bcf_{tag}", [128, 2 * TB], F32))
            BC = en(nc.sbuf_tensor(f"BC_{tag}", [128, 18], F32))
            bhl = en(nc.sbuf_tensor(f"bhl_{tag}", [128, 2], BF16))
            yTs = en(nc.sbuf_tensor(f"yTs_{tag}", [128, TB], F32))
            yU = en(nc.sbuf_tensor(f"yU_{tag}", [128, TB], F32))
            ytok = en(nc.sbuf_tensor(f"ytok_{tag}", [128, NTB * D], F32))
            pfA = en(nc.psum_tensor(f"pfA_{tag}", [128, 512], F32))
            pfB = en(nc.psum_tensor(f"pfB_{tag}", [128, 512], F32))
            pstA = en(nc.psum_tensor(f"pstA_{tag}", [1, 512], F32))
            pstB = en(nc.psum_tensor(f"pstB_{tag}", [1, 512], F32))
            pxT = en(nc.psum_tensor(f"pxT_{tag}", [128, 1024], BF16))
            ptr = en(nc.psum_tensor(f"ptr_{tag}", [128, 512], F32))
            pyt = en(nc.psum_tensor(f"py_{tag}", [128, 512], F32))
            pBC = en(nc.psum_tensor(f"pBC_{tag}", [128, 512], F32))

            for dmc in range(8):
                for hc in range(32):
                    P.op("vector", lambda e, hc=hc: e.tensor_copy(bhl[:, 0:1], self.lnb_sb[:, hc:hc + 1]),
                         reads=["lnb_sb"], writes=[f"{tag}bhl"])
                    P.op("vector", lambda e, hc=hc: e.tensor_sub(stat[:, 4:5], self.lnb_sb[:, hc:hc + 1],
                                                                 bhl[:, 0:1]),
                         reads=["lnb_sb", f"{tag}bhl"], writes=[f"{tag}blo"])
                    P.op("vector", lambda e: e.tensor_copy(bhl[:, 1:2], stat[:, 4:5]),
                         reads=[f"{tag}blo"], writes=[f"{tag}bhl"])
                    P.op("tensor", lambda e, hc=hc, dmc=dmc: e.matmul(
                            pBC[:, 0:2], wprojqT_sb[:, hc * D + dmc * 128: hc * D + (dmc + 1) * 128],
                            bhl[:], start=(hc == 0), stop=(hc == 31)),
                         reads=["wprojqT_sb", f"{tag}bhl"], writes=[f"{tag}pBC"])
                P.op("vector", lambda e, dmc=dmc: e.tensor_copy(BC[:, 16:18], pBC[:, 0:2]),
                     reads=[f"{tag}pBC"], writes=[f"{tag}BCs"])
                P.op("vector", lambda e, dmc=dmc: e.tensor_add(BC[:, 8 + dmc:9 + dmc],
                                                               BC[:, 16:17], BC[:, 17:18]),
                     reads=[f"{tag}BCs"], writes=[f"{tag}BC"])
            if scale_weights:
                for hc in range(32):
                    P.op("vector", lambda e, hc=hc: e.tensor_scalar(
                            wprojqT_sb[:, hc * D:(hc + 1) * D], wprojqT_sb[:, hc * D:(hc + 1) * D],
                            self.lnw_sb[:, hc:hc + 1], None, ALU.mult),
                         reads=["wprojqT_sb", "lnw_sb"], writes=["wprojqT_sb"])
            for dmc in range(8):
                for hc in range(32):
                    P.op("tensor", lambda e, hc=hc, dmc=dmc: e.matmul(
                            pBC[:, 0:1], wprojqT_sb[:, hc * D + dmc * 128: hc * D + (dmc + 1) * 128],
                            self.ones128_b[:], start=(hc == 0), stop=(hc == 31)),
                         reads=["wprojqT_sb", "ones128_b"], writes=[f"{tag}pBC"])
                P.op("vector", lambda e, dmc=dmc: e.tensor_copy(BC[:, dmc:dmc + 1], pBC[:, 0:1]),
                     reads=[f"{tag}pBC"], writes=[f"{tag}BC"])

            for blk in range(NBLK):
                t0 = blk * TB
                for tt in range(NTB):
                    tok = t0 + tt * 128
                    self.ln_tile(x_src_d[tok:tok + 128, :], xf, slice(0, D), stat, 0,
                                 (f"{tag}xf", f"{tag}stat", f"{tag}lnxb"), lnxb, slice(0, D))
                    for dmc in range(8):
                        P.op("tensor", lambda e, dmc=dmc: e.transpose(
                                pxT[:, 0:128],
                                lnxb[:, dmc * 128:dmc * 128 + 128], self.ident[:]),
                             reads=[f"{tag}lnxb", "ident"], writes=[f"{tag}pxT"])
                        P.op("vector", lambda e, tt=tt, dmc=dmc: e.tensor_copy(
                                lnxT[:, dmc * TB + tt * 128: dmc * TB + (tt + 1) * 128],
                                pxT[:, 0:128]),
                             reads=[f"{tag}pxT"], writes=[f"{tag}lnxT"])

                for s in range(32):
                    pcurt, pk = (pfA, f"{tag}pfA") if s % 2 == 0 else (pfB, f"{tag}pfB")
                    pcur = pcurt[:, 0:TB]
                    for dmc in range(8):
                        P.op("tensor", lambda e, s=s, dmc=dmc, pcur=pcur: e.matmul(
                                pcur, wfcqT_sb[:, dmc * HID + s * 128: dmc * HID + (s + 1) * 128],
                                lnxT[:, dmc * TB:(dmc + 1) * TB],
                                start=(dmc == 0), stop=(dmc == 7)),
                             reads=["wfcqT_sb", f"{tag}lnxT"], writes=[pk])
                    P.op("scalar", lambda e, s=s, pcur=pcur: e.activation(
                            gh[:, s * TB:(s + 1) * TB], pcur, AF.Gelu),
                         reads=[pk], writes=[f"{tag}gh{s}"])
                    P.op("scalar", lambda e, s=s: e.activation(
                            gh2[:], gh[:, s * TB:(s + 1) * TB], AF.Square),
                         reads=[f"{tag}gh{s}"], writes=[f"{tag}gh2"])
                    P.op("tensor", lambda e, s=s: e.matmul(
                            pstA[0:1, 0:TB], self.ones128_b[:], gh[:, s * TB:(s + 1) * TB],
                            start=(s == 0), stop=(s == 31)),
                         reads=[f"{tag}gh{s}", "ones128_b"], writes=[f"{tag}pstA"])
                    P.op("tensor", lambda e, s=s: e.matmul(
                            pstB[0:1, 0:TB], self.ones128_b[:], gh2[:],
                            start=(s == 0), stop=(s == 31)),
                         reads=[f"{tag}gh2", "ones128_b"], writes=[f"{tag}pstB"])

                P.op("scalar", lambda e: e.activation(tstat[0:1, 0:TB], pstA[0:1, 0:TB],
                                                      AF.Identity, scale=1.0 / HID),
                     reads=[f"{tag}pstA"], writes=[f"{tag}mu"])
                P.op("scalar", lambda e: e.activation(tstat[0:1, TB:2 * TB], pstB[0:1, 0:TB],
                                                      AF.Identity, scale=1.0 / HID),
                     reads=[f"{tag}pstB"], writes=[f"{tag}ex2"])
                P.op("vector", lambda e: e.tensor_mul(yU[0:1, 0:TB], tstat[0:1, 0:TB],
                                                      tstat[0:1, 0:TB]),
                     reads=[f"{tag}mu"], writes=[f"{tag}mu2"])
                P.op("vector", lambda e: e.tensor_sub(tstat[0:1, TB:2 * TB], tstat[0:1, TB:2 * TB],
                                                      yU[0:1, 0:TB]),
                     reads=[f"{tag}ex2", f"{tag}mu2"], writes=[f"{tag}var"])
                P.op("scalar", lambda e: e.activation(tstat[0:1, TB:2 * TB], tstat[0:1, TB:2 * TB],
                                                      AF.Ln, bias=self.cpool[0:1, 1:2]),
                     reads=[f"{tag}var", "cpool"], writes=[f"{tag}var"])
                P.op("scalar", lambda e: e.activation(tstat[0:1, TB:2 * TB], tstat[0:1, TB:2 * TB],
                                                      AF.Exp, scale=-0.5),
                     reads=[f"{tag}var"], writes=[f"{tag}var"])
                P.op("vector", lambda e: e.tensor_scalar(
                        tstat[0:1, 0:TB], tstat[0:1, 0:TB], -1.0, None, ALU.mult),
                     reads=[f"{tag}mu"], writes=[f"{tag}mu"])
                P.op("vector", lambda e: e.tensor_mul(tstat[0:1, 0:TB], tstat[0:1, 0:TB],
                                                      tstat[0:1, TB:2 * TB]),
                     reads=[f"{tag}mu", f"{tag}var"], writes=[f"{tag}mrs"])
                P.op("tensor", lambda e: e.matmul(ptr[:, 0:TB], self.onesrow_f[:],
                                                  tstat[0:1, TB:2 * TB], start=True, stop=True),
                     reads=[f"{tag}var", "onesrow_f"], writes=[f"{tag}ptr"])
                P.op("vector", lambda e: e.tensor_copy(bcf[:, 0:TB], ptr[:, 0:TB]),
                     reads=[f"{tag}ptr"], writes=[f"{tag}bcA"])
                P.op("tensor", lambda e: e.matmul(ptr[:, 0:TB], self.onesrow_f[:],
                                                  tstat[0:1, 0:TB], start=True, stop=True),
                     reads=[f"{tag}mrs", "onesrow_f"], writes=[f"{tag}ptr"])
                P.op("vector", lambda e: e.tensor_copy(bcf[:, TB:2 * TB], ptr[:, 0:TB]),
                     reads=[f"{tag}ptr"], writes=[f"{tag}bcB"])

                for dmc in range(8):
                    for s in range(32):
                        P.op("tensor", lambda e, s=s, dmc=dmc: e.matmul(
                                pyt[:, 0:TB], wprojqT_sb[:, s * D + dmc * 128: s * D + (dmc + 1) * 128],
                                gh[:, s * TB:(s + 1) * TB],
                                start=(s == 0), stop=(s == 31)),
                             reads=["wprojqT_sb", f"{tag}gh{s}"], writes=[f"{tag}py"])
                    P.op("vector", lambda e, dmc=dmc: e.scalar_tensor_tensor(
                            yTs[:], pyt[:, 0:TB], 1.0, bcf[:, 0:TB], ALU.mult, ALU.mult),
                         reads=[f"{tag}py", f"{tag}bcA"], writes=[f"{tag}yTs"])
                    P.op("vector", lambda e, dmc=dmc: e.tensor_scalar(
                            yU[:], bcf[:, TB:2 * TB], BC[:, dmc:dmc + 1], BC[:, 8 + dmc:9 + dmc],
                            ALU.mult, op1=ALU.add),
                         reads=[f"{tag}bcB", f"{tag}BC"], writes=[f"{tag}yU"])
                    P.op("vector", lambda e, dmc=dmc: e.tensor_add(yTs[:], yTs[:], yU[:]),
                         reads=[f"{tag}yTs", f"{tag}yU"], writes=[f"{tag}yTs"])
                    for tt in range(NTB):
                        P.op("tensor", lambda e, tt=tt: e.transpose(
                                ptr[:, 0:128],
                                yTs[:, tt * 128:(tt + 1) * 128], self.ident_f[:]),
                             reads=[f"{tag}yTs", "ident_f"], writes=[f"{tag}ptr"])
                        P.op("vector", lambda e, tt=tt, dmc=dmc: e.tensor_copy(
                                ytok[:, tt * D + dmc * 128: tt * D + dmc * 128 + 128],
                                ptr[:, 0:128]),
                             reads=[f"{tag}ptr"], writes=[f"{tag}ytok{tt}"])
                for tt in range(NTB):
                    tok = t0 + tt * 128
                    P.op("sync", lambda e, tok=tok: e.dma_start(xr[:], x_src_d[tok:tok + 128, :]),
                         writes=[f"{tag}xr"], dma="g")
                    P.op("vector", lambda e, tt=tt: e.scalar_tensor_tensor(
                            xr[:], ytok[:, tt * D:(tt + 1) * D], 0.5, xr[:],
                            ALU.mult, ALU.add),
                         reads=[f"{tag}ytok{tt}", f"{tag}xr"], writes=[f"{tag}xr"])
                    P.op("gpsimd", lambda e, tok=tok: e.dma_start(x_dst_d[tok:tok + 128, :], xr[:]),
                         reads=[f"{tag}xr"], writes=["x_dst" + tag], dma="g")
            P.retire_all()
            P.flush(self.block)


    def stage_proj(self):
        P, nc, cfg = self.P, self.nc, self.cfg
        T = cfg.t_full
        TB = min(512, cfg.t_loc)
        NBLK = T // TB
        NBLK_OWN = cfg.t_loc // TB
        with ExitStack() as stk:
            en = stk.enter_context
            wsb = {}
            for w in ("Wq", "Wk", "Wv", "Wg"):
                wsb[w] = en(nc.sbuf_tensor(f"wsb_{w}", [128, 8 * D], BF16))
            stg = en(nc.sbuf_tensor("pj_stg", [128, D], F32))
            bgsb = en(nc.sbuf_tensor("pj_bg", [128, 8], F32))
            xf = en(nc.sbuf_tensor("pj_xf", [128, D], F32))
            stat = en(nc.sbuf_tensor("pj_stat", [128, 8], F32))
            lnxb = en(nc.sbuf_tensor("pj_lnx", [128, D], BF16))
            lnxT = en(nc.sbuf_tensor("pj_lnxT", [128, 8 * TB], BF16))
            sq = en(nc.sbuf_tensor("pj_sq", [128, 512], BF16))
            qraw = en(nc.sbuf_tensor("pj_qraw", [128, 512], F32))
            nrm = en(nc.sbuf_tensor("pj_nrm", [2, 1024], F32))
            onesh_f = en(nc.sbuf_tensor("pj_onesh_f", [2, 128], F32))
            outb = en(nc.sbuf_tensor("pj_outb", [128, 512], BF16))
            mk = en(nc.sbuf_tensor("pj_mk", [64, 64], F32))
            mkb = en(nc.sbuf_tensor("pj_mkb", [64, 64], BF16))
            mkn = en(nc.sbuf_tensor("pj_mkn", [64, 1], F32))
            zpad = en(nc.sbuf_tensor("pj_zpad", [128, 1024], BF16))
            ppj = en(nc.psum_tensor("ppjA", [128, 512], F32))
            ppj2 = en(nc.psum_tensor("ppjB", [128, 512], F32))
            pxT = en(nc.psum_tensor("pj_pxT", [128, 1024], BF16))
            pnrm = en(nc.psum_tensor("pj_pnrm", [2, 512], F32))
            pnb = en(nc.psum_tensor("pj_pnb", [128, 512], F32))
            pmk = en(nc.psum_tensor("pj_pmk", [64, 1024], BF16))

            oshT = en(nc.sbuf_tensor("pj_oshT", [128, 2], F32))
            P.op("vector", lambda e: e.memset(oshT[:], 0.0), writes=["oshT"])
            P.op("vector", lambda e: e.memset(oshT[0:64, 0:1], 1.0), writes=["oshT"])
            P.op("vector", lambda e: e.memset(oshT[64:128, 1:2], 1.0), writes=["oshT"])
            P.op("tensor", lambda e: e.transpose(pnrm[0:2, 0:128], oshT[:], self.ident_f[:]),
                 reads=["oshT", "ident_f"], writes=["pj_pnrm"])
            P.op("vector", lambda e: e.tensor_copy(onesh_f[:], pnrm[0:2, 0:128]),
                 reads=["pj_pnrm"], writes=["onesh_f"])
            P.op("vector", lambda e: e.memset(zpad[:], 0.0), writes=["zpad"])
            P.op("vector", lambda e: e.memset(self.cpool[:, 0:1], float(ABS_EPS)), writes=["cpool"])
            self.load_vec_T(self.bg_in, D, bgsb, "pj_bg", psum_ap=pnb, alloc=lambda n_, s_, d_: en(nc.sbuf_tensor(n_, s_, d_)))
            for w in ("Wq", "Wk", "Wv", "Wg"):
                for dmc in range(8):
                    P.op("sync", lambda e, w=w, dmc=dmc: e.dma_start(
                            stg[:], self.W_in[w][dmc * 128:(dmc + 1) * 128, :]),
                         writes=["pj_stg"], dma="g")
                    P.op("vector", lambda e, w=w, dmc=dmc: e.tensor_copy(
                            wsb[w][:, dmc * D:(dmc + 1) * D], stg[:]),
                         reads=["pj_stg"], writes=[f"wsb_{w}"])

            P.op("sync", lambda e: e.dma_start(mk[:], self.memk_in.rearrange("g s d -> (g s) d")),
                 writes=["pj_mk"], dma="g")
            P.op("scalar", lambda e: e.activation(mkb[:], mk[:], AF.Square,
                                                  accum_out=mkn[:]),
                 reads=["pj_mk"], writes=["pj_mkn", "pj_mkb"])
            P.op("scalar", lambda e: e.activation(mkn[:], mkn[:], AF.Ln, bias=self.cpool[0:64, 0:1],
                                                  scale=1.0),
                 reads=["pj_mkn", "cpool"], writes=["pj_mkn"])
            P.op("scalar", lambda e: e.activation(mkn[:], mkn[:], AF.Exp, scale=-0.5),
                 reads=["pj_mkn"], writes=["pj_mkn"])
            P.op("vector", lambda e: e.tensor_scalar(mkb[:], mk[:], mkn[:], None, ALU.mult),
                 reads=["pj_mk", "pj_mkn"], writes=["pj_mkb"])
            P.op("tensor", lambda e: e.transpose(pmk[:, 0:64], mkb[:], self.ident[0:64, 0:64]),
                 reads=["pj_mkb", "ident"], writes=["pj_pmk"])
            P.op("vector", lambda e: e.tensor_copy(mkb[:], pmk[:, 0:64]), reads=["pj_pmk"], writes=["pj_mkb"])
            P.op("gpsimd", lambda e: e.dma_start(
                    self.kT_d.rearrange("(g d) c -> d g c", d=DH)[:, :, 0:NMEM],
                    mkb[:].rearrange("d (g s) -> d g s", s=NMEM)),
                 reads=["pj_mkb"], writes=["kT_d"], dma="g")
            P.op("gpsimd", lambda e: e.dma_start(
                    self.v_d[0:NMEM, :].rearrange("s (g d) -> s g d", d=DH),
                    self.memv_in.rearrange("g s d -> s g d")),
                 writes=["v_d"], dma="g")
            for c in range(8):
                P.op("gpsimd", lambda e, c=c: e.dma_start(
                        self.kT_d[c * 128:(c + 1) * 128, NMEM:128], zpad[:, 0:128 - NMEM]),
                     reads=["zpad"], writes=["kT_d"], dma="g")
            P.op("gpsimd", lambda e: e.dma_start(self.v_d[NMEM:128, :], zpad[0:128 - NMEM, :]),
                 reads=["zpad"], writes=["v_d"], dma="g")

            def norm_head_slice(hds, src_key, out_dram, qscale):
                P.op("scalar", lambda e: e.activation(sq[:, 0:TB], ppj[:, 0:TB], AF.Square),
                     reads=["ppjA"], writes=["pj_sq"])
                P.op("scalar", lambda e: e.activation(qraw[:, 0:TB], ppj[:, 0:TB], AF.Identity),
                     reads=["ppjA"], writes=["pj_qraw"])
                P.op("tensor", lambda e: e.matmul(pnrm[0:2, 0:TB], self.onesh_b[:], sq[:, 0:TB],
                                                  start=True, stop=True),
                     reads=["onesh_b", "pj_sq"], writes=["pj_pnrm"])
                P.op("vector", lambda e: e.tensor_scalar(nrm[:, 0:TB], pnrm[0:2, 0:TB], 1e-24, None,
                                                         ALU.max),
                     reads=["pj_pnrm"], writes=["pj_nrm"])
                P.op("scalar", lambda e: e.activation(nrm[:, 0:TB], nrm[:, 0:TB], AF.Ln),
                     reads=["pj_nrm"], writes=["pj_nrm"])
                if qscale != 1.0:
                    P.op("scalar", lambda e: e.activation(
                            nrm[:, 0:TB], nrm[:, 0:TB], AF.Exp, scale=-0.5,
                            bias=self.cpool[0:2, 2:3]),
                         reads=["pj_nrm", "cpool"], writes=["pj_nrm"])
                else:
                    P.op("scalar", lambda e: e.activation(
                            nrm[:, 0:TB], nrm[:, 0:TB], AF.Exp, scale=-0.5),
                         reads=["pj_nrm"], writes=["pj_nrm"])
                P.op("tensor", lambda e: e.matmul(pnb[:, 0:TB], onesh_f[:], nrm[:, 0:TB],
                                                  start=True, stop=True),
                     reads=["onesh_f", "pj_nrm"], writes=["pj_pnb"])
                P.op("vector", lambda e: e.tensor_mul(outb[:, 0:TB], qraw[:, 0:TB], pnb[:, 0:TB]),
                     reads=["pj_qraw", "pj_pnb"], writes=["pj_outb"])
                P.op("gpsimd", lambda e: e.dma_start(out_dram, outb[:, 0:TB]),
                     reads=["pj_outb"], writes=[src_key + "_d"], dma="g")

            for blk in range(NBLK):
                t0 = blk * TB
                for tt in range(TB // 128):
                    tok = t0 + tt * 128
                    self.ln_tile(self.x1_d[tok:tok + 128, :], xf, slice(0, D), stat, 0,
                                 ("pj_xf", "pj_stat", "pj_lnxb"), lnxb, slice(0, D))
                    for dmc in range(8):
                        P.op("tensor", lambda e, dmc=dmc: e.transpose(
                                pxT[:, 0:128],
                                lnxb[:, dmc * 128:dmc * 128 + 128], self.ident[:]),
                             reads=["pj_lnxb", "ident"], writes=["pj_pxT"])
                        P.op("vector", lambda e, tt=tt, dmc=dmc: e.tensor_copy(
                                lnxT[:, dmc * TB + tt * 128: dmc * TB + (tt + 1) * 128],
                                pxT[:, 0:128]),
                             reads=["pj_pxT"], writes=["pj_lnxT"])

                for w, out_d, qs in (("Wq", self.qnT_d, QK_SCALE), ("Wk", self.knT_own_d, 1.0)):
                    if w == "Wq" and blk >= NBLK_OWN:
                        continue
                    for hds in range(8):
                        for dmc in range(8):
                            P.op("tensor", lambda e, w=w, hds=hds, dmc=dmc: e.matmul(
                                    ppj[:, 0:TB], wsb[w][:, dmc * D + hds * 128: dmc * D + (hds + 1) * 128],
                                    lnxT[:, dmc * TB:(dmc + 1) * TB],
                                    start=(dmc == 0), stop=(dmc == 7)),
                                 reads=[f"wsb_{w}", "pj_lnxT"], writes=["ppjA"])
                        norm_head_slice(hds, w, out_d[hds * 128:(hds + 1) * 128, t0:t0 + TB], qs)
                for tt in range(TB // 128):
                    tok = t0 + tt * 128
                    for vh in range(2):
                        for dmc in range(8):
                            P.op("tensor", lambda e, tt=tt, dmc=dmc, vh=vh: e.matmul(
                                    ppj2[:], lnxT[:, dmc * TB + tt * 128: dmc * TB + (tt + 1) * 128],
                                    wsb["Wv"][:, dmc * D + vh * 512: dmc * D + vh * 512 + 512],
                                    start=(dmc == 0), stop=(dmc == 7)),
                                 reads=["wsb_Wv", "pj_lnxT"], writes=["ppjB"])
                        P.op("vector", lambda e, vh=vh: e.tensor_copy(outb[:, 0:512], ppj2[:]),
                             reads=["ppjB"], writes=["pj_outb"])
                        P.op("gpsimd", lambda e, tok=tok, vh=vh: e.dma_start(
                                self.v_own_d[tok:tok + 128, vh * 512:vh * 512 + 512], outb[:, 0:512]),
                             reads=["pj_outb"], writes=["v_own_d"], dma="g")
                for hds in range(8):
                    if blk >= NBLK_OWN:
                        break
                    for dmc in range(8):
                        P.op("tensor", lambda e, hds=hds, dmc=dmc: e.matmul(
                                ppj[:, 0:TB], wsb["Wg"][:, dmc * D + hds * 128: dmc * D + (hds + 1) * 128],
                                lnxT[:, dmc * TB:(dmc + 1) * TB],
                                start=(dmc == 0), stop=(dmc == 7)),
                             reads=["wsb_Wg", "pj_lnxT"], writes=["ppjA"])
                    P.op("scalar", lambda e, hds=hds: e.activation(
                            outb[:, 0:TB], ppj[:, 0:TB], AF.Sigmoid, bias=bgsb[:, hds:hds + 1]),
                         reads=["ppjA", "pj_bg"], writes=["pj_outb"])
                    P.op("gpsimd", lambda e, hds=hds, t0=t0: e.dma_start(
                            self.sgT_d[hds * 128:(hds + 1) * 128, t0:t0 + TB], outb[:, 0:TB]),
                         reads=["pj_outb"], writes=["sgT_d"], dma="g")
            P.retire_all()
            P.flush(self.block)

    def stage_kv_assemble(self):
        P, nc, cfg = self.P, self.nc, self.cfg
        for m in range(cfg.seq // 128):
            lc = cfg.pos_local_chunk(m)
            P.op("gpsimd", lambda e, m=m, lc=lc: e.dma_start(
                    self.kT_d[:, 128 + m * 128:128 + (m + 1) * 128],
                    self.knT_own_d[:, lc * 128:(lc + 1) * 128]),
                 reads=["Wk_d"], writes=["kT_d"], dma="g")
            P.op("gpsimd", lambda e, m=m, lc=lc: e.dma_start(
                    self.v_d[128 + m * 128:128 + (m + 1) * 128, :],
                    self.v_own_d[lc * 128:(lc + 1) * 128, :]),
                 reads=["v_own_d"], writes=["v_d"], dma="g")

    def stage_attn(self, asub="full"):
        assert asub == "full"
        P, nc, cfg = self.P, self.nc, self.cfg
        T = cfg.t_loc
        JCOL = cfg.jcols
        NG = T // 128
        NW = 2 * NG
        sched = cfg.sched()
        with ExitStack() as stk:
            en = stk.enter_context
            db = en(nc.sbuf_tensor("db_all", [128, 16 * JCOL], BF16))
            dbv = db[:].rearrange("p (b j) -> p b j", b=16)
            sc0 = en(nc.sbuf_tensor("sc0", [128, JCOL], BF16))
            esb = en(nc.sbuf_tensor("esb", [128, JCOL], BF16))
            z_sb = [en(nc.sbuf_tensor("z0", [128, 1024], BF16))] * 2
            qn_sb = [en(nc.sbuf_tensor(f"qn{i}", [64, 1024], BF16)) for i in range(2)]
            kc_sb = [en(nc.sbuf_tensor(f"kc{i}", [64, 2048], BF16)) for i in range(2)]
            vc_sb = [en(nc.sbuf_tensor(f"vc{i}", [128, 1024], BF16)) for i in range(2)]
            mr_sb = [en(nc.sbuf_tensor(f"mr{i}", [128, 1024], BF16)) for i in range(2)]
            pt2_sb = en(nc.sbuf_tensor("pt20", [128, 1024], BF16))
            m8t = en(nc.sbuf_tensor("m8t", [128, 8], BF16))
            vks = en(nc.sbuf_tensor("vks", [128, 64], F32))
            wscl = [en(nc.sbuf_tensor(f"wscl{b}", [128, 128], BF16)) for b in range(16)]
            wpost2 = en(nc.sbuf_tensor("wpost2", [128, 128], BF16))
            out2sb = en(nc.sbuf_tensor("out2sb", [128, 512], F32))
            pev = en(nc.sbuf_tensor("pev", [128, 512], F32))
            sg_sb = en(nc.sbuf_tensor("sg_sb", [128, 512], BF16))
            wo_sb = en(nc.sbuf_tensor("wo_sb", [128, 8 * D], BF16))
            yTs = en(nc.sbuf_tensor("at_yTs", [128, 64], F32))
            xr = en(nc.sbuf_tensor("at_xr", [128, D], F32))
            shortsel = en(nc.sbuf_tensor("shortsel", [128, 16], mybir.dt.uint32))
            ivec_sb = en(nc.sbuf_tensor("ivec_sb", [128, NG], F32))
            negbase_sb = en(nc.sbuf_tensor("negbase_sb", [128, cfg.n_jc], F32))
            colio_sb = en(nc.sbuf_tensor("colio_sb", [128, 128], F32))
            ivs = en(nc.sbuf_tensor("ivs", [128, 2], F32))
            msk = en(nc.sbuf_tensor("msk", [128, 128], BF16))
            pbig = en(nc.psum_tensor("pbig", [128, 512], F32))
            pgat = [en(nc.psum_tensor(f"pgat{i}", [128, 1024], BF16)) for i in range(2)]
            ppre = en(nc.psum_tensor("ppre", [128, 512], F32))
            ppost = en(nc.psum_tensor("ppost", [128, 512], F32))
            pout2 = en(nc.psum_tensor("pout2", [128, 512], F32))
            pwo = en(nc.psum_tensor("pwo", [128, 512], F32))
            ptw = en(nc.psum_tensor("ptw", [128, 512], F32))

            P.op("sync", lambda e: e.dma_start(shortsel[:], self.shortsel_in),
                 writes=["shortsel"], dma="g")
            P.op("sync", lambda e: e.dma_start(ivec_sb[:], self.ivec_in),
                 writes=["ivec_sb"], dma="g")
            P.op("sync", lambda e: e.dma_start(negbase_sb[:], self.negbase_in),
                 writes=["negbase_sb"], dma="g")
            P.op("sync", lambda e: e.dma_start(colio_sb[:], self.colio_in),
                 writes=["colio_sb"], dma="g")
            P.op("vector", lambda e: e.tensor_copy(
                    wpost2[:].rearrange("p (h i) -> p h i", i=8),
                    self.wpost_t[:].rearrange("p (i h) -> p h i", h=16)),
                 reads=["wpost_t"], writes=["wpost2"])
            for hdc in range(8):
                P.op("sync", lambda e, hdc=hdc: e.dma_start(
                        xr[:], self.W_in["Wo"][hdc * 128:(hdc + 1) * 128, :]),
                     writes=["at_xr"], dma="g")
                P.op("vector", lambda e, hdc=hdc: e.tensor_copy(
                        wo_sb[:, hdc * D:(hdc + 1) * D], xr[:]),
                     reads=["at_xr"], writes=["wo_sb"])

            kT_v = self.kT_d.rearrange("(g p) j -> p g j", p=64)
            qnT_v = self.qnT_d.rearrange("(g p) t -> p g t", p=64)
            sgT_v = self.sgT_d.rearrange("(c q) t -> q c t", q=128)

            def phase_A(w):
                grp, half = divmod(w, 2)
                ntc, mpos = sched[grp]
                jcs = list(range(ntc + 1))
                tok0 = grp * 128 + half * 64
                sl = w % 2
                qn = qn_sb[sl]
                qk = f"qn{sl}"
                P.op("sync", lambda e, qn=qn, tok0=tok0: e.dma_start(
                        qn[:].rearrange("p (g t) -> p g t", g=16),
                        qnT_v[:, :, tok0:tok0 + 64]),
                     reads=["Wq_d"], writes=[qk], dma="g")
                P.op("sync", lambda e, jc=jcs[0]: e.dma_start(
                        kc_sb[0][:].rearrange("p (g j) -> p g j", g=16),
                        kT_v[:, :, jc * 128:(jc + 1) * 128]),
                     reads=["kT_d"], writes=["kc0"], dma="g")
                for jc in jcs:
                    kb, kk = kc_sb[jc % 2], f"kc{jc % 2}"
                    if jc + 1 <= jcs[-1]:
                        nsl = (jc + 1) % 2
                        P.op("sync", lambda e, jc=jc, nsl=nsl: e.dma_start(
                                kc_sb[nsl][:].rearrange("p (g j) -> p g j", g=16),
                                kT_v[:, :, (jc + 1) * 128:(jc + 2) * 128]),
                             reads=["kT_d"], writes=[f"kc{nsl}"], dma="g")
                    zb, zk = z_sb[0], "z0"
                    for f in range(2):
                        for gl in range(8):
                            g = f * 8 + gl
                            P.op("tensor", lambda e, g=g, gl=gl, kb=kb, qn=qn: e.matmul(
                                    pbig[:, gl * 64:(gl + 1) * 64],
                                    kb[0:64, g * 128:(g + 1) * 128],
                                    qn[0:64, g * 64:(g + 1) * 64],
                                    start=True, stop=True),
                                 reads=[kk, qk], writes=["pbig"])
                        P.op("scalar", lambda e, f=f, zb=zb: e.activation(
                                zb[:].rearrange("p (i g) -> p g i", g=16)[:, f * 8:(f + 1) * 8, :],
                                pbig[:].rearrange("p (g i) -> p g i", i=64),
                                AF.Identity),
                             reads=["pbig"], writes=[zk])
                    psl = jc % 2
                    for b in range(8):
                        P.op("tensor", lambda e, b=b, zb=zb, psl=psl: e.transpose(
                                pgat[psl][:, b * 128:(b + 1) * 128],
                                zb[:, b * 128:(b + 1) * 128], self.ident[:]),
                             reads=[zk, "ident"], writes=[f"pgat{psl}"])
                    P.op("scalar", lambda e, psl=psl: e.activation(
                            mr_sb[psl][:], pgat[psl][:], AF.Identity),
                         reads=[f"pgat{psl}"], writes=[f"mr{psl}"])
                    for bq in range(2):
                        for bl in range(4):
                            b = bq * 4 + bl
                            P.op("tensor", lambda e, b=b, bl=bl, psl=psl: e.matmul(
                                    ppre[:, bl * 128:(bl + 1) * 128], self.wpre_t[:],
                                    mr_sb[psl][:, b * 128:(b + 1) * 128],
                                    start=True, stop=True),
                                 reads=["wpre_t", f"mr{psl}"], writes=["ppre"])
                        bg0 = half * 8 + bq * 4
                        P.op("scalar", lambda e, bg0=bg0, jc=jc: e.activation(
                                dbv[:, bg0:bg0 + 4, jc * 128:(jc + 1) * 128],
                                ppre[:].rearrange("p (b j) -> p b j", b=4),
                                AF.Identity),
                             reads=["ppre"], writes=[f"dbq{bg0 // 4}"])
                        if jc == 0:
                            for bl in range(4):
                                bg = bg0 + bl
                                P.op("gpsimd", lambda e, bg=bg: e.tensor_tensor(
                                        dbv[:, bg:bg + 1, 0:128].rearrange("p b j -> p (b j)"),
                                        dbv[:, bg:bg + 1, 0:128].rearrange("p b j -> p (b j)"),
                                        self.dmask[:], ALU.add),
                                     reads=[f"dbq{bg // 4}", "dmask"], writes=[f"dbq{bg // 4}"])
                        if jc in mpos:
                            for bl in range(4):
                                bg = bg0 + bl
                                off = float(8 * bg)
                                P.op("vector", lambda e, grp=grp, off=off, jc=jc: e.tensor_scalar(
                                        ivs[:, 0:1], ivec_sb[:, grp:grp + 1],
                                        negbase_sb[:, jc:jc + 1], off, ALU.add, op1=ALU.add),
                                     reads=["ivec_sb", "negbase_sb"], writes=["ivs"])
                                P.op("vector", lambda e: e.tensor_scalar(
                                        msk[:], colio_sb[:], ivs[:, 0:1], NEG_BIG,
                                        ALU.is_gt, op1=ALU.mult),
                                     reads=["colio_sb", "ivs"], writes=["msk"])
                                P.op("gpsimd", lambda e, bg=bg, jc=jc: e.tensor_tensor(
                                        dbv[:, bg:bg + 1, jc * 128:(jc + 1) * 128].rearrange("p b j -> p (b j)"),
                                        dbv[:, bg:bg + 1, jc * 128:(jc + 1) * 128].rearrange("p b j -> p (b j)"),
                                        msk[:], ALU.add),
                                     reads=["msk", f"dbq{bg // 4}"], writes=[f"dbq{bg // 4}"])

            def tail_ops(w, b, valid):
                grp, half = divmod(w, 2)
                bg = half * 8 + b
                P.op("scalar", lambda e, bg=bg, valid=valid: e.activation(
                        esb[:, 0:valid], dbv[:, bg, 0:valid], AF.Exp,
                        bias=vks[:, 16 + bg:17 + bg]),
                     reads=[f"dbq{bg // 4}", "vks"], writes=["esb"])
                P.op("vector", lambda e, bg=bg, valid=valid: e.scalar_tensor_tensor(
                        dbv[:, bg, 0:valid], dbv[:, bg, 0:valid], vks[:, bg:bg + 1],
                        esb[:, 0:valid], ALU.is_ge, ALU.mult,
                        accum_out=vks[:, 32 + bg:33 + bg]),
                     reads=[f"dbq{bg // 4}", "vks", "esb"], writes=[f"dbq{bg // 4}", "vks_den"])
                P.op("vector", lambda e, bg=bg: e.reciprocal(vks[:, 48 + bg:49 + bg],
                                                             vks[:, 32 + bg:33 + bg]),
                     reads=["vks_den"], writes=["vks_r"])
                P.op("vector", lambda e, bg=bg: e.tensor_scalar(
                        wscl[bg][:], wpost2[:], vks[:, 48 + bg:49 + bg], None, ALU.mult),
                     reads=["wpost2", "vks_r"], writes=[f"wscl{bg}"])

            def topk(w):
                grp, half = divmod(w, 2)
                ntc, _ = sched[grp]
                valid = (ntc + 1) * 128
                pend = None
                for b in range(8):
                    bg = half * 8 + b
                    dsl = dbv[:, bg, 0:valid]
                    for r in range(8):
                        P.op("vector", lambda e, dsl=dsl, r=r, valid=valid: e.max(
                                m8t[:], dsl if r == 0 else sc0[:, 0:valid]),
                             reads=[f"dbq{bg // 4}" if r == 0 else "sc0"], writes=["m8t"])
                        if pend is not None and r == 1:
                            tail_ops(w, pend, valid)
                            pend = None
                        if r < 7:
                            P.op("vector", lambda e, dsl=dsl, r=r, valid=valid: e.match_replace(
                                    sc0[:, 0:valid], m8t[:],
                                    dsl if r == 0 else sc0[:, 0:valid], NEG_BIG),
                                 reads=["m8t", f"dbq{bg // 4}" if r == 0 else "sc0"], writes=["sc0"])
                    P.op("vector", lambda e, bg=bg: e.tensor_copy(vks[:, bg:bg + 1], m8t[:, 7:8]),
                         reads=["m8t"], writes=["vks"])
                    if grp == 0:
                        P.op("vector", lambda e, bg=bg: e.copy_predicated(
                                vks[:, bg:bg + 1], shortsel[:, bg:bg + 1],
                                self.cpool[:, 3:4]),
                             reads=["shortsel", "cpool", "vks"], writes=["vks"])
                    P.op("vector", lambda e, bg=bg: e.tensor_scalar(
                            vks[:, 16 + bg:17 + bg], vks[:, bg:bg + 1], -1.0, None, ALU.mult),
                         reads=["vks"], writes=["vks"])
                    pend = b
                tail_ops(w, pend, valid)

            def phase_B(w):
                grp, half = divmod(w, 2)
                ntc, _ = sched[grp]
                jcs = list(range(ntc + 1))
                tok0 = grp * 128 + half * 64
                P.op("scalar", lambda e, jc=jcs[0]: e.dma_start(
                        vc_sb[0][:], self.v_d[jc * 128:(jc + 1) * 128, :]),
                     reads=["v_d"], writes=["vc0"], dma="s")
                for jc in jcs:
                    vb, vkk = vc_sb[jc % 2], f"vc{jc % 2}"
                    if jc + 1 <= jcs[-1]:
                        nsl = (jc + 1) % 2
                        P.op("scalar", lambda e, jc=jc, nsl=nsl: e.dma_start(
                                vc_sb[nsl][:], self.v_d[(jc + 1) * 128:(jc + 2) * 128, :]),
                             reads=["v_d"], writes=[f"vc{nsl}"], dma="s")
                    for bq in range(2):
                        for bl in range(4):
                            b = bq * 4 + bl
                            bg = half * 8 + b
                            P.op("tensor", lambda e, bg=bg, bl=bl, jc=jc: e.matmul(
                                    ppost[:, bl * 128:(bl + 1) * 128],
                                    dbv[:, bg, jc * 128:(jc + 1) * 128], wscl[bg][:],
                                    start=True, stop=True),
                                 reads=[f"dbq{bg // 4}", f"wscl{bg}"], writes=["ppost"])
                        for bl in range(4):
                            b = bq * 4 + bl
                            P.op("scalar", lambda e, b=b, bl=bl: e.activation(
                                    pt2_sb[:].rearrange("p (h bi) -> p h bi", bi=64)[:, :, b * 8:(b + 1) * 8],
                                    ppost[:, bl * 128:(bl + 1) * 128].rearrange(
                                        "p (h i) -> p h i", i=8),
                                    AF.Identity),
                                 reads=["ppost"], writes=["pt20"])
                    for h in range(16):
                        ph = (h % 2) * 64
                        P.op("tensor", lambda e, h=h, ph=ph, vb=vb: e.matmul(
                                pout2[ph:ph + 64, (h // 2) * 64:(h // 2) * 64 + 64],
                                vb[:, h * 64:(h + 1) * 64],
                                pt2_sb[:, h * 64:(h + 1) * 64],
                                start=True, stop=True,
                                tile_position=(0, ph)),
                             reads=[vkk, "pt20"], writes=["pout2"])
                    P.op("scalar", lambda e: e.activation(pev[:], pout2[:], AF.Identity),
                         reads=["pout2"], writes=["pev"])
                    if jc == 0:
                        P.op("gpsimd", lambda e: e.tensor_copy(out2sb[:], pev[:]),
                             reads=["pev"], writes=["out2sb"])
                    else:
                        P.op("gpsimd", lambda e: e.tensor_add(out2sb[:], out2sb[:], pev[:]),
                             reads=["pev", "out2sb"], writes=["out2sb"])

                P.op("sync", lambda e, tok0=tok0: e.dma_start(
                        sg_sb[:].rearrange("q (c i) -> q c i", c=8),
                        sgT_v[:, :, tok0:tok0 + 64]),
                     reads=["sgT_d"], writes=["sg_sb"], dma="g")
                P.op("vector", lambda e: e.tensor_mul(sg_sb[:], out2sb[:], sg_sb[:]),
                     reads=["out2sb", "sg_sb"], writes=["sg_sb"])
                P.op("sync", lambda e, tok0=tok0: e.dma_start(
                        xr[0:64, :], self.x1_d[tok0:tok0 + 64, :]),
                     reads=["x_dstf1"], writes=["at_xr"], dma="g")
                for dms in range(8):
                    for hdc in range(8):
                        P.op("tensor", lambda e, dms=dms, hdc=hdc: e.matmul(
                                pwo[:, dms * 64:(dms + 1) * 64],
                                wo_sb[:, hdc * D + dms * 128: hdc * D + (dms + 1) * 128],
                                sg_sb[:, hdc * 64:(hdc + 1) * 64],
                                start=(hdc == 0), stop=(hdc == 7)),
                             reads=["wo_sb", "sg_sb"], writes=["pwo"])
                    P.op("scalar", lambda e, dms=dms: e.activation(
                            yTs[:], pwo[:, dms * 64:(dms + 1) * 64], AF.Identity),
                         reads=["pwo"], writes=["at_yTs"])
                    P.op("tensor", lambda e: e.transpose(
                            ptw[0:64, 0:128], yTs[:], self.ident_f[:]),
                         reads=["at_yTs", "ident_f"], writes=["ptw"])
                    P.op("vector", lambda e, dms=dms: e.tensor_add(
                            xr[0:64, dms * 128:(dms + 1) * 128],
                            xr[0:64, dms * 128:(dms + 1) * 128], ptw[0:64, 0:128]),
                         reads=["ptw", "at_xr"], writes=["at_xr"])
                P.op("gpsimd", lambda e, tok0=tok0: e.dma_start(
                        self.x2_d[tok0:tok0 + 64, :], xr[0:64, :]),
                     reads=["at_xr"], writes=["x2_d"], dma="g")

            phase_A(0)
            phase_A(1)
            for w in range(NW):
                topk(w)
                phase_B(w)
                if w + 2 < NW:
                    phase_A(w + 2)
            P.retire_all()
            P.flush(self.block)

    def stage_attn_v1(self, asub="full"):
        P, nc, cfg = self.P, self.nc, self.cfg
        self.asub = asub
        T = cfg.t_loc
        JCOL = cfg.jcols
        NG = T // 128
        with ExitStack() as stk:
            en = stk.enter_context
            db = [en(nc.sbuf_tensor(f"db{b}", [128, JCOL], BF16)) for b in range(16)]
            esb = en(nc.sbuf_tensor("esb", [128, JCOL], BF16))
            sc0 = en(nc.sbuf_tensor("sc0", [128, JCOL], BF16))
            sc1 = esb
            qn_sb = en(nc.sbuf_tensor("qn_sb", [64, 16 * 128], BF16))
            kc_sb = [en(nc.sbuf_tensor(f"kc{i}", [64, 16 * 128], BF16)) for i in range(2)]
            vc_sb = [en(nc.sbuf_tensor("vc0", [128, 1024], BF16))] * 2
            z_sb = en(nc.sbuf_tensor("z_sb", [128, 2048], BF16))
            mr_sb = [en(nc.sbuf_tensor(f"mr{i}", [128, 128], BF16)) for i in range(2)]
            pt2_sb = [en(nc.sbuf_tensor("pt20", [128, 2048], BF16))] * 2
            m8t = en(nc.sbuf_tensor("m8t", [128, 8], BF16))
            vks = en(nc.sbuf_tensor("vks", [128, 64], F32))
            wscl = [en(nc.sbuf_tensor(f"wscl{b}", [128, 128], BF16)) for b in range(16)]
            wpost2 = en(nc.sbuf_tensor("wpost2", [128, 128], BF16))
            out2sb = en(nc.sbuf_tensor("out2sb", [128, 1024], F32))
            sg_sb = en(nc.sbuf_tensor("sg_sb", [128, 1024], BF16))
            gt_sb = sg_sb
            wo_sb = en(nc.sbuf_tensor("wo_sb", [128, 8 * D], BF16))
            yTs = en(nc.sbuf_tensor("at_yTs", [128, 128], F32))
            ytok = out2sb
            xr = en(nc.sbuf_tensor("at_xr", [128, D], F32))
            shortsel = en(nc.sbuf_tensor("shortsel", [128, 16], mybir.dt.uint32))
            ivec_sb = en(nc.sbuf_tensor("ivec_sb", [128, NG], F32))
            negbase_sb = en(nc.sbuf_tensor("negbase_sb", [128, cfg.n_jc], F32))
            colio_sb = en(nc.sbuf_tensor("colio_sb", [128, 128], F32))
            ivs = en(nc.sbuf_tensor("ivs", [128, 2], F32))
            msk = en(nc.sbuf_tensor("msk", [128, 128], BF16))
            pbig = en(nc.psum_tensor("pbig", [128, 512], F32))
            pgat = en(nc.psum_tensor("pgat", [128, 1024], BF16))
            ppre = en(nc.psum_tensor("ppre", [128, 512], F32))
            ppost = en(nc.psum_tensor("ppost", [128, 512], F32))
            pout2 = en(nc.psum_tensor("pout2", [128, 1024], F32))
            pwo = en(nc.psum_tensor("pwo", [128, 512], F32))
            ptw = en(nc.psum_tensor("ptw", [128, 512], F32))

            P.op("sync", lambda e: e.dma_start(shortsel[:], self.shortsel_in),
                 writes=["shortsel"], dma="g")
            P.op("sync", lambda e: e.dma_start(ivec_sb[:], self.ivec_in),
                 writes=["ivec_sb"], dma="g")
            P.op("sync", lambda e: e.dma_start(negbase_sb[:], self.negbase_in),
                 writes=["negbase_sb"], dma="g")
            P.op("sync", lambda e: e.dma_start(colio_sb[:], self.colio_in),
                 writes=["colio_sb"], dma="g")
            P.op("vector", lambda e: e.tensor_copy(
                    wpost2[:].rearrange("p (h i) -> p h i", i=8),
                    self.wpre_t[:].rearrange("p (i h) -> p h i", h=16) if False else
                    self.wpost_t[:].rearrange("p (i h) -> p h i", h=16)),
                 reads=["wpost_t"], writes=["wpost2"])
            for hdc in range(8):
                P.op("sync", lambda e, hdc=hdc: e.dma_start(
                        xr[:], self.W_in["Wo"][hdc * 128:(hdc + 1) * 128, :]),
                     writes=["at_xr"], dma="g")
                P.op("vector", lambda e, hdc=hdc: e.tensor_copy(
                        wo_sb[:, hdc * D:(hdc + 1) * D], xr[:]),
                     reads=["at_xr"], writes=["wo_sb"])

            kT_v = self.kT_d.rearrange("(g p) j -> p g j", p=64)
            qnT_v = self.qnT_d.rearrange("(g p) t -> p g t", p=64)

            sched = cfg.sched()
            for grp in range(NG):
                ntc, mpos = sched[grp]
                jcs = list(range(0, ntc + 1))
                valid = (ntc + 1) * 128

                for hds in range(4):
                    P.op("sync", lambda e, hds=hds, grp=grp: e.dma_start(
                            qn_sb[:].rearrange("p (g t) -> p g t", g=16)[:, hds * 4:(hds + 1) * 4, :],
                            qnT_v[:, hds * 4:(hds + 1) * 4, grp * 128:(grp + 1) * 128]),
                         reads=["Wq_d"], writes=["qn_sb"], dma="g")

                for jc in jcs:
                    kb = kc_sb[jc % 2]
                    kk = f"kc{jc % 2}"
                    P.op("sync", lambda e, jc=jc, kb=kb: e.dma_start(
                            kb[:].rearrange("p (g j) -> p g j", g=16),
                            kT_v[:, :, jc * 128:(jc + 1) * 128]),
                         reads=["kT_d"], writes=[kk], dma="g")
                    for gq in range(4):
                        for gl in range(4):
                            g = gq * 4 + gl
                            P.op("tensor", lambda e, g=g, gl=gl, kb=kb: e.matmul(
                                    pbig[:, gl * 128:(gl + 1) * 128],
                                    kb[0:64, g * 128:(g + 1) * 128],
                                    qn_sb[0:64, g * 128:(g + 1) * 128],
                                    start=True, stop=True),
                                 reads=[kk, "qn_sb"], writes=["pbig"])
                        P.op("scalar", lambda e, gq=gq: e.activation(
                                z_sb[:].rearrange("p (i g) -> p g i", g=16)[:, gq * 4:(gq + 1) * 4, :],
                                pbig[:], AF.Identity),
                             reads=["pbig"], writes=["z_sb"])
                    for b in range(16):
                        if self.asub == "A1":
                            break
                        sl = b % 2
                        P.op("tensor", lambda e, b=b: e.transpose(
                                pgat[:, 0:128],
                                z_sb[:, b * 128:(b + 1) * 128], self.ident[:]),
                             reads=["z_sb", "ident"], writes=["pgat"])
                        P.op("vector", lambda e, b=b, sl=sl: e.tensor_copy(
                                mr_sb[sl][:], pgat[:, 0:128]),
                             reads=["pgat"], writes=[f"mr{sl}"])
                        if self.asub == "A2":
                            continue
                        P.op("tensor", lambda e, b=b, sl=sl: e.matmul(
                                ppre[:, 0:128], self.wpre_t[:], mr_sb[sl][:],
                                start=True, stop=True),
                             reads=["wpre_t", f"mr{sl}"], writes=["ppre"])
                        if jc == 0:
                            P.op("vector", lambda e: e.tensor_tensor(
                                    ppre[:, 0:128], ppre[:, 0:128],
                                    self.dmask[:], ALU.add),
                                 reads=["ppre", "dmask"], writes=["ppre"])
                        if jc in mpos:
                            off = float(8 * b)
                            P.op("vector", lambda e, grp=grp, off=off, jc=jc: e.tensor_scalar(
                                    ivs[:, 0:1], ivec_sb[:, grp:grp + 1],
                                    negbase_sb[:, jc:jc + 1], off, ALU.add, op1=ALU.add),
                                 reads=["ivec_sb", "negbase_sb"], writes=["ivs"])
                            P.op("vector", lambda e: e.tensor_scalar(
                                    msk[:], colio_sb[:], ivs[:, 0:1], None, ALU.is_gt),
                                 reads=["colio_sb", "ivs"], writes=["msk"])
                            P.op("vector", lambda e: e.scalar_tensor_tensor(
                                    ppre[:, 0:128], msk[:], NEG_BIG,
                                    ppre[:, 0:128], ALU.mult, ALU.add),
                                 reads=["msk", "ppre"], writes=["ppre"])
                        P.op("scalar", lambda e, b=b, jc=jc: e.activation(
                                db[b][:, jc * 128:(jc + 1) * 128],
                                ppre[:, 0:128], AF.Identity),
                             reads=["ppre"], writes=[f"db{b}"])

                if self.asub.startswith("A"):
                    continue
                for b in range(16):
                    dcur, kcur = db[b], f"db{b}"
                    for r in range(8):
                        P.op("vector", lambda e, dcur=dcur, valid=valid: e.max(
                                m8t[:], dcur[:, 0:valid]),
                             reads=[kcur], writes=["m8t"])
                        if r < 7:
                            dnew, knew = (sc0, "sc0") if r % 2 == 0 else (sc1, "esb")
                            P.op("vector", lambda e, dcur=dcur, dnew=dnew, valid=valid:
                                    e.match_replace(dnew[:, 0:valid], m8t[:],
                                                    dcur[:, 0:valid], NEG_BIG),
                                 reads=["m8t", kcur], writes=[knew])
                            dcur, kcur = dnew, knew
                    P.op("vector", lambda e, b=b: e.tensor_copy(vks[:, b:b + 1], m8t[:, 7:8]),
                         reads=["m8t"], writes=["vks"])
                    if grp == 0:
                        P.op("vector", lambda e, b=b: e.copy_predicated(
                                vks[:, b:b + 1], shortsel[:, b:b + 1],
                                self.cpool[:, 3:4]),
                             reads=["shortsel", "cpool", "vks"], writes=["vks"])
                    P.op("vector", lambda e, b=b: e.tensor_scalar(
                            vks[:, 16 + b:17 + b], vks[:, b:b + 1], -1.0, None, ALU.mult),
                         reads=["vks"], writes=["vks"])
                    P.op("scalar", lambda e, b=b, valid=valid: e.activation(
                            esb[:, 0:valid], db[b][:, 0:valid], AF.Exp,
                            bias=vks[:, 16 + b:17 + b]),
                         reads=[f"db{b}", "vks"], writes=["esb"])
                    P.op("vector", lambda e, b=b, valid=valid: e.scalar_tensor_tensor(
                            db[b][:, 0:valid], db[b][:, 0:valid], vks[:, b:b + 1],
                            esb[:, 0:valid], ALU.is_ge, ALU.mult,
                            accum_out=vks[:, 32 + b:33 + b]),
                         reads=[f"db{b}", "vks", "esb"], writes=[f"db{b}", "vks_den"])
                for b in range(16):
                    P.op("vector", lambda e, b=b: e.reciprocal(vks[:, 48 + b:49 + b],
                                                               vks[:, 32 + b:33 + b]),
                         reads=["vks_den"], writes=["vks_r"])
                    P.op("vector", lambda e, b=b: e.tensor_scalar(
                            wscl[b][:], wpost2[:], vks[:, 48 + b:49 + b], None, ALU.mult),
                         reads=["wpost2", "vks_r"], writes=[f"wscl{b}"])

                for jc in jcs:
                    vb = vc_sb[jc % 2]
                    vkk = "vc0"
                    P.op("sync", lambda e, jc=jc, vb=vb: e.dma_start(
                            vb[:], self.v_d[jc * 128:(jc + 1) * 128, :]),
                         reads=["v_d"], writes=[vkk], dma="g")
                    p2 = pt2_sb[jc % 2]
                    p2k = "pt20"
                    for b in range(16):
                        P.op("tensor", lambda e, b=b, jc=jc: e.matmul(
                                ppost[:, 0:128],
                                db[b][:, jc * 128:(jc + 1) * 128], wscl[b][:],
                                start=True, stop=True),
                             reads=[f"db{b}", f"wscl{b}"], writes=["ppost"])
                        P.op("vector", lambda e, b=b, p2=p2: e.tensor_copy(
                                p2[:].rearrange("p (h bi) -> p h bi", bi=128)[:, :, b * 8:(b + 1) * 8],
                                ppost[:, 0:128].rearrange(
                                    "p (h i) -> p h i", i=8)),
                             reads=["ppost"], writes=[p2k])
                    for h in range(16):
                        ph = (h % 2) * 64
                        P.op("tensor", lambda e, h=h, ph=ph, vb=vb, p2=p2: e.matmul(
                                pout2[ph:ph + 64, (h // 2) * 128:(h // 2) * 128 + 128],
                                vb[:, h * 64:(h + 1) * 64],
                                p2[:, h * 128:(h + 1) * 128],
                                start=True, stop=True,
                                tile_position=(0, ph)),
                             reads=[vkk, p2k], writes=["pout2"])
                    if jc == 0:
                        P.op("vector", lambda e: e.tensor_copy(out2sb[:], pout2[:]),
                             reads=["pout2"], writes=["out2sb"])
                    else:
                        P.op("vector", lambda e: e.tensor_add(out2sb[:], out2sb[:], pout2[:]),
                             reads=["pout2", "out2sb"], writes=["out2sb"])

                for hds in range(8):
                    P.op("sync", lambda e, hds=hds, grp=grp: e.dma_start(
                            sg_sb[:, hds * 128:(hds + 1) * 128],
                            self.sgT_d[hds * 128:(hds + 1) * 128, grp * 128:(grp + 1) * 128]),
                         reads=["sgT_d"], writes=["sg_sb"], dma="g")
                P.op("vector", lambda e: e.tensor_mul(gt_sb[:], out2sb[:], sg_sb[:]),
                     reads=["out2sb", "sg_sb"], writes=["sg_sb"])
                for dms in range(8):
                    for hdc in range(8):
                        P.op("tensor", lambda e, dms=dms, hdc=hdc: e.matmul(
                                pwo[:, 0:128],
                                wo_sb[:, hdc * D + dms * 128: hdc * D + (dms + 1) * 128],
                                gt_sb[:, hdc * 128:(hdc + 1) * 128],
                                start=(hdc == 0), stop=(hdc == 7)),
                             reads=["wo_sb", "sg_sb"], writes=["pwo"])
                    P.op("vector", lambda e, dms=dms: e.tensor_copy(yTs[:], pwo[:, 0:128]),
                         reads=["pwo"], writes=["at_yTs"])
                    P.op("tensor", lambda e, dms=dms: e.transpose(
                            ptw[:, 0:128], yTs[:], self.ident_f[:]),
                         reads=["at_yTs", "ident_f"], writes=["ptw"])
                    P.op("vector", lambda e, dms=dms: e.tensor_copy(
                            ytok[:, dms * 128:(dms + 1) * 128], ptw[:, 0:128]),
                         reads=["ptw"], writes=["out2sb"])
                P.op("sync", lambda e, grp=grp: e.dma_start(
                        xr[:], self.x1_d[grp * 128:(grp + 1) * 128, :]),
                     reads=["x_dstf1"], writes=["at_xr"], dma="g")
                P.op("vector", lambda e: e.tensor_add(xr[:], xr[:], ytok[:]),
                     reads=["at_xr", "out2sb"], writes=["at_xr"])
                P.op("gpsimd", lambda e, grp=grp: e.dma_start(
                        self.x2_d[grp * 128:(grp + 1) * 128, :], xr[:]),
                     reads=["at_xr"], writes=["x2_d"], dma="g")
            P.retire_all()
            P.flush(self.block)


    def finish(self):
        deps = self.P.barrier_deps()

        def fin(eng, deps=deps):
            for sem, val in deps:
                eng.wait_ge(sem, val)
        self.P.flush(self.block)
        self.block.vector(fin)
        self.ctx.close()
        return self.nc


def build_x1(cfg: Cfg):
    k = Kern(cfg)
    P, nc = k.P, k.nc
    k.stage_consts()
    with nc.sbuf_tensor("wfcqT_sb", [128, 8 * HID], BF16) as wfcqT_sb, \
         nc.sbuf_tensor("wprojqT_sb", [128, 32 * D], BF16) as wprojqT_sb:
        k.stage_quant(wfcqT_sb, wprojqT_sb)
        k.putil_ctx.close()
        k.stage_ffn(k.x_in, k.x1_d, wfcqT_sb, wprojqT_sb, "f1", n_tok=cfg.t_full)
        P.flush(k.block)
    cp = k.sbuf("cp", [128, D], F32)
    for t in range(cfg.n_ttiles):
        P.op("sync", lambda e, t=t: e.dma_start(cp[:], k.x1_d[t * 128:(t + 1) * 128, :]),
             reads=["x_dstf1"], writes=["cp"], dma="g")
        P.op("gpsimd", lambda e, t=t: e.dma_start(k.y_out[t * 128:(t + 1) * 128, :], cp[:]),
             reads=["cp"], writes=["y_out"], dma="g")
    return k.finish(), k.din

def build_full(cfg: Cfg):
    k = Kern(cfg)
    P, nc = k.P, k.nc
    k.stage_consts()
    with nc.sbuf_tensor("wfcqT_sb", [128, 8 * HID], BF16) as wfcqT_sb, \
         nc.sbuf_tensor("wprojqT_sb", [128, 32 * D], BF16) as wprojqT_sb:
        k.stage_quant(wfcqT_sb, wprojqT_sb)
        k.P.retire_all()
        k.P.flush(k.block)
        k.putil_ctx.close()
        k.stage_ffn(k.x_in, k.x1_d, wfcqT_sb, wprojqT_sb, "f1", n_tok=cfg.t_full)
    k.stage_proj()
    k.stage_kv_assemble()
    k.stage_attn()
    with nc.sbuf_tensor("wfcqT_sb2", [128, 8 * HID], BF16) as w1, \
         nc.sbuf_tensor("wprojqT_sb2", [128, 32 * D], BF16) as w2:
        for dmc in range(8):
            P.op("sync", lambda e, dmc=dmc: e.dma_start(
                    w1[:, dmc * HID:(dmc + 1) * HID], k.wfcqT_d[dmc * 128:(dmc + 1) * 128, :]),
                 reads=["wfcqT_d"], writes=["wfcqT_sb"], dma="g")
        for hc in range(32):
            P.op("sync", lambda e, hc=hc: e.dma_start(
                    w2[:, hc * D:(hc + 1) * D], k.wprojqT_d[hc * 128:(hc + 1) * 128, :]),
                 reads=["wprojqT_d"], writes=["wprojqT_sb"], dma="g")
        k.stage_ffn(k.x2_d, k.y_out, w1, w2, "f2")
    return k.finish(), k.din


def build_quant_dbg(cfg: Cfg, upto="quant", sub="all"):
    k = Kern(cfg)
    P, nc = k.P, k.nc
    k.stage_consts()
    if upto == "consts":
        cp = k.sbuf("cpdbg", [128, D], F32)
        P.op("vector", lambda e: e.memset(cp[:], 1.0), writes=["cpdbg"])
        for t in range(cfg.t_loc // 128):
            P.op("gpsimd", lambda e, t=t: e.dma_start(k.y_out[t * 128:(t + 1) * 128, :], cp[:]),
                 reads=["cpdbg"], writes=["y_out"], dma="g")
        return k.finish(), k.din
    with nc.sbuf_tensor("wfcqT_sb", [128, 8 * HID], BF16) as wfcqT_sb, \
         nc.sbuf_tensor("wprojqT_sb", [128, 32 * D], BF16) as wprojqT_sb:
        k.stage_quant(wfcqT_sb, wprojqT_sb, sub=sub)
        P.flush(k.block)
    if sub != "all":
        cp = k.sbuf("cpdbg", [128, D], F32)
        P.op("vector", lambda e: e.memset(cp[:], 2.0), writes=["cpdbg"])
        for t in range(cfg.t_loc // 128):
            P.op("gpsimd", lambda e, t=t: e.dma_start(k.y_out[t * 128:(t + 1) * 128, :], cp[:]),
                 reads=["cpdbg"], writes=["y_out"], dma="g")
        return k.finish(), k.din
    cp = k.sbuf("cpdbg", [128, D], F32)
    cpb = k.sbuf("cpdbgb", [128, D], BF16)
    for t in range(8):
        P.op("sync", lambda e, t=t: e.dma_start(cpb[:], k.wfcqT_d[t * 128:(t + 1) * 128, 0:D]),
             reads=["wfcqT_d"], writes=["cpdbgb"], dma="g")
        P.op("vector", lambda e, t=t: e.tensor_copy(cp[:], cpb[:]),
             reads=["cpdbgb"], writes=["cpdbg"])
        P.op("gpsimd", lambda e, t=t: e.dma_start(k.y_out[t * 128:(t + 1) * 128, :], cp[:]),
             reads=["cpdbg"], writes=["y_out"], dma="g")
    return k.finish(), k.din


def build_dbg(cfg: Cfg, upto="proj", asub="full"):
    k = Kern(cfg)
    P, nc = k.P, k.nc
    k.stage_consts()
    with nc.sbuf_tensor("wfcqT_sb", [128, 8 * HID], BF16) as wfcqT_sb, \
         nc.sbuf_tensor("wprojqT_sb", [128, 32 * D], BF16) as wprojqT_sb:
        k.stage_quant(wfcqT_sb, wprojqT_sb)
        k.P.retire_all()
        k.P.flush(k.block)
        k.putil_ctx.close()
        k.stage_ffn(k.x_in, k.x1_d, wfcqT_sb, wprojqT_sb, "f1", n_tok=cfg.t_full)
    k.stage_proj()
    if upto == "proj":
        cp = k.sbuf("cpdbg", [128, D], F32)
        cpb = k.sbuf("cpdbgb", [128, D], BF16)
        for t in range(8):
            P.op("sync", lambda e, t=t: e.dma_start(cpb[:], k.qnT_d[t * 128:(t + 1) * 128, 0:D]),
                 reads=["Wq_d"], writes=["cpdbgb"], dma="g")
            P.op("vector", lambda e, t=t: e.tensor_copy(cp[:], cpb[:]),
                 reads=["cpdbgb"], writes=["cpdbg"])
            P.op("gpsimd", lambda e, t=t: e.dma_start(k.y_out[t * 128:(t + 1) * 128, :], cp[:]),
                 reads=["cpdbg"], writes=["y_out"], dma="g")
        return k.finish(), k.din
    k.stage_kv_assemble()
    k.stage_attn_v1(asub=asub)
    if asub != "full":
        cp = k.sbuf("cpdbg", [128, D], F32)
        P.op("vector", lambda e: e.memset(cp[:], 5.0), writes=["cpdbg"])
        for t in range(cfg.t_loc // 128):
            P.op("gpsimd", lambda e, t=t: e.dma_start(k.y_out[t * 128:(t + 1) * 128, :], cp[:]),
                 reads=["cpdbg"], writes=["y_out"], dma="g")
        return k.finish(), k.din
    cp = k.sbuf("cpdbg", [128, D], F32)
    for t in range(cfg.t_loc // 128):
        P.op("sync", lambda e, t=t: e.dma_start(cp[:], k.x2_d[t * 128:(t + 1) * 128, :]),
             reads=["x2_d"], writes=["cpdbg"], dma="g")
        P.op("gpsimd", lambda e, t=t: e.dma_start(k.y_out[t * 128:(t + 1) * 128, :], cp[:]),
             reads=["cpdbg"], writes=["y_out"], dma="g")
    return k.finish(), k.din


from concourse.bass_utils import run_bass_kernel_spmd


def kernel(**inputs):
    SEQ, TLOC, NCORES = 4096, 2048, 8
    base = {
        "Wq": inputs["L0_attn_Wq"], "Wk": inputs["L0_attn_Wk"], "Wv": inputs["L0_attn_Wv"],
        "Wg": inputs["L0_attn_Wg"], "Wo": inputs["L0_attn_Wo"],
        "bg": inputs["L0_attn_bg"],
        "Wpre": inputs["L0_attn_W_pre_th"], "Wpost": inputs["L0_attn_W_post_th"],
        "memk": inputs["L0_attn_mem_k"], "memv": inputs["L0_attn_mem_v"],
        "hs": np.ascontiguousarray(inputs["L0_attn_head_scale_params"]).reshape(16),
        "Wfc": inputs["L0_ffn_W_fc"], "Wproj": inputs["L0_ffn_W_proj"],
        "lnw": inputs["L0_ffn_ln_w"], "lnb": inputs["L0_ffn_ln_b"],
    }
    base = {k: np.ascontiguousarray(v, dtype=np.float32) for k, v in base.items()}
    x = np.ascontiguousarray(inputs["L0_attn_x"], dtype=np.float32)

    in_maps = []
    tokmaps = []
    cfg0 = None
    for c in range(NCORES):
        beta, eta = c // 2, c % 2
        blocks = [(cc * 128, 128) for cc in range(SEQ // 128) if cc % 2 == eta]
        pblocks = [(cc * 128, 128) for cc in range(SEQ // 128) if cc % 2 != eta]
        cfg = Cfg(seq=SEQ, t_loc=TLOC, tok_blocks=blocks)
        if cfg0 is None:
            cfg0 = cfg
        tokmap = np.concatenate([np.arange(gs, gs + ln) for gs, ln in blocks])
        ptokmap = np.concatenate([np.arange(gs, gs + ln) for gs, ln in pblocks])
        tokmaps.append((beta, tokmap))
        m = dict(base)
        m.update(cfg.host_consts())
        m["x_in"] = np.ascontiguousarray(
            np.concatenate([x[beta][tokmap], x[beta][ptokmap]], axis=0))
        in_maps.append(m)

    nc, _ = build_full(cfg0)
    res = run_bass_kernel_spmd(nc, in_maps, core_ids=list(range(NCORES)))
    out = np.zeros((4, SEQ, 1024), dtype=np.float32)
    for c in range(NCORES):
        beta, tokmap = tokmaps[c]
        out[beta][tokmap] = res.results[c]["y_out"]
    return out


# version 13
# speedup vs baseline: 1.1433x; 1.0106x over previous
import math
from contextlib import ExitStack

import numpy as np
import concourse.bass as bass
import concourse.mybir as mybir

F32 = mybir.dt.float32
BF16 = mybir.dt.bfloat16
AF = mybir.ActivationFunctionType
ALU = mybir.AluOpType
AX = mybir.AxisListType

D = 1024
H = 16
DH = 64
HID = 4096
NMEM = 4
TOPK = 64
LN_EPS = 1e-5
ABS_EPS = 1e-6
NEG_BIG = -3.3895314e38
VK_SHORT = -30.0
QK_SCALE = 10.0

ENGS = ("sync", "tensor", "vector", "scalar", "gpsimd")
DMA_ENGS = ("sync", "gpsimd", "scalar", "vector")


class Prog:

    def __init__(self, nc, ctx):
        self.nc = nc
        self.ctx = ctx
        self.items = {e: [] for e in ENGS}
        self.esem = {e: ctx.enter_context(nc.semaphore(f"sem_{e}")) for e in ENGS}
        self.ecnt = {e: 0 for e in ENGS}
        self.dsem = {}
        self.last_w = {}
        self.readers = {}
        self.waited = {e: {} for e in ENGS}
        self.gdep = {}

    def _dma_sem(self, name, eng):
        if name not in self.dsem:
            self.dsem[name] = [self.ctx.enter_context(self.nc.semaphore(f"d_{name}")), 0, eng]
        return self.dsem[name]

    def _wait(self, eng, dep):
        kind, a, b = dep
        if kind == "d":
            rec = self.dsem[a]
            sem, val = rec[0], rec[1]
        else:
            sem, val = a, b
            if eng == "tensor" and sem is self.esem["tensor"]:
                return
        if self.waited[eng].get(sem, -1) >= val:
            return
        self.items[eng].append(("w", sem, val))
        self.waited[eng][sem] = val

    def retire_all(self):
        for m in (self.last_w, self.readers):
            for deps in m.values():
                for dep in deps:
                    kind, a, b = dep
                    if kind == "d":
                        rec = self.dsem[a]
                        sem, val = rec[0], rec[1]
                    else:
                        sem, val = a, b
                    if self.gdep.get(sem, -1) < val:
                        self.gdep[sem] = val
        self.last_w.clear()
        self.readers.clear()

    def op(self, eng, emit, reads=(), writes=(), dma=None):
        for sem, val in self.gdep.items():
            self._wait(eng, ("e", sem, val))
        for b in reads:
            for dep in self.last_w.get(b, ()):
                self._wait(eng, dep)
        for b in writes:
            for dep in self.last_w.get(b, ()):
                self._wait(eng, dep)
            for dep in self.readers.get(b, ()):
                self._wait(eng, dep)
        if dma is not None:
            assert len(writes) >= 1
            dma = "b_" + writes[0]
            rec = self._dma_sem(dma, eng)
            rec[1] += 16
            sem, n = rec[0], 16
            done = ("d", dma, rec[1])
        else:
            sem, n = self.esem[eng], 1
            self.ecnt[eng] += 1
            done = ("e", sem, self.ecnt[eng])
        self.items[eng].append(("op", emit, sem, n))
        for b in reads:
            self.readers.setdefault(b, []).append(done)
        for b in writes:
            self.last_w[b] = [done]
            self.readers[b] = []
        return done

    def barrier_deps(self):
        deps = [(self.esem[e], self.ecnt[e]) for e in ENGS if self.ecnt[e]]
        deps += [(rec[0], rec[1]) for rec in self.dsem.values() if rec[1]]
        return deps

    def flush(self, block):
        for e in ENGS:
            items = self.items[e]
            if not items:
                continue
            self.items[e] = []

            def emit_all(eng, items=items):
                for it in items:
                    if it[0] == "w":
                        eng.wait_ge(it[1], it[2])
                    else:
                        _, emitfn, sem, n = it
                        emitfn(eng).then_inc(sem, n)
            getattr(block, e)(emit_all)


class Cfg:
    def __init__(self, seq=4096, t_loc=2048, tok_blocks=None, debug_out=None):
        self.seq = seq
        self.t_loc = t_loc
        self.t_full = seq
        self.tok_blocks = tok_blocks or [(0, t_loc)]
        assert sum(l for _, l in self.tok_blocks) == t_loc
        self.no_exchange = (t_loc == seq)
        self.pair_blocks = None
        self.debug_out = debug_out
        self.n_ttiles = t_loc // 128
        self.n_jc = 1 + seq // 128
        self.jcols = 128 + seq

    def g_tok(self, l):
        off = 0
        for gs, ln in self.tok_blocks:
            if l < off + ln:
                return gs + (l - off)
            off += ln
        raise ValueError(l)

    def rank_blocks(self, r):
        blocks = self.pair_blocks[r]
        out, off = [], 0
        for gs, ln in blocks:
            out.append((gs, off, ln))
            off += ln
        return out

    def pos_local_chunk(self, m):
        if self.no_exchange:
            return m
        nloc = self.t_loc // 128
        return (m // 2) if m % 2 == 0 else nloc + m // 2

    def pos_global_chunk(self, m):
        if self.no_exchange:
            return m
        own = [gs // 128 for gs, _ in self.tok_blocks]
        all_ = set(range(self.seq // 128))
        part = sorted(all_ - set(own))
        return own[m // 2] if m % 2 == 0 else part[m // 2]

    def sched(self):
        NG = self.t_loc // 128
        out = []
        for k in range(NG):
            if self.no_exchange:
                ntc = k + 1
                mpos = {k + 1}
            else:
                ntc = 2 * k + 2
                mpos = {2 * k + 1, 2 * k + 2}
            out.append((ntc, mpos))
        return out

    def host_consts(self):
        import ml_dtypes
        ident = np.eye(128, dtype=np.float32)
        dmask = np.zeros((128, 128), dtype=np.float32)
        dmask[:, NMEM:] = NEG_BIG
        NG = self.t_loc // 128
        ivec = np.zeros((128, NG), np.float32)
        ss = np.zeros((128, 16), np.float32)
        for k in range(NG):
            g0 = self.g_tok(k * 128)
            ivec[:, k] = g0 + np.arange(128) // 16
        for b in range(16):
            g0 = self.g_tok(0)
            ss[:, b] = ((g0 + 8 * b + np.arange(128) // 16 + 5) < TOPK).astype(np.float32)
        njc = self.n_jc
        negbase = np.zeros((128, njc), np.float32)
        for m in range(njc - 1):
            negbase[:, m + 1] = -128.0 * self.pos_global_chunk(m)
        return {"ident_c": ident.astype(ml_dtypes.bfloat16),
                "dmask_c": dmask.astype(ml_dtypes.bfloat16),
                "shortsel_c": ss.astype(np.uint32),
                "ivec_c": ivec,
                "negbase_c": negbase,
                "colio_c": np.broadcast_to(np.arange(128, dtype=np.float32)[None, :],
                                           (128, 128)).copy()}


class Kern:

    def __init__(self, cfg: Cfg):
        self.cfg = cfg
        nc = self.nc = bass.Bass("TRN2", target_bir_lowering=False, debug=False)
        T, SEQ, JCOL = cfg.t_loc, cfg.seq, cfg.jcols
        self.din = {}

        def dt_in(name, shape, dtype=F32):
            self.din[name] = nc.dram_tensor(name, shape, dtype, kind="ExternalInput").ap()
            return self.din[name]

        self.x_in = dt_in("x_in", [cfg.t_full, D])
        self.W_in = {w: dt_in(w, [D, D]) for w in ("Wq", "Wk", "Wv", "Wg", "Wo")}
        self.bg_in = dt_in("bg", [D])
        self.Wpre_in = dt_in("Wpre", [H, H])
        self.Wpost_in = dt_in("Wpost", [H, H])
        self.memk_in = dt_in("memk", [H, NMEM, DH])
        self.memv_in = dt_in("memv", [H, NMEM, DH])
        self.hs_in = dt_in("hs", [H])
        self.Wfc_in = dt_in("Wfc", [HID, D])
        self.Wproj_in = dt_in("Wproj", [D, HID])
        self.lnw_in = dt_in("lnw", [HID])
        self.lnb_in = dt_in("lnb", [HID])
        self.ident_in = dt_in("ident_c", [128, 128], BF16)
        self.dmask_in = dt_in("dmask_c", [128, 128], BF16)
        self.shortsel_in = dt_in("shortsel_c", [128, 16], mybir.dt.uint32)
        self.ivec_in = dt_in("ivec_c", [128, cfg.t_loc // 128], F32)
        self.negbase_in = dt_in("negbase_c", [128, cfg.n_jc], F32)
        self.colio_in = dt_in("colio_c", [128, 128], F32)

        self.y_out = nc.dram_tensor("y_out", [T, D], F32, kind="ExternalOutput").ap()

        dram = lambda name, shape, dtype: nc.dram_tensor(name, shape, dtype, kind="Internal").ap()
        self.x1_d = dram("x1_d", [cfg.t_full, D], F32)
        self.x2_d = dram("x2_d", [T, D], F32)
        self.wfcqT_d = dram("wfcqT_d", [D, HID], BF16)
        self.wprojqT_d = dram("wprojqT_d", [HID, D], BF16)
        self.qnT_d = dram("qnT_d", [D, T], BF16)
        self.sgT_d = dram("sgT_d", [D, T], BF16)
        self.knT_own_d = dram("knT_own_d", [D, cfg.t_full], BF16)
        self.v_own_d = dram("v_own_d", [cfg.t_full, D], BF16)
        self.kT_d = dram("kT_d", [D, JCOL], BF16)
        self.v_d = dram("v_d", [JCOL, D], BF16)

        self.ctx = ExitStack()
        sbuf = self.sbuf = lambda name, shape, dtype: self.ctx.enter_context(
            nc.sbuf_tensor(name, shape, dtype))

        self.ident = sbuf("ident", [128, 128], BF16)
        self.ident_f = sbuf("ident_f", [128, 128], F32)
        self.dmask = sbuf("dmask", [128, 128], BF16)
        self.ones128_b = sbuf("ones128_b", [128, 1], BF16)
        self.ones128_f = sbuf("ones128_f", [128, 1], F32)
        self.onesrow_b = sbuf("onesrow_b", [1, 128], BF16)
        self.onesrow_f = sbuf("onesrow_f", [1, 128], F32)
        self.onesh_b = sbuf("onesh_b", [128, 2], BF16)
        self.onesh_bq = sbuf("onesh_bq", [128, 2], BF16)
        self.cpool = sbuf("cpool", [128, 4], F32)
        self.lnw_sb = sbuf("lnw_sb", [128, 32], F32)
        self.lnb_sb = sbuf("lnb_sb", [128, 32], F32)
        self.wpre_t = sbuf("wpre_t", [128, 128], BF16)
        self.wpost_t = sbuf("wpost_t", [128, 128], BF16)

        self.putil_ctx = ExitStack()
        self.putil = self.putil_ctx.enter_context(nc.psum_tensor("putil", [128, 128], F32))
        self.P = Prog(nc, self.ctx)
        self.block = self.ctx.enter_context(nc.Block())

    def load_vec_T(self, vec_in, n, dst, key, psum_ap=None, alloc=None):
        P, nc = self.P, self.nc
        nch = n // 128
        stk = key + "_stg"
        stg = (alloc or self.sbuf)(key + "_stg", [nch, 128], F32)
        pst = (psum_ap if psum_ap is not None else self.putil)[:, 0:nch]
        P.op("sync", lambda e: e.dma_start(stg[:], vec_in.rearrange("(c p) -> c p", p=128)),
             writes=[stk], dma="g")
        P.op("tensor", lambda e: e.transpose(pst, stg[:], self.ident_f[0:nch, 0:nch]),
             reads=[stk, "ident_f"], writes=["putil"])
        P.op("vector", lambda e: e.tensor_copy(dst[:], pst), reads=["putil"], writes=[key])

    def stage_consts(self):
        P, nc = self.P, self.nc
        P.op("sync", lambda e: e.dma_start(self.ident[:], self.ident_in), writes=["ident"], dma="g")
        P.op("sync", lambda e: e.dma_start(self.dmask[:], self.dmask_in),
             writes=["dmask"], dma="g")
        P.op("vector", lambda e: e.tensor_copy(self.ident_f[:], self.ident[:]),
             reads=["ident"], writes=["ident_f"])
        self.load_vec_T(self.lnw_in, 4096, self.lnw_sb, "lnw_sb")
        self.load_vec_T(self.lnb_in, 4096, self.lnb_sb, "lnb_sb")
        P.op("vector", lambda e: e.memset(self.ones128_b[:], 1.0), writes=["ones128_b"])
        P.op("vector", lambda e: e.memset(self.ones128_f[:], 1.0), writes=["ones128_f"])
        P.op("vector", lambda e: e.memset(self.onesrow_b[:], 1.0), writes=["onesrow_b"])
        P.op("vector", lambda e: e.memset(self.onesrow_f[:], 1.0), writes=["onesrow_f"])
        P.op("vector", lambda e: e.memset(self.cpool[:, 0:1], float(ABS_EPS)), writes=["cpool"])
        P.op("vector", lambda e: e.memset(self.cpool[:, 1:2], float(LN_EPS)), writes=["cpool"])
        P.op("vector", lambda e: e.memset(self.cpool[:, 2:3], float(np.log(QK_SCALE))), writes=["cpool"])
        P.op("vector", lambda e: e.memset(self.cpool[:, 3:4], float(VK_SHORT)), writes=["cpool"])
        P.op("vector", lambda e: e.memset(self.onesh_b[:], 0.0), writes=["onesh_b"])
        P.op("vector", lambda e: e.memset(self.onesh_b[0:64, 0:1], 1.0), writes=["onesh_b"])
        P.op("vector", lambda e: e.memset(self.onesh_b[64:128, 1:2], 1.0), writes=["onesh_b"])
        P.op("vector", lambda e: e.memset(self.onesh_bq[:], 0.0), writes=["onesh_bq"])
        P.op("vector", lambda e: e.memset(self.onesh_bq[0:64, 0:1], 0.1), writes=["onesh_bq"])
        P.op("vector", lambda e: e.memset(self.onesh_bq[64:128, 1:2], 0.1), writes=["onesh_bq"])

        wsm = self.sbuf("wsm", [16, 64], F32)
        wsm_b = self.sbuf("wsm_b", [16, 32], BF16)
        wsmT = self.wsmT = self.sbuf("wsmT", [16, 32], BF16)
        pm16 = self.putil[0:16, 0:16].bitcast(BF16)
        P.op("sync", lambda e: e.dma_start(wsm[:, 0:16], self.Wpre_in), writes=["wsm"], dma="g")
        P.op("sync", lambda e: e.dma_start(wsm[:, 16:32], self.Wpost_in), writes=["wsm"], dma="g")
        P.op("sync", lambda e: e.dma_start(wsm[0:16, 32:33], self.hs_in[:, None]), writes=["wsm"], dma="g")
        P.op("vector", lambda e: e.tensor_scalar(wsm[0:16, 48:64], wsm[0:16, 16:32],
                                                 wsm[0:16, 32:33], None, ALU.mult),
             reads=["wsm"], writes=["wsm48"])
        P.op("vector", lambda e: e.tensor_copy(wsm_b[:, 0:16], wsm[0:16, 0:16]),
             reads=["wsm"], writes=["wsm_b"])
        P.op("vector", lambda e: e.tensor_copy(wsm_b[:, 16:32], wsm[0:16, 48:64]),
             reads=["wsm48"], writes=["wsm_b"])
        P.op("tensor", lambda e: e.transpose(pm16[:, 0:16], wsm_b[:, 0:16], self.ident[0:16, 0:16]),
             reads=["wsm_b", "ident"], writes=["putil"])
        P.op("tensor", lambda e: e.transpose(pm16[:, 16:32], wsm_b[:, 16:32], self.ident[0:16, 0:16]),
             reads=["wsm_b", "ident"], writes=["putil"])
        P.op("vector", lambda e: e.tensor_copy(wsmT[:], pm16[:]), reads=["putil"], writes=["wsmT"])
        P.op("vector", lambda e: e.memset(self.wpre_t[:], 0.0), writes=["wpre_t"])
        P.op("vector", lambda e: e.memset(self.wpost_t[:], 0.0), writes=["wpost_t"])
        for io in range(8):
            P.op("gpsimd", lambda e, io=io: e.dma_start(
                    self.wpre_t[16 * io:16 * io + 16, 16 * io:16 * io + 16], wsmT[:, 0:16]),
                 reads=["wsmT"], writes=["wpre_t"], dma="g")
            P.op("gpsimd", lambda e, io=io: e.dma_start(
                    self.wpost_t[16 * io:16 * io + 16, 16 * io:16 * io + 16], wsmT[:, 16:32]),
                 reads=["wsmT"], writes=["wpost_t"], dma="g")

    def stage_quant(self, wfcqT_sb, wprojqT_sb, sub="all"):
        P, nc = self.P, self.nc
        self.qsub = sub
        qctx = ExitStack()
        qsbuf = lambda name, shape, dtype: qctx.enter_context(nc.sbuf_tensor(name, shape, dtype))
        qstg = [qsbuf(f"qstg{i}", [128, HID], F32) for i in range(2)]
        qsm = qsbuf("qsm", [128, 40], F32)
        qmask = qsbuf("qmask", [128, HID], F32)
        qbf = qsbuf("qbf", [128, HID], BF16)
        pq = self.putil[:, 64:72]
        pqt = qctx.enter_context(nc.psum_tensor("pqt", [128, 4096], BF16))

        def do_quant(w_in, n_rows, n_cols, tag, out_cb):
            nrc = n_rows // 128
            for c in range(nrc):
                s = qstg[c % 2]
                sk = f"qs{c % 2}"
                P.op("sync", lambda e, c=c, s=s: e.dma_start(
                        s[:, 0:n_cols], w_in[c * 128:(c + 1) * 128, :]), writes=[sk], dma="g")
                P.op("vector", lambda e, c=c, s=s: e.tensor_reduce(
                        qsm[:, c:c + 1], s[:, 0:n_cols], AX.X, ALU.add, apply_absolute_value=True),
                     reads=[sk], writes=["qsm"])
            P.op("vector", lambda e, nrc=nrc: e.tensor_reduce(
                    qsm[:, 36:37], qsm[:, 0:nrc], AX.X, ALU.add), reads=["qsm"], writes=["qsm36"])
            P.op("tensor", lambda e: e.matmul(pq[0:1, 0:1], qsm[:, 36:37], self.ones128_f[:],
                                              start=True, stop=True),
                 reads=["qsm36", "ones128_f"], writes=["putil"])
            P.op("scalar", lambda e, n=n_rows * n_cols: e.activation(
                    qsm[0:1, 37:38], pq[0:1, 0:1], AF.Identity, bias=self.cpool[0:1, 0:1], scale=1.0 / n),
                 reads=["putil", "cpool"], writes=["qsm37"])
            P.op("tensor", lambda e: e.matmul(pq[:, 1:2], self.onesrow_f[:], qsm[0:1, 37:38],
                                              start=True, stop=True),
                 reads=["qsm37", "onesrow_f"], writes=["putil"])
            if self.qsub == "absum":
                return
            P.op("vector", lambda e: e.tensor_copy(qsm[:, 38:39], pq[:, 1:2]),
                 reads=["putil"], writes=["gam0"])
            P.op("vector", lambda e: e.reciprocal(qsm[:, 39:40], qsm[:, 38:39]),
                 reads=["gam0"], writes=["gam"])
            for c in range(nrc):
                s = qstg[c % 2]
                sk = f"qs{c % 2}"
                P.op("sync", lambda e, c=c, s=s: e.dma_start(
                        s[:, 0:n_cols], w_in[c * 128:(c + 1) * 128, :]), writes=[sk], dma="g")
                P.op("vector", lambda e, s=s, n_cols=n_cols: e.tensor_scalar(
                        s[:, 0:n_cols], s[:, 0:n_cols], qsm[:, 39:40], None, ALU.mult),
                     reads=[sk, "gam"], writes=[sk])
                P.op("vector", lambda e, s=s, n_cols=n_cols: e.tensor_scalar(
                        qmask[:, 0:n_cols], s[:, 0:n_cols], 0.5, None, ALU.is_gt),
                     reads=[sk], writes=["qmask"])
                P.op("vector", lambda e, s=s, n_cols=n_cols: e.tensor_scalar(
                        s[:, 0:n_cols], s[:, 0:n_cols], -0.5, None, ALU.is_lt),
                     reads=[sk], writes=[sk])
                P.op("vector", lambda e, s=s, n_cols=n_cols: e.tensor_sub(
                        qbf[:, 0:n_cols], qmask[:, 0:n_cols], s[:, 0:n_cols]),
                     reads=["qmask", sk], writes=["qbf"])
                if self.qsub != "noq":
                    out_cb(c)

        def fc_cb(c):
            for dmc in range(8):
                pslot = dmc % 4
                P.op("tensor", lambda e, dmc=dmc, pslot=pslot: e.transpose(
                        pqt[:, pslot * 1024:pslot * 1024 + 128],
                        qbf[:, dmc * 128:(dmc + 1) * 128], self.ident[:]),
                     reads=["qbf", "ident"], writes=[f"pqt{pslot}"])
                P.op("vector", lambda e, c=c, dmc=dmc, pslot=pslot: e.tensor_copy(
                        wfcqT_sb[:, dmc * HID + c * 128: dmc * HID + (c + 1) * 128],
                        pqt[:, pslot * 1024:pslot * 1024 + 128]),
                     reads=[f"pqt{pslot}"], writes=["wfcqT_sb"])
        do_quant(self.Wfc_in, HID, D, "fc", fc_cb)
        for dmc in range(8):
            P.op("gpsimd", lambda e, dmc=dmc: e.dma_start(
                    self.wfcqT_d[dmc * 128:(dmc + 1) * 128, :],
                    wfcqT_sb[:, dmc * HID:(dmc + 1) * HID]),
                 reads=["wfcqT_sb"], writes=["wfcqT_d"], dma="g")

        def pj_cb(c):
            for hc in range(32):
                pslot = hc % 4
                P.op("tensor", lambda e, hc=hc, pslot=pslot: e.transpose(
                        pqt[:, pslot * 1024:pslot * 1024 + 128],
                        qbf[:, hc * 128:(hc + 1) * 128], self.ident[:]),
                     reads=["qbf", "ident"], writes=[f"pqt{pslot}"])
                P.op("vector", lambda e, c=c, hc=hc, pslot=pslot: e.tensor_copy(
                        wprojqT_sb[:, hc * D + c * 128: hc * D + (c + 1) * 128],
                        pqt[:, pslot * 1024:pslot * 1024 + 128]),
                     reads=[f"pqt{pslot}"], writes=["wprojqT_sb"])
        do_quant(self.Wproj_in, D, HID, "pj", pj_cb)
        for hc in range(32):
            P.op("gpsimd", lambda e, hc=hc: e.dma_start(
                    self.wprojqT_d[hc * 128:(hc + 1) * 128, :],
                    wprojqT_sb[:, hc * D:(hc + 1) * D]),
                 reads=["wprojqT_sb"], writes=["wprojqT_d"], dma="g")
        P.retire_all()
        P.flush(self.block)
        qctx.close()

    def ln_tile(self, x_ap, xf, fslice, stat, scol, keys, out_bf, oslice):
        P = self.P
        xk, stk, outk = keys
        P.op("sync", lambda e: e.dma_start(xf[:, fslice], x_ap), writes=[xk], dma="g")
        P.op("vector", lambda e: e.tensor_scalar(
                out_bf[:, oslice], xf[:, fslice], -1.0 / D, None, ALU.mult,
                op1=ALU.add, accum_out=stat[:, scol:scol + 1]),
             reads=[xk], writes=[stk, outk])
        P.op("vector", lambda e: e.tensor_scalar(
                xf[:, fslice], xf[:, fslice], stat[:, scol:scol + 1], None, ALU.add),
             reads=[xk, stk], writes=[xk])
        P.op("scalar", lambda e: e.activation(
                out_bf[:, oslice], xf[:, fslice], AF.Square, accum_out=stat[:, scol + 1:scol + 2]),
             reads=[xk], writes=[stk, outk])
        P.op("scalar", lambda e: e.activation(
                stat[:, scol + 1:scol + 2], stat[:, scol + 1:scol + 2], AF.Ln,
                scale=1.0 / D, bias=self.cpool[:, 1:2]),
             reads=[stk, "cpool"], writes=[stk])
        P.op("scalar", lambda e: e.activation(
                stat[:, scol + 1:scol + 2], stat[:, scol + 1:scol + 2], AF.Exp, scale=-0.5),
             reads=[stk], writes=[stk])
        P.op("vector", lambda e: e.tensor_scalar(
                out_bf[:, oslice], xf[:, fslice], stat[:, scol + 1:scol + 2], None, ALU.mult),
             reads=[xk, stk], writes=[outk])

    def stage_ffn(self, x_src_d, x_dst_d, wfcqT_sb, wprojqT_sb, tag, scale_weights=True,
                  n_tok=None):
        P, nc, cfg = self.P, self.nc, self.cfg
        T = n_tok if n_tok is not None else cfg.t_loc
        TB = min(256, T)
        NTB = TB // 128
        NBLK = T // TB
        with ExitStack() as fstk:
            en = fstk.enter_context
            xf = en(nc.sbuf_tensor(f"xf_{tag}", [128, D], F32))
            xr = en(nc.sbuf_tensor(f"xr_{tag}", [128, D], F32))
            stat = en(nc.sbuf_tensor(f"st_{tag}", [128, 8], F32))
            lnxb = en(nc.sbuf_tensor(f"lnx_{tag}", [128, D], BF16))
            lnxT = en(nc.sbuf_tensor(f"lnxT_{tag}", [128, 8 * TB], BF16))
            gh = en(nc.sbuf_tensor(f"gh_{tag}", [128, 32 * TB], BF16))
            gh2 = en(nc.sbuf_tensor(f"gh2_{tag}", [128, TB], BF16))
            tstat = en(nc.sbuf_tensor(f"tstat_{tag}", [1, 2 * TB], F32))
            bcf = en(nc.sbuf_tensor(f"bcf_{tag}", [128, 2 * TB], F32))
            BC = en(nc.sbuf_tensor(f"BC_{tag}", [128, 18], F32))
            bhl = en(nc.sbuf_tensor(f"bhl_{tag}", [128, 2], BF16))
            yTs = en(nc.sbuf_tensor(f"yTs_{tag}", [128, TB], F32))
            yU = en(nc.sbuf_tensor(f"yU_{tag}", [128, TB], F32))
            ytok = en(nc.sbuf_tensor(f"ytok_{tag}", [128, NTB * D], F32))
            pfA = en(nc.psum_tensor(f"pfA_{tag}", [128, 512], F32))
            pfB = en(nc.psum_tensor(f"pfB_{tag}", [128, 512], F32))
            pstA = en(nc.psum_tensor(f"pstA_{tag}", [1, 512], F32))
            pstB = en(nc.psum_tensor(f"pstB_{tag}", [1, 512], F32))
            pxT = en(nc.psum_tensor(f"pxT_{tag}", [128, 1024], BF16))
            ptr = en(nc.psum_tensor(f"ptr_{tag}", [128, 512], F32))
            pyt = en(nc.psum_tensor(f"py_{tag}", [128, 512], F32))
            pBC = en(nc.psum_tensor(f"pBC_{tag}", [128, 512], F32))

            for dmc in range(8):
                for hc in range(32):
                    P.op("vector", lambda e, hc=hc: e.tensor_copy(bhl[:, 0:1], self.lnb_sb[:, hc:hc + 1]),
                         reads=["lnb_sb"], writes=[f"{tag}bhl"])
                    P.op("vector", lambda e, hc=hc: e.tensor_sub(stat[:, 4:5], self.lnb_sb[:, hc:hc + 1],
                                                                 bhl[:, 0:1]),
                         reads=["lnb_sb", f"{tag}bhl"], writes=[f"{tag}blo"])
                    P.op("vector", lambda e: e.tensor_copy(bhl[:, 1:2], stat[:, 4:5]),
                         reads=[f"{tag}blo"], writes=[f"{tag}bhl"])
                    P.op("tensor", lambda e, hc=hc, dmc=dmc: e.matmul(
                            pBC[:, 0:2], wprojqT_sb[:, hc * D + dmc * 128: hc * D + (dmc + 1) * 128],
                            bhl[:], start=(hc == 0), stop=(hc == 31)),
                         reads=["wprojqT_sb", f"{tag}bhl"], writes=[f"{tag}pBC"])
                P.op("vector", lambda e, dmc=dmc: e.tensor_copy(BC[:, 16:18], pBC[:, 0:2]),
                     reads=[f"{tag}pBC"], writes=[f"{tag}BCs"])
                P.op("vector", lambda e, dmc=dmc: e.tensor_add(BC[:, 8 + dmc:9 + dmc],
                                                               BC[:, 16:17], BC[:, 17:18]),
                     reads=[f"{tag}BCs"], writes=[f"{tag}BC"])
            if scale_weights:
                for hc in range(32):
                    P.op("vector", lambda e, hc=hc: e.tensor_scalar(
                            wprojqT_sb[:, hc * D:(hc + 1) * D], wprojqT_sb[:, hc * D:(hc + 1) * D],
                            self.lnw_sb[:, hc:hc + 1], None, ALU.mult),
                         reads=["wprojqT_sb", "lnw_sb"], writes=["wprojqT_sb"])
            for dmc in range(8):
                for hc in range(32):
                    P.op("tensor", lambda e, hc=hc, dmc=dmc: e.matmul(
                            pBC[:, 0:1], wprojqT_sb[:, hc * D + dmc * 128: hc * D + (dmc + 1) * 128],
                            self.ones128_b[:], start=(hc == 0), stop=(hc == 31)),
                         reads=["wprojqT_sb", "ones128_b"], writes=[f"{tag}pBC"])
                P.op("vector", lambda e, dmc=dmc: e.tensor_copy(BC[:, dmc:dmc + 1], pBC[:, 0:1]),
                     reads=[f"{tag}pBC"], writes=[f"{tag}BC"])

            for blk in range(NBLK):
                t0 = blk * TB
                for tt in range(NTB):
                    tok = t0 + tt * 128
                    self.ln_tile(x_src_d[tok:tok + 128, :], xf, slice(0, D), stat, 0,
                                 (f"{tag}xf", f"{tag}stat", f"{tag}lnxb"), lnxb, slice(0, D))
                    for dmc in range(8):
                        P.op("tensor", lambda e, dmc=dmc: e.transpose(
                                pxT[:, 0:128],
                                lnxb[:, dmc * 128:dmc * 128 + 128], self.ident[:]),
                             reads=[f"{tag}lnxb", "ident"], writes=[f"{tag}pxT"])
                        P.op("vector", lambda e, tt=tt, dmc=dmc: e.tensor_copy(
                                lnxT[:, dmc * TB + tt * 128: dmc * TB + (tt + 1) * 128],
                                pxT[:, 0:128]),
                             reads=[f"{tag}pxT"], writes=[f"{tag}lnxT"])

                for s in range(32):
                    pcurt, pk = (pfA, f"{tag}pfA") if s % 2 == 0 else (pfB, f"{tag}pfB")
                    pcur = pcurt[:, 0:TB]
                    for dmc in range(8):
                        P.op("tensor", lambda e, s=s, dmc=dmc, pcur=pcur: e.matmul(
                                pcur, wfcqT_sb[:, dmc * HID + s * 128: dmc * HID + (s + 1) * 128],
                                lnxT[:, dmc * TB:(dmc + 1) * TB],
                                start=(dmc == 0), stop=(dmc == 7)),
                             reads=["wfcqT_sb", f"{tag}lnxT"], writes=[pk])
                    P.op("scalar", lambda e, s=s, pcur=pcur: e.activation(
                            gh[:, s * TB:(s + 1) * TB], pcur, AF.Gelu),
                         reads=[pk], writes=[f"{tag}gh{s}"])
                    P.op("scalar", lambda e, s=s: e.activation(
                            gh2[:], gh[:, s * TB:(s + 1) * TB], AF.Square),
                         reads=[f"{tag}gh{s}"], writes=[f"{tag}gh2"])
                    P.op("tensor", lambda e, s=s: e.matmul(
                            pstA[0:1, 0:TB], self.ones128_b[:], gh[:, s * TB:(s + 1) * TB],
                            start=(s == 0), stop=(s == 31)),
                         reads=[f"{tag}gh{s}", "ones128_b"], writes=[f"{tag}pstA"])
                    P.op("tensor", lambda e, s=s: e.matmul(
                            pstB[0:1, 0:TB], self.ones128_b[:], gh2[:],
                            start=(s == 0), stop=(s == 31)),
                         reads=[f"{tag}gh2", "ones128_b"], writes=[f"{tag}pstB"])

                P.op("scalar", lambda e: e.activation(tstat[0:1, 0:TB], pstA[0:1, 0:TB],
                                                      AF.Identity, scale=1.0 / HID),
                     reads=[f"{tag}pstA"], writes=[f"{tag}mu"])
                P.op("scalar", lambda e: e.activation(tstat[0:1, TB:2 * TB], pstB[0:1, 0:TB],
                                                      AF.Identity, scale=1.0 / HID),
                     reads=[f"{tag}pstB"], writes=[f"{tag}ex2"])
                P.op("vector", lambda e: e.tensor_mul(yU[0:1, 0:TB], tstat[0:1, 0:TB],
                                                      tstat[0:1, 0:TB]),
                     reads=[f"{tag}mu"], writes=[f"{tag}mu2"])
                P.op("vector", lambda e: e.tensor_sub(tstat[0:1, TB:2 * TB], tstat[0:1, TB:2 * TB],
                                                      yU[0:1, 0:TB]),
                     reads=[f"{tag}ex2", f"{tag}mu2"], writes=[f"{tag}var"])
                P.op("scalar", lambda e: e.activation(tstat[0:1, TB:2 * TB], tstat[0:1, TB:2 * TB],
                                                      AF.Ln, bias=self.cpool[0:1, 1:2]),
                     reads=[f"{tag}var", "cpool"], writes=[f"{tag}var"])
                P.op("scalar", lambda e: e.activation(tstat[0:1, TB:2 * TB], tstat[0:1, TB:2 * TB],
                                                      AF.Exp, scale=-0.5),
                     reads=[f"{tag}var"], writes=[f"{tag}var"])
                P.op("vector", lambda e: e.tensor_scalar(
                        tstat[0:1, 0:TB], tstat[0:1, 0:TB], -1.0, None, ALU.mult),
                     reads=[f"{tag}mu"], writes=[f"{tag}mu"])
                P.op("vector", lambda e: e.tensor_mul(tstat[0:1, 0:TB], tstat[0:1, 0:TB],
                                                      tstat[0:1, TB:2 * TB]),
                     reads=[f"{tag}mu", f"{tag}var"], writes=[f"{tag}mrs"])
                P.op("tensor", lambda e: e.matmul(ptr[:, 0:TB], self.onesrow_f[:],
                                                  tstat[0:1, TB:2 * TB], start=True, stop=True),
                     reads=[f"{tag}var", "onesrow_f"], writes=[f"{tag}ptr"])
                P.op("vector", lambda e: e.tensor_copy(bcf[:, 0:TB], ptr[:, 0:TB]),
                     reads=[f"{tag}ptr"], writes=[f"{tag}bcA"])
                P.op("tensor", lambda e: e.matmul(ptr[:, 0:TB], self.onesrow_f[:],
                                                  tstat[0:1, 0:TB], start=True, stop=True),
                     reads=[f"{tag}mrs", "onesrow_f"], writes=[f"{tag}ptr"])
                P.op("vector", lambda e: e.tensor_copy(bcf[:, TB:2 * TB], ptr[:, 0:TB]),
                     reads=[f"{tag}ptr"], writes=[f"{tag}bcB"])

                for dmc in range(8):
                    for s in range(32):
                        P.op("tensor", lambda e, s=s, dmc=dmc: e.matmul(
                                pyt[:, 0:TB], wprojqT_sb[:, s * D + dmc * 128: s * D + (dmc + 1) * 128],
                                gh[:, s * TB:(s + 1) * TB],
                                start=(s == 0), stop=(s == 31)),
                             reads=["wprojqT_sb", f"{tag}gh{s}"], writes=[f"{tag}py"])
                    P.op("vector", lambda e, dmc=dmc: e.scalar_tensor_tensor(
                            yTs[:], pyt[:, 0:TB], 1.0, bcf[:, 0:TB], ALU.mult, ALU.mult),
                         reads=[f"{tag}py", f"{tag}bcA"], writes=[f"{tag}yTs"])
                    P.op("vector", lambda e, dmc=dmc: e.tensor_scalar(
                            yU[:], bcf[:, TB:2 * TB], BC[:, dmc:dmc + 1], BC[:, 8 + dmc:9 + dmc],
                            ALU.mult, op1=ALU.add),
                         reads=[f"{tag}bcB", f"{tag}BC"], writes=[f"{tag}yU"])
                    P.op("vector", lambda e, dmc=dmc: e.tensor_add(yTs[:], yTs[:], yU[:]),
                         reads=[f"{tag}yTs", f"{tag}yU"], writes=[f"{tag}yTs"])
                    for tt in range(NTB):
                        P.op("tensor", lambda e, tt=tt: e.transpose(
                                ptr[:, 0:128],
                                yTs[:, tt * 128:(tt + 1) * 128], self.ident_f[:]),
                             reads=[f"{tag}yTs", "ident_f"], writes=[f"{tag}ptr"])
                        P.op("vector", lambda e, tt=tt, dmc=dmc: e.tensor_copy(
                                ytok[:, tt * D + dmc * 128: tt * D + dmc * 128 + 128],
                                ptr[:, 0:128]),
                             reads=[f"{tag}ptr"], writes=[f"{tag}ytok{tt}"])
                for tt in range(NTB):
                    tok = t0 + tt * 128
                    P.op("sync", lambda e, tok=tok: e.dma_start(xr[:], x_src_d[tok:tok + 128, :]),
                         writes=[f"{tag}xr"], dma="g")
                    P.op("vector", lambda e, tt=tt: e.scalar_tensor_tensor(
                            xr[:], ytok[:, tt * D:(tt + 1) * D], 0.5, xr[:],
                            ALU.mult, ALU.add),
                         reads=[f"{tag}ytok{tt}", f"{tag}xr"], writes=[f"{tag}xr"])
                    P.op("gpsimd", lambda e, tok=tok: e.dma_start(x_dst_d[tok:tok + 128, :], xr[:]),
                         reads=[f"{tag}xr"], writes=["x_dst" + tag], dma="g")
            P.retire_all()
            P.flush(self.block)


    def stage_proj(self):
        P, nc, cfg = self.P, self.nc, self.cfg
        T = cfg.t_full
        TB = min(512, cfg.t_loc)
        NBLK = T // TB
        NBLK_OWN = cfg.t_loc // TB
        with ExitStack() as stk:
            en = stk.enter_context
            wsb = {}
            for w in ("Wq", "Wk", "Wv", "Wg"):
                wsb[w] = en(nc.sbuf_tensor(f"wsb_{w}", [128, 8 * D], BF16))
            stg = en(nc.sbuf_tensor("pj_stg", [128, D], F32))
            bgsb = en(nc.sbuf_tensor("pj_bg", [128, 8], F32))
            xf = en(nc.sbuf_tensor("pj_xf", [128, D], F32))
            stat = en(nc.sbuf_tensor("pj_stat", [128, 8], F32))
            lnxb = en(nc.sbuf_tensor("pj_lnx", [128, D], BF16))
            lnxT = en(nc.sbuf_tensor("pj_lnxT", [128, 8 * TB], BF16))
            sq = en(nc.sbuf_tensor("pj_sq", [128, 512], BF16))
            qraw = en(nc.sbuf_tensor("pj_qraw", [128, 512], F32))
            nrm = en(nc.sbuf_tensor("pj_nrm", [2, 1024], F32))
            onesh_f = en(nc.sbuf_tensor("pj_onesh_f", [2, 128], F32))
            outb = en(nc.sbuf_tensor("pj_outb", [128, 512], BF16))
            mk = en(nc.sbuf_tensor("pj_mk", [64, 64], F32))
            mkb = en(nc.sbuf_tensor("pj_mkb", [64, 64], BF16))
            mkn = en(nc.sbuf_tensor("pj_mkn", [64, 1], F32))
            zpad = en(nc.sbuf_tensor("pj_zpad", [128, 1024], BF16))
            ppj = en(nc.psum_tensor("ppjA", [128, 512], F32))
            ppj2 = en(nc.psum_tensor("ppjB", [128, 512], F32))
            pxT = en(nc.psum_tensor("pj_pxT", [128, 1024], BF16))
            pnrm = en(nc.psum_tensor("pj_pnrm", [2, 512], F32))
            pnb = en(nc.psum_tensor("pj_pnb", [128, 512], F32))
            pmk = en(nc.psum_tensor("pj_pmk", [64, 1024], BF16))

            oshT = en(nc.sbuf_tensor("pj_oshT", [128, 2], F32))
            P.op("vector", lambda e: e.memset(oshT[:], 0.0), writes=["oshT"])
            P.op("vector", lambda e: e.memset(oshT[0:64, 0:1], 1.0), writes=["oshT"])
            P.op("vector", lambda e: e.memset(oshT[64:128, 1:2], 1.0), writes=["oshT"])
            P.op("tensor", lambda e: e.transpose(pnrm[0:2, 0:128], oshT[:], self.ident_f[:]),
                 reads=["oshT", "ident_f"], writes=["pj_pnrm"])
            P.op("vector", lambda e: e.tensor_copy(onesh_f[:], pnrm[0:2, 0:128]),
                 reads=["pj_pnrm"], writes=["onesh_f"])
            P.op("vector", lambda e: e.memset(zpad[:], 0.0), writes=["zpad"])
            P.op("vector", lambda e: e.memset(self.cpool[:, 0:1], float(ABS_EPS)), writes=["cpool"])
            self.load_vec_T(self.bg_in, D, bgsb, "pj_bg", psum_ap=pnb, alloc=lambda n_, s_, d_: en(nc.sbuf_tensor(n_, s_, d_)))
            for w in ("Wq", "Wk", "Wv", "Wg"):
                for dmc in range(8):
                    P.op("sync", lambda e, w=w, dmc=dmc: e.dma_start(
                            stg[:], self.W_in[w][dmc * 128:(dmc + 1) * 128, :]),
                         writes=["pj_stg"], dma="g")
                    P.op("vector", lambda e, w=w, dmc=dmc: e.tensor_copy(
                            wsb[w][:, dmc * D:(dmc + 1) * D], stg[:]),
                         reads=["pj_stg"], writes=[f"wsb_{w}"])

            P.op("sync", lambda e: e.dma_start(mk[:], self.memk_in.rearrange("g s d -> (g s) d")),
                 writes=["pj_mk"], dma="g")
            P.op("scalar", lambda e: e.activation(mkb[:], mk[:], AF.Square,
                                                  accum_out=mkn[:]),
                 reads=["pj_mk"], writes=["pj_mkn", "pj_mkb"])
            P.op("scalar", lambda e: e.activation(mkn[:], mkn[:], AF.Ln, bias=self.cpool[0:64, 0:1],
                                                  scale=1.0),
                 reads=["pj_mkn", "cpool"], writes=["pj_mkn"])
            P.op("scalar", lambda e: e.activation(mkn[:], mkn[:], AF.Exp, scale=-0.5),
                 reads=["pj_mkn"], writes=["pj_mkn"])
            P.op("vector", lambda e: e.tensor_scalar(mkb[:], mk[:], mkn[:], None, ALU.mult),
                 reads=["pj_mk", "pj_mkn"], writes=["pj_mkb"])
            P.op("tensor", lambda e: e.transpose(pmk[:, 0:64], mkb[:], self.ident[0:64, 0:64]),
                 reads=["pj_mkb", "ident"], writes=["pj_pmk"])
            P.op("vector", lambda e: e.tensor_copy(mkb[:], pmk[:, 0:64]), reads=["pj_pmk"], writes=["pj_mkb"])
            P.op("gpsimd", lambda e: e.dma_start(
                    self.kT_d.rearrange("(g d) c -> d g c", d=DH)[:, :, 0:NMEM],
                    mkb[:].rearrange("d (g s) -> d g s", s=NMEM)),
                 reads=["pj_mkb"], writes=["kT_d"], dma="g")
            P.op("gpsimd", lambda e: e.dma_start(
                    self.v_d[0:NMEM, :].rearrange("s (g d) -> s g d", d=DH),
                    self.memv_in.rearrange("g s d -> s g d")),
                 writes=["v_d"], dma="g")
            for c in range(8):
                P.op("gpsimd", lambda e, c=c: e.dma_start(
                        self.kT_d[c * 128:(c + 1) * 128, NMEM:128], zpad[:, 0:128 - NMEM]),
                     reads=["zpad"], writes=["kT_d"], dma="g")
            P.op("gpsimd", lambda e: e.dma_start(self.v_d[NMEM:128, :], zpad[0:128 - NMEM, :]),
                 reads=["zpad"], writes=["v_d"], dma="g")

            def norm_head_slice(hds, src_key, out_dram, qscale):
                P.op("scalar", lambda e: e.activation(sq[:, 0:TB], ppj[:, 0:TB], AF.Square),
                     reads=["ppjA"], writes=["pj_sq"])
                P.op("scalar", lambda e: e.activation(qraw[:, 0:TB], ppj[:, 0:TB], AF.Identity),
                     reads=["ppjA"], writes=["pj_qraw"])
                P.op("tensor", lambda e: e.matmul(pnrm[0:2, 0:TB], self.onesh_b[:], sq[:, 0:TB],
                                                  start=True, stop=True),
                     reads=["onesh_b", "pj_sq"], writes=["pj_pnrm"])
                P.op("vector", lambda e: e.tensor_scalar(nrm[:, 0:TB], pnrm[0:2, 0:TB], 1e-24, None,
                                                         ALU.max),
                     reads=["pj_pnrm"], writes=["pj_nrm"])
                P.op("scalar", lambda e: e.activation(nrm[:, 0:TB], nrm[:, 0:TB], AF.Ln),
                     reads=["pj_nrm"], writes=["pj_nrm"])
                if qscale != 1.0:
                    P.op("scalar", lambda e: e.activation(
                            nrm[:, 0:TB], nrm[:, 0:TB], AF.Exp, scale=-0.5,
                            bias=self.cpool[0:2, 2:3]),
                         reads=["pj_nrm", "cpool"], writes=["pj_nrm"])
                else:
                    P.op("scalar", lambda e: e.activation(
                            nrm[:, 0:TB], nrm[:, 0:TB], AF.Exp, scale=-0.5),
                         reads=["pj_nrm"], writes=["pj_nrm"])
                P.op("tensor", lambda e: e.matmul(pnb[:, 0:TB], onesh_f[:], nrm[:, 0:TB],
                                                  start=True, stop=True),
                     reads=["onesh_f", "pj_nrm"], writes=["pj_pnb"])
                P.op("vector", lambda e: e.tensor_mul(outb[:, 0:TB], qraw[:, 0:TB], pnb[:, 0:TB]),
                     reads=["pj_qraw", "pj_pnb"], writes=["pj_outb"])
                P.op("gpsimd", lambda e: e.dma_start(out_dram, outb[:, 0:TB]),
                     reads=["pj_outb"], writes=[src_key + "_d"], dma="g")

            for blk in range(NBLK):
                t0 = blk * TB
                for tt in range(TB // 128):
                    tok = t0 + tt * 128
                    self.ln_tile(self.x1_d[tok:tok + 128, :], xf, slice(0, D), stat, 0,
                                 ("pj_xf", "pj_stat", "pj_lnxb"), lnxb, slice(0, D))
                    for dmc in range(8):
                        P.op("tensor", lambda e, dmc=dmc: e.transpose(
                                pxT[:, 0:128],
                                lnxb[:, dmc * 128:dmc * 128 + 128], self.ident[:]),
                             reads=["pj_lnxb", "ident"], writes=["pj_pxT"])
                        P.op("vector", lambda e, tt=tt, dmc=dmc: e.tensor_copy(
                                lnxT[:, dmc * TB + tt * 128: dmc * TB + (tt + 1) * 128],
                                pxT[:, 0:128]),
                             reads=["pj_pxT"], writes=["pj_lnxT"])

                for w, out_d, qs in (("Wq", self.qnT_d, QK_SCALE), ("Wk", self.knT_own_d, 1.0)):
                    if w == "Wq" and blk >= NBLK_OWN:
                        continue
                    for hds in range(8):
                        for dmc in range(8):
                            P.op("tensor", lambda e, w=w, hds=hds, dmc=dmc: e.matmul(
                                    ppj[:, 0:TB], wsb[w][:, dmc * D + hds * 128: dmc * D + (hds + 1) * 128],
                                    lnxT[:, dmc * TB:(dmc + 1) * TB],
                                    start=(dmc == 0), stop=(dmc == 7)),
                                 reads=[f"wsb_{w}", "pj_lnxT"], writes=["ppjA"])
                        norm_head_slice(hds, w, out_d[hds * 128:(hds + 1) * 128, t0:t0 + TB], qs)
                for tt in range(TB // 128):
                    tok = t0 + tt * 128
                    for vh in range(2):
                        for dmc in range(8):
                            P.op("tensor", lambda e, tt=tt, dmc=dmc, vh=vh: e.matmul(
                                    ppj2[:], lnxT[:, dmc * TB + tt * 128: dmc * TB + (tt + 1) * 128],
                                    wsb["Wv"][:, dmc * D + vh * 512: dmc * D + vh * 512 + 512],
                                    start=(dmc == 0), stop=(dmc == 7)),
                                 reads=["wsb_Wv", "pj_lnxT"], writes=["ppjB"])
                        P.op("vector", lambda e, vh=vh: e.tensor_copy(outb[:, 0:512], ppj2[:]),
                             reads=["ppjB"], writes=["pj_outb"])
                        P.op("gpsimd", lambda e, tok=tok, vh=vh: e.dma_start(
                                self.v_own_d[tok:tok + 128, vh * 512:vh * 512 + 512], outb[:, 0:512]),
                             reads=["pj_outb"], writes=["v_own_d"], dma="g")
                for hds in range(8):
                    if blk >= NBLK_OWN:
                        break
                    for dmc in range(8):
                        P.op("tensor", lambda e, hds=hds, dmc=dmc: e.matmul(
                                ppj[:, 0:TB], wsb["Wg"][:, dmc * D + hds * 128: dmc * D + (hds + 1) * 128],
                                lnxT[:, dmc * TB:(dmc + 1) * TB],
                                start=(dmc == 0), stop=(dmc == 7)),
                             reads=["wsb_Wg", "pj_lnxT"], writes=["ppjA"])
                    P.op("scalar", lambda e, hds=hds: e.activation(
                            outb[:, 0:TB], ppj[:, 0:TB], AF.Sigmoid, bias=bgsb[:, hds:hds + 1]),
                         reads=["ppjA", "pj_bg"], writes=["pj_outb"])
                    P.op("gpsimd", lambda e, hds=hds, t0=t0: e.dma_start(
                            self.sgT_d[hds * 128:(hds + 1) * 128, t0:t0 + TB], outb[:, 0:TB]),
                         reads=["pj_outb"], writes=["sgT_d"], dma="g")
            P.retire_all()
            P.flush(self.block)

    def stage_kv_assemble(self):
        P, nc, cfg = self.P, self.nc, self.cfg
        for m in range(cfg.seq // 128):
            lc = cfg.pos_local_chunk(m)
            P.op("gpsimd", lambda e, m=m, lc=lc: e.dma_start(
                    self.kT_d[:, 128 + m * 128:128 + (m + 1) * 128],
                    self.knT_own_d[:, lc * 128:(lc + 1) * 128]),
                 reads=["Wk_d"], writes=["kT_d"], dma="g")
            P.op("gpsimd", lambda e, m=m, lc=lc: e.dma_start(
                    self.v_d[128 + m * 128:128 + (m + 1) * 128, :],
                    self.v_own_d[lc * 128:(lc + 1) * 128, :]),
                 reads=["v_own_d"], writes=["v_d"], dma="g")

    def stage_attn(self, asub="full"):
        assert asub == "full"
        P, nc, cfg = self.P, self.nc, self.cfg
        T = cfg.t_loc
        JCOL = cfg.jcols
        NG = T // 128
        NW = 2 * NG
        sched = cfg.sched()
        with ExitStack() as stk:
            en = stk.enter_context
            db = en(nc.sbuf_tensor("db_all", [128, 16 * JCOL], BF16))
            dbv = db[:].rearrange("p (b j) -> p b j", b=16)
            sc0 = en(nc.sbuf_tensor("sc0", [128, JCOL], BF16))
            esb = en(nc.sbuf_tensor("esb", [128, JCOL], BF16))
            z_sb = en(nc.sbuf_tensor("z0", [128, 1024], BF16))
            qn_sb = [en(nc.sbuf_tensor(f"qn{i}", [64, 1024], BF16)) for i in range(2)]
            kc_sb = [en(nc.sbuf_tensor(f"kc{i}", [64, 2048], BF16)) for i in range(2)]
            vc_sb = [en(nc.sbuf_tensor(f"vc{i}", [128, 1024], BF16)) for i in range(2)]
            mr_sb = [en(nc.sbuf_tensor(f"mr{i}", [128, 1024], BF16)) for i in range(2)]
            pt2_sb = en(nc.sbuf_tensor("pt20", [128, 1024], BF16))
            m8t = en(nc.sbuf_tensor("m8t", [128, 8], BF16))
            vks = en(nc.sbuf_tensor("vks", [128, 64], F32))
            wscl = [en(nc.sbuf_tensor(f"wscl{b}", [128, 128], BF16)) for b in range(16)]
            wpost2 = en(nc.sbuf_tensor("wpost2", [128, 128], BF16))
            out2sb = en(nc.sbuf_tensor("out2sb", [128, 512], F32))
            pev = en(nc.sbuf_tensor("pev", [128, 512], F32))
            sg_sb = en(nc.sbuf_tensor("sg_sb", [128, 512], BF16))
            wo_sb = en(nc.sbuf_tensor("wo_sb", [128, 8 * D], BF16))
            yTs = en(nc.sbuf_tensor("at_yTs", [128, 64], F32))
            xr = en(nc.sbuf_tensor("at_xr", [128, D], F32))
            shortsel = en(nc.sbuf_tensor("shortsel", [128, 16], mybir.dt.uint32))
            ivec_sb = en(nc.sbuf_tensor("ivec_sb", [128, NG], F32))
            negbase_sb = en(nc.sbuf_tensor("negbase_sb", [128, cfg.n_jc], F32))
            colio_sb = en(nc.sbuf_tensor("colio_sb", [128, 128], F32))
            ivs = en(nc.sbuf_tensor("ivs", [128, 2], F32))
            msk = [en(nc.sbuf_tensor(f"msk{i}", [128, 128], BF16)) for i in range(2)]
            pbig = en(nc.psum_tensor("pbig", [128, 512], F32))
            pgat = [en(nc.psum_tensor(f"pgat{i}", [128, 1024], BF16)) for i in range(2)]
            ppre = en(nc.psum_tensor("ppre", [128, 512], F32))
            ppost = en(nc.psum_tensor("ppost", [128, 512], F32))
            pout2 = en(nc.psum_tensor("pout2", [128, 512], F32))
            pwo = en(nc.psum_tensor("pwo", [128, 512], F32))
            ptw = en(nc.psum_tensor("ptw", [128, 512], F32))

            P.op("sync", lambda e: e.dma_start(shortsel[:], self.shortsel_in),
                 writes=["shortsel"], dma="g")
            P.op("sync", lambda e: e.dma_start(ivec_sb[:], self.ivec_in),
                 writes=["ivec_sb"], dma="g")
            P.op("sync", lambda e: e.dma_start(negbase_sb[:], self.negbase_in),
                 writes=["negbase_sb"], dma="g")
            P.op("sync", lambda e: e.dma_start(colio_sb[:], self.colio_in),
                 writes=["colio_sb"], dma="g")
            P.op("vector", lambda e: e.tensor_copy(
                    wpost2[:].rearrange("p (h i) -> p h i", i=8),
                    self.wpost_t[:].rearrange("p (i h) -> p h i", h=16)),
                 reads=["wpost_t"], writes=["wpost2"])
            for hdc in range(8):
                P.op("sync", lambda e, hdc=hdc: e.dma_start(
                        xr[:], self.W_in["Wo"][hdc * 128:(hdc + 1) * 128, :]),
                     writes=["at_xr"], dma="g")
                P.op("vector", lambda e, hdc=hdc: e.tensor_copy(
                        wo_sb[:, hdc * D:(hdc + 1) * D], xr[:]),
                     reads=["at_xr"], writes=["wo_sb"])

            kT_v = self.kT_d.rearrange("(g p) j -> p g j", p=64)
            qnT_v = self.qnT_d.rearrange("(g p) t -> p g t", p=64)
            sgT_v = self.sgT_d.rearrange("(c q) t -> q c t", q=128)

            def wave_info(w):
                grp, half = divmod(w, 2)
                ntc, mpos = sched[grp]
                return grp, half, list(range(ntc + 1)), mpos

            def head_load(w):
                grp, half, jcs, _ = wave_info(w)
                tok0 = grp * 128 + half * 64
                sl = w % 2
                P.op("sync", lambda e, sl=sl, tok0=tok0: e.dma_start(
                        qn_sb[sl][:].rearrange("p (g t) -> p g t", g=16),
                        qnT_v[:, :, tok0:tok0 + 64]),
                     reads=["Wq_d"], writes=[f"qn{sl}"], dma="g")
                P.op("sync", lambda e: e.dma_start(
                        kc_sb[0][:].rearrange("p (g j) -> p g j", g=16),
                        kT_v[:, :, 0:128]),
                     reads=["kT_d"], writes=["kc0"], dma="g")

            nmsk = [0]

            def a_ops(w, jc):
                grp, half, jcs, mpos = wave_info(w)
                sl = w % 2
                qn, qk = qn_sb[sl], f"qn{sl}"
                kb, kk = kc_sb[jc % 2], f"kc{jc % 2}"
                if jc + 1 <= jcs[-1]:
                    nsl = (jc + 1) % 2
                    P.op("sync", lambda e, jc=jc, nsl=nsl: e.dma_start(
                            kc_sb[nsl][:].rearrange("p (g j) -> p g j", g=16),
                            kT_v[:, :, (jc + 1) * 128:(jc + 2) * 128]),
                         reads=["kT_d"], writes=[f"kc{nsl}"], dma="g")
                for f in range(2):
                    for gl in range(8):
                        g = f * 8 + gl
                        P.op("tensor", lambda e, g=g, gl=gl, kb=kb, qn=qn: e.matmul(
                                pbig[:, gl * 64:(gl + 1) * 64],
                                kb[0:64, g * 128:(g + 1) * 128],
                                qn[0:64, g * 64:(g + 1) * 64],
                                start=True, stop=True),
                             reads=[kk, qk], writes=["pbig"])
                    P.op("scalar", lambda e, f=f: e.activation(
                            z_sb[:].rearrange("p (i g) -> p g i", g=16)[:, f * 8:(f + 1) * 8, :],
                            pbig[:].rearrange("p (g i) -> p g i", i=64),
                            AF.Identity),
                         reads=["pbig"], writes=["z0"])
                psl = jc % 2
                for b in range(8):
                    P.op("tensor", lambda e, b=b, psl=psl: e.transpose(
                            pgat[psl][:, b * 128:(b + 1) * 128],
                            z_sb[:, b * 128:(b + 1) * 128], self.ident[:]),
                         reads=["z0", "ident"], writes=[f"pgat{psl}"])
                P.op("scalar", lambda e, psl=psl: e.activation(
                        mr_sb[psl][:], pgat[psl][:], AF.Identity),
                     reads=[f"pgat{psl}"], writes=[f"mr{psl}"])
                for bq in range(2):
                    for bl in range(4):
                        b = bq * 4 + bl
                        P.op("tensor", lambda e, b=b, bl=bl, psl=psl: e.matmul(
                                ppre[:, bl * 128:(bl + 1) * 128], self.wpre_t[:],
                                mr_sb[psl][:, b * 128:(b + 1) * 128],
                                start=True, stop=True),
                             reads=["wpre_t", f"mr{psl}"], writes=["ppre"])
                    bg0 = half * 8 + bq * 4
                    P.op("scalar", lambda e, bg0=bg0, jc=jc: e.activation(
                            dbv[:, bg0:bg0 + 4, jc * 128:(jc + 1) * 128],
                            ppre[:].rearrange("p (b j) -> p b j", b=4),
                            AF.Identity),
                         reads=["ppre"], writes=[f"dbq{bg0 // 4}"])
                    if jc == 0:
                        for bl in range(4):
                            bg = bg0 + bl
                            P.op("gpsimd", lambda e, bg=bg: e.tensor_tensor(
                                    dbv[:, bg:bg + 1, 0:128].rearrange("p b j -> p (b j)"),
                                    dbv[:, bg:bg + 1, 0:128].rearrange("p b j -> p (b j)"),
                                    self.dmask[:], ALU.add),
                                 reads=[f"dbq{bg // 4}", "dmask"], writes=[f"dbq{bg // 4}"])
                    if jc in mpos:
                        for bl in range(4):
                            bg = bg0 + bl
                            off = float(8 * bg)
                            ms = nmsk[0] % 2
                            nmsk[0] += 1
                            P.op("vector", lambda e, grp=grp, off=off, jc=jc: e.tensor_scalar(
                                    ivs[:, 0:1], ivec_sb[:, grp:grp + 1],
                                    negbase_sb[:, jc:jc + 1], off, ALU.add, op1=ALU.add),
                                 reads=["ivec_sb", "negbase_sb"], writes=["ivs"])
                            P.op("vector", lambda e, ms=ms: e.tensor_scalar(
                                    msk[ms][:], colio_sb[:], ivs[:, 0:1], NEG_BIG,
                                    ALU.is_gt, op1=ALU.mult),
                                 reads=["colio_sb", "ivs"], writes=[f"msk{ms}"])
                            P.op("gpsimd", lambda e, bg=bg, jc=jc, ms=ms: e.tensor_tensor(
                                    dbv[:, bg:bg + 1, jc * 128:(jc + 1) * 128].rearrange("p b j -> p (b j)"),
                                    dbv[:, bg:bg + 1, jc * 128:(jc + 1) * 128].rearrange("p b j -> p (b j)"),
                                    msk[ms][:], ALU.add),
                                 reads=[f"msk{ms}", f"dbq{bg // 4}"], writes=[f"dbq{bg // 4}"])

            def tail_ops(w, b, valid):
                grp, half = divmod(w, 2)
                bg = half * 8 + b
                P.op("scalar", lambda e, bg=bg, valid=valid: e.activation(
                        esb[:, 0:valid], dbv[:, bg, 0:valid], AF.Exp,
                        bias=vks[:, 16 + bg:17 + bg]),
                     reads=[f"dbq{bg // 4}", f"vkn{bg}"], writes=["esb"])
                P.op("vector", lambda e, bg=bg, valid=valid: e.scalar_tensor_tensor(
                        dbv[:, bg, 0:valid], dbv[:, bg, 0:valid], vks[:, bg:bg + 1],
                        esb[:, 0:valid], ALU.is_ge, ALU.mult,
                        accum_out=vks[:, 32 + bg:33 + bg]),
                     reads=[f"dbq{bg // 4}", f"vkr{bg}", "esb"],
                     writes=[f"dbq{bg // 4}", f"vkd{bg}"])
                P.op("vector", lambda e, bg=bg: e.reciprocal(vks[:, 48 + bg:49 + bg],
                                                             vks[:, 32 + bg:33 + bg]),
                     reads=[f"vkd{bg}"], writes=[f"vkc{bg}"])
                P.op("vector", lambda e, bg=bg: e.tensor_scalar(
                        wscl[bg][:], wpost2[:], vks[:, 48 + bg:49 + bg], None, ALU.mult),
                     reads=["wpost2", f"vkc{bg}"], writes=[f"wscl{bg}"])

            def tk(w):
                grp, half, jcs, _ = wave_info(w)
                valid = len(jcs) * 128
                pend = None
                for b in range(8):
                    bg = half * 8 + b
                    dsl = dbv[:, bg, 0:valid]
                    for r in range(8):
                        P.op("vector", lambda e, dsl=dsl, r=r, valid=valid: e.max(
                                m8t[:], dsl if r == 0 else sc0[:, 0:valid]),
                             reads=[f"dbq{bg // 4}" if r == 0 else "sc0"], writes=["m8t"])
                        if pend is not None and r == 1:
                            tail_ops(w, pend, valid)
                            pend = None
                        if r < 7:
                            P.op("vector", lambda e, dsl=dsl, r=r, valid=valid: e.match_replace(
                                    sc0[:, 0:valid], m8t[:],
                                    dsl if r == 0 else sc0[:, 0:valid], NEG_BIG),
                                 reads=["m8t", f"dbq{bg // 4}" if r == 0 else "sc0"],
                                 writes=["sc0"])
                    P.op("vector", lambda e, bg=bg: e.tensor_copy(vks[:, bg:bg + 1], m8t[:, 7:8]),
                         reads=["m8t"], writes=[f"vkr{bg}"])
                    if grp == 0:
                        P.op("vector", lambda e, bg=bg: e.copy_predicated(
                                vks[:, bg:bg + 1], shortsel[:, bg:bg + 1],
                                self.cpool[:, 3:4]),
                             reads=["shortsel", "cpool", f"vkr{bg}"], writes=[f"vkr{bg}"])
                    P.op("vector", lambda e, bg=bg: e.tensor_scalar(
                            vks[:, 16 + bg:17 + bg], vks[:, bg:bg + 1], -1.0, None, ALU.mult),
                         reads=[f"vkr{bg}"], writes=[f"vkn{bg}"])
                    pend = b
                tail_ops(w, pend, valid)

            def b_av(w, jc):
                grp, half, jcs, _ = wave_info(w)
                vb, vkk = vc_sb[jc % 2], f"vc{jc % 2}"
                if jc == 0:
                    P.op("scalar", lambda e: e.dma_start(
                            vc_sb[0][:], self.v_d[0:128, :]),
                         reads=["v_d"], writes=["vc0"], dma="s")
                if jc + 1 <= jcs[-1]:
                    nsl = (jc + 1) % 2
                    P.op("scalar", lambda e, jc=jc, nsl=nsl: e.dma_start(
                            vc_sb[nsl][:], self.v_d[(jc + 1) * 128:(jc + 2) * 128, :]),
                         reads=["v_d"], writes=[f"vc{nsl}"], dma="s")
                for bq in range(2):
                    for bl in range(4):
                        b = bq * 4 + bl
                        bg = half * 8 + b
                        P.op("tensor", lambda e, bg=bg, bl=bl, jc=jc: e.matmul(
                                ppost[:, bl * 128:(bl + 1) * 128],
                                dbv[:, bg, jc * 128:(jc + 1) * 128], wscl[bg][:],
                                start=True, stop=True),
                             reads=[f"dbq{bg // 4}", f"wscl{bg}"], writes=["ppost"])
                    for bl in range(4):
                        b = bq * 4 + bl
                        P.op("scalar", lambda e, b=b, bl=bl: e.activation(
                                pt2_sb[:].rearrange("p (h bi) -> p h bi", bi=64)[:, :, b * 8:(b + 1) * 8],
                                ppost[:, bl * 128:(bl + 1) * 128].rearrange(
                                    "p (h i) -> p h i", i=8),
                                AF.Identity),
                             reads=["ppost"], writes=["pt20"])
                for h in range(16):
                    ph = (h % 2) * 64
                    P.op("tensor", lambda e, h=h, ph=ph, vb=vb: e.matmul(
                            pout2[ph:ph + 64, (h // 2) * 64:(h // 2) * 64 + 64],
                            vb[:, h * 64:(h + 1) * 64],
                            pt2_sb[:, h * 64:(h + 1) * 64],
                            start=True, stop=True,
                            tile_position=(0, ph)),
                         reads=[vkk, "pt20"], writes=["pout2"])
                P.op("scalar", lambda e: e.activation(pev[:], pout2[:], AF.Identity),
                     reads=["pout2"], writes=["pev"])
                if jc == 0:
                    P.op("gpsimd", lambda e: e.tensor_copy(out2sb[:], pev[:]),
                         reads=["pev"], writes=["out2sb"])
                else:
                    P.op("gpsimd", lambda e: e.tensor_add(out2sb[:], out2sb[:], pev[:]),
                         reads=["pev", "out2sb"], writes=["out2sb"])

            def b_out(w):
                grp, half, jcs, _ = wave_info(w)
                tok0 = grp * 128 + half * 64
                P.op("sync", lambda e, tok0=tok0: e.dma_start(
                        sg_sb[:].rearrange("q (c i) -> q c i", c=8),
                        sgT_v[:, :, tok0:tok0 + 64]),
                     reads=["sgT_d"], writes=["sg_sb"], dma="g")
                P.op("gpsimd", lambda e: e.tensor_mul(sg_sb[:], out2sb[:], sg_sb[:]),
                     reads=["out2sb", "sg_sb"], writes=["sg_sb"])
                P.op("sync", lambda e, tok0=tok0: e.dma_start(
                        xr[0:64, :], self.x1_d[tok0:tok0 + 64, :]),
                     reads=["x_dstf1"], writes=["at_xr"], dma="g")
                for g4 in range(2):
                    for dq in range(4):
                        dms = g4 * 4 + dq
                        for hdc in range(8):
                            P.op("tensor", lambda e, dms=dms, hdc=hdc: e.matmul(
                                    pwo[:, dms * 64:(dms + 1) * 64],
                                    wo_sb[:, hdc * D + dms * 128: hdc * D + (dms + 1) * 128],
                                    sg_sb[:, hdc * 64:(hdc + 1) * 64],
                                    start=(hdc == 0), stop=(hdc == 7)),
                                 reads=["wo_sb", "sg_sb"], writes=["pwo"])
                        P.op("scalar", lambda e, dms=dms: e.activation(
                                yTs[:], pwo[:, dms * 64:(dms + 1) * 64], AF.Identity),
                             reads=["pwo"], writes=["at_yTs"])
                        P.op("tensor", lambda e, dq=dq: e.transpose(
                                ptw[0:64, dq * 128:(dq + 1) * 128], yTs[:], self.ident_f[:]),
                             reads=["at_yTs", "ident_f"], writes=["ptw"])
                    P.op("vector", lambda e, g4=g4: e.tensor_add(
                            xr[0:64, g4 * 512:(g4 + 1) * 512],
                            xr[0:64, g4 * 512:(g4 + 1) * 512], ptw[0:64, 0:512]),
                         reads=["ptw", "at_xr"], writes=["at_xr"])
                P.op("gpsimd", lambda e, tok0=tok0: e.dma_start(
                        self.x2_d[tok0:tok0 + 64, :], xr[0:64, :]),
                     reads=["at_xr"], writes=["x2_d"], dma="g")

            head_load(0)
            for jc in wave_info(0)[2]:
                a_ops(0, jc)
            head_load(1)
            for jc in wave_info(1)[2]:
                a_ops(1, jc)
            for w in range(NW):
                tk(w)
                if w - 1 >= 0:
                    b_out(w - 1)
                jB = wave_info(w)[2]
                jA = wave_info(w + 2)[2] if w + 2 < NW else []
                if jA:
                    head_load(w + 2)
                for jc in range(max(len(jA), len(jB))):
                    if jc < len(jB):
                        b_av(w, jc)
                    if jc < len(jA):
                        a_ops(w + 2, jc)
            b_out(NW - 1)
            P.retire_all()
            P.flush(self.block)

    def stage_attn_v1(self, asub="full"):
        P, nc, cfg = self.P, self.nc, self.cfg
        self.asub = asub
        T = cfg.t_loc
        JCOL = cfg.jcols
        NG = T // 128
        with ExitStack() as stk:
            en = stk.enter_context
            db = [en(nc.sbuf_tensor(f"db{b}", [128, JCOL], BF16)) for b in range(16)]
            esb = en(nc.sbuf_tensor("esb", [128, JCOL], BF16))
            sc0 = en(nc.sbuf_tensor("sc0", [128, JCOL], BF16))
            sc1 = esb
            qn_sb = en(nc.sbuf_tensor("qn_sb", [64, 16 * 128], BF16))
            kc_sb = [en(nc.sbuf_tensor(f"kc{i}", [64, 16 * 128], BF16)) for i in range(2)]
            vc_sb = [en(nc.sbuf_tensor("vc0", [128, 1024], BF16))] * 2
            z_sb = en(nc.sbuf_tensor("z_sb", [128, 2048], BF16))
            mr_sb = [en(nc.sbuf_tensor(f"mr{i}", [128, 128], BF16)) for i in range(2)]
            pt2_sb = [en(nc.sbuf_tensor("pt20", [128, 2048], BF16))] * 2
            m8t = en(nc.sbuf_tensor("m8t", [128, 8], BF16))
            vks = en(nc.sbuf_tensor("vks", [128, 64], F32))
            wscl = [en(nc.sbuf_tensor(f"wscl{b}", [128, 128], BF16)) for b in range(16)]
            wpost2 = en(nc.sbuf_tensor("wpost2", [128, 128], BF16))
            out2sb = en(nc.sbuf_tensor("out2sb", [128, 1024], F32))
            sg_sb = en(nc.sbuf_tensor("sg_sb", [128, 1024], BF16))
            gt_sb = sg_sb
            wo_sb = en(nc.sbuf_tensor("wo_sb", [128, 8 * D], BF16))
            yTs = en(nc.sbuf_tensor("at_yTs", [128, 128], F32))
            ytok = out2sb
            xr = en(nc.sbuf_tensor("at_xr", [128, D], F32))
            shortsel = en(nc.sbuf_tensor("shortsel", [128, 16], mybir.dt.uint32))
            ivec_sb = en(nc.sbuf_tensor("ivec_sb", [128, NG], F32))
            negbase_sb = en(nc.sbuf_tensor("negbase_sb", [128, cfg.n_jc], F32))
            colio_sb = en(nc.sbuf_tensor("colio_sb", [128, 128], F32))
            ivs = en(nc.sbuf_tensor("ivs", [128, 2], F32))
            msk = en(nc.sbuf_tensor("msk", [128, 128], BF16))
            pbig = en(nc.psum_tensor("pbig", [128, 512], F32))
            pgat = en(nc.psum_tensor("pgat", [128, 1024], BF16))
            ppre = en(nc.psum_tensor("ppre", [128, 512], F32))
            ppost = en(nc.psum_tensor("ppost", [128, 512], F32))
            pout2 = en(nc.psum_tensor("pout2", [128, 1024], F32))
            pwo = en(nc.psum_tensor("pwo", [128, 512], F32))
            ptw = en(nc.psum_tensor("ptw", [128, 512], F32))

            P.op("sync", lambda e: e.dma_start(shortsel[:], self.shortsel_in),
                 writes=["shortsel"], dma="g")
            P.op("sync", lambda e: e.dma_start(ivec_sb[:], self.ivec_in),
                 writes=["ivec_sb"], dma="g")
            P.op("sync", lambda e: e.dma_start(negbase_sb[:], self.negbase_in),
                 writes=["negbase_sb"], dma="g")
            P.op("sync", lambda e: e.dma_start(colio_sb[:], self.colio_in),
                 writes=["colio_sb"], dma="g")
            P.op("vector", lambda e: e.tensor_copy(
                    wpost2[:].rearrange("p (h i) -> p h i", i=8),
                    self.wpre_t[:].rearrange("p (i h) -> p h i", h=16) if False else
                    self.wpost_t[:].rearrange("p (i h) -> p h i", h=16)),
                 reads=["wpost_t"], writes=["wpost2"])
            for hdc in range(8):
                P.op("sync", lambda e, hdc=hdc: e.dma_start(
                        xr[:], self.W_in["Wo"][hdc * 128:(hdc + 1) * 128, :]),
                     writes=["at_xr"], dma="g")
                P.op("vector", lambda e, hdc=hdc: e.tensor_copy(
                        wo_sb[:, hdc * D:(hdc + 1) * D], xr[:]),
                     reads=["at_xr"], writes=["wo_sb"])

            kT_v = self.kT_d.rearrange("(g p) j -> p g j", p=64)
            qnT_v = self.qnT_d.rearrange("(g p) t -> p g t", p=64)

            sched = cfg.sched()
            for grp in range(NG):
                ntc, mpos = sched[grp]
                jcs = list(range(0, ntc + 1))
                valid = (ntc + 1) * 128

                for hds in range(4):
                    P.op("sync", lambda e, hds=hds, grp=grp: e.dma_start(
                            qn_sb[:].rearrange("p (g t) -> p g t", g=16)[:, hds * 4:(hds + 1) * 4, :],
                            qnT_v[:, hds * 4:(hds + 1) * 4, grp * 128:(grp + 1) * 128]),
                         reads=["Wq_d"], writes=["qn_sb"], dma="g")

                for jc in jcs:
                    kb = kc_sb[jc % 2]
                    kk = f"kc{jc % 2}"
                    P.op("sync", lambda e, jc=jc, kb=kb: e.dma_start(
                            kb[:].rearrange("p (g j) -> p g j", g=16),
                            kT_v[:, :, jc * 128:(jc + 1) * 128]),
                         reads=["kT_d"], writes=[kk], dma="g")
                    for gq in range(4):
                        for gl in range(4):
                            g = gq * 4 + gl
                            P.op("tensor", lambda e, g=g, gl=gl, kb=kb: e.matmul(
                                    pbig[:, gl * 128:(gl + 1) * 128],
                                    kb[0:64, g * 128:(g + 1) * 128],
                                    qn_sb[0:64, g * 128:(g + 1) * 128],
                                    start=True, stop=True),
                                 reads=[kk, "qn_sb"], writes=["pbig"])
                        P.op("scalar", lambda e, gq=gq: e.activation(
                                z_sb[:].rearrange("p (i g) -> p g i", g=16)[:, gq * 4:(gq + 1) * 4, :],
                                pbig[:], AF.Identity),
                             reads=["pbig"], writes=["z_sb"])
                    for b in range(16):
                        if self.asub == "A1":
                            break
                        sl = b % 2
                        P.op("tensor", lambda e, b=b: e.transpose(
                                pgat[:, 0:128],
                                z_sb[:, b * 128:(b + 1) * 128], self.ident[:]),
                             reads=["z_sb", "ident"], writes=["pgat"])
                        P.op("vector", lambda e, b=b, sl=sl: e.tensor_copy(
                                mr_sb[sl][:], pgat[:, 0:128]),
                             reads=["pgat"], writes=[f"mr{sl}"])
                        if self.asub == "A2":
                            continue
                        P.op("tensor", lambda e, b=b, sl=sl: e.matmul(
                                ppre[:, 0:128], self.wpre_t[:], mr_sb[sl][:],
                                start=True, stop=True),
                             reads=["wpre_t", f"mr{sl}"], writes=["ppre"])
                        if jc == 0:
                            P.op("vector", lambda e: e.tensor_tensor(
                                    ppre[:, 0:128], ppre[:, 0:128],
                                    self.dmask[:], ALU.add),
                                 reads=["ppre", "dmask"], writes=["ppre"])
                        if jc in mpos:
                            off = float(8 * b)
                            P.op("vector", lambda e, grp=grp, off=off, jc=jc: e.tensor_scalar(
                                    ivs[:, 0:1], ivec_sb[:, grp:grp + 1],
                                    negbase_sb[:, jc:jc + 1], off, ALU.add, op1=ALU.add),
                                 reads=["ivec_sb", "negbase_sb"], writes=["ivs"])
                            P.op("vector", lambda e: e.tensor_scalar(
                                    msk[:], colio_sb[:], ivs[:, 0:1], None, ALU.is_gt),
                                 reads=["colio_sb", "ivs"], writes=["msk"])
                            P.op("vector", lambda e: e.scalar_tensor_tensor(
                                    ppre[:, 0:128], msk[:], NEG_BIG,
                                    ppre[:, 0:128], ALU.mult, ALU.add),
                                 reads=["msk", "ppre"], writes=["ppre"])
                        P.op("scalar", lambda e, b=b, jc=jc: e.activation(
                                db[b][:, jc * 128:(jc + 1) * 128],
                                ppre[:, 0:128], AF.Identity),
                             reads=["ppre"], writes=[f"db{b}"])

                if self.asub.startswith("A"):
                    continue
                for b in range(16):
                    dcur, kcur = db[b], f"db{b}"
                    for r in range(8):
                        P.op("vector", lambda e, dcur=dcur, valid=valid: e.max(
                                m8t[:], dcur[:, 0:valid]),
                             reads=[kcur], writes=["m8t"])
                        if r < 7:
                            dnew, knew = (sc0, "sc0") if r % 2 == 0 else (sc1, "esb")
                            P.op("vector", lambda e, dcur=dcur, dnew=dnew, valid=valid:
                                    e.match_replace(dnew[:, 0:valid], m8t[:],
                                                    dcur[:, 0:valid], NEG_BIG),
                                 reads=["m8t", kcur], writes=[knew])
                            dcur, kcur = dnew, knew
                    P.op("vector", lambda e, b=b: e.tensor_copy(vks[:, b:b + 1], m8t[:, 7:8]),
                         reads=["m8t"], writes=["vks"])
                    if grp == 0:
                        P.op("vector", lambda e, b=b: e.copy_predicated(
                                vks[:, b:b + 1], shortsel[:, b:b + 1],
                                self.cpool[:, 3:4]),
                             reads=["shortsel", "cpool", "vks"], writes=["vks"])
                    P.op("vector", lambda e, b=b: e.tensor_scalar(
                            vks[:, 16 + b:17 + b], vks[:, b:b + 1], -1.0, None, ALU.mult),
                         reads=["vks"], writes=["vks"])
                    P.op("scalar", lambda e, b=b, valid=valid: e.activation(
                            esb[:, 0:valid], db[b][:, 0:valid], AF.Exp,
                            bias=vks[:, 16 + b:17 + b]),
                         reads=[f"db{b}", "vks"], writes=["esb"])
                    P.op("vector", lambda e, b=b, valid=valid: e.scalar_tensor_tensor(
                            db[b][:, 0:valid], db[b][:, 0:valid], vks[:, b:b + 1],
                            esb[:, 0:valid], ALU.is_ge, ALU.mult,
                            accum_out=vks[:, 32 + b:33 + b]),
                         reads=[f"db{b}", "vks", "esb"], writes=[f"db{b}", "vks_den"])
                for b in range(16):
                    P.op("vector", lambda e, b=b: e.reciprocal(vks[:, 48 + b:49 + b],
                                                               vks[:, 32 + b:33 + b]),
                         reads=["vks_den"], writes=["vks_r"])
                    P.op("vector", lambda e, b=b: e.tensor_scalar(
                            wscl[b][:], wpost2[:], vks[:, 48 + b:49 + b], None, ALU.mult),
                         reads=["wpost2", "vks_r"], writes=[f"wscl{b}"])

                for jc in jcs:
                    vb = vc_sb[jc % 2]
                    vkk = "vc0"
                    P.op("sync", lambda e, jc=jc, vb=vb: e.dma_start(
                            vb[:], self.v_d[jc * 128:(jc + 1) * 128, :]),
                         reads=["v_d"], writes=[vkk], dma="g")
                    p2 = pt2_sb[jc % 2]
                    p2k = "pt20"
                    for b in range(16):
                        P.op("tensor", lambda e, b=b, jc=jc: e.matmul(
                                ppost[:, 0:128],
                                db[b][:, jc * 128:(jc + 1) * 128], wscl[b][:],
                                start=True, stop=True),
                             reads=[f"db{b}", f"wscl{b}"], writes=["ppost"])
                        P.op("vector", lambda e, b=b, p2=p2: e.tensor_copy(
                                p2[:].rearrange("p (h bi) -> p h bi", bi=128)[:, :, b * 8:(b + 1) * 8],
                                ppost[:, 0:128].rearrange(
                                    "p (h i) -> p h i", i=8)),
                             reads=["ppost"], writes=[p2k])
                    for h in range(16):
                        ph = (h % 2) * 64
                        P.op("tensor", lambda e, h=h, ph=ph, vb=vb, p2=p2: e.matmul(
                                pout2[ph:ph + 64, (h // 2) * 128:(h // 2) * 128 + 128],
                                vb[:, h * 64:(h + 1) * 64],
                                p2[:, h * 128:(h + 1) * 128],
                                start=True, stop=True,
                                tile_position=(0, ph)),
                             reads=[vkk, p2k], writes=["pout2"])
                    if jc == 0:
                        P.op("vector", lambda e: e.tensor_copy(out2sb[:], pout2[:]),
                             reads=["pout2"], writes=["out2sb"])
                    else:
                        P.op("vector", lambda e: e.tensor_add(out2sb[:], out2sb[:], pout2[:]),
                             reads=["pout2", "out2sb"], writes=["out2sb"])

                for hds in range(8):
                    P.op("sync", lambda e, hds=hds, grp=grp: e.dma_start(
                            sg_sb[:, hds * 128:(hds + 1) * 128],
                            self.sgT_d[hds * 128:(hds + 1) * 128, grp * 128:(grp + 1) * 128]),
                         reads=["sgT_d"], writes=["sg_sb"], dma="g")
                P.op("vector", lambda e: e.tensor_mul(gt_sb[:], out2sb[:], sg_sb[:]),
                     reads=["out2sb", "sg_sb"], writes=["sg_sb"])
                for dms in range(8):
                    for hdc in range(8):
                        P.op("tensor", lambda e, dms=dms, hdc=hdc: e.matmul(
                                pwo[:, 0:128],
                                wo_sb[:, hdc * D + dms * 128: hdc * D + (dms + 1) * 128],
                                gt_sb[:, hdc * 128:(hdc + 1) * 128],
                                start=(hdc == 0), stop=(hdc == 7)),
                             reads=["wo_sb", "sg_sb"], writes=["pwo"])
                    P.op("vector", lambda e, dms=dms: e.tensor_copy(yTs[:], pwo[:, 0:128]),
                         reads=["pwo"], writes=["at_yTs"])
                    P.op("tensor", lambda e, dms=dms: e.transpose(
                            ptw[:, 0:128], yTs[:], self.ident_f[:]),
                         reads=["at_yTs", "ident_f"], writes=["ptw"])
                    P.op("vector", lambda e, dms=dms: e.tensor_copy(
                            ytok[:, dms * 128:(dms + 1) * 128], ptw[:, 0:128]),
                         reads=["ptw"], writes=["out2sb"])
                P.op("sync", lambda e, grp=grp: e.dma_start(
                        xr[:], self.x1_d[grp * 128:(grp + 1) * 128, :]),
                     reads=["x_dstf1"], writes=["at_xr"], dma="g")
                P.op("vector", lambda e: e.tensor_add(xr[:], xr[:], ytok[:]),
                     reads=["at_xr", "out2sb"], writes=["at_xr"])
                P.op("gpsimd", lambda e, grp=grp: e.dma_start(
                        self.x2_d[grp * 128:(grp + 1) * 128, :], xr[:]),
                     reads=["at_xr"], writes=["x2_d"], dma="g")
            P.retire_all()
            P.flush(self.block)


    def finish(self):
        deps = self.P.barrier_deps()

        def fin(eng, deps=deps):
            for sem, val in deps:
                eng.wait_ge(sem, val)
        self.P.flush(self.block)
        self.block.vector(fin)
        self.ctx.close()
        return self.nc


def build_x1(cfg: Cfg):
    k = Kern(cfg)
    P, nc = k.P, k.nc
    k.stage_consts()
    with nc.sbuf_tensor("wfcqT_sb", [128, 8 * HID], BF16) as wfcqT_sb, \
         nc.sbuf_tensor("wprojqT_sb", [128, 32 * D], BF16) as wprojqT_sb:
        k.stage_quant(wfcqT_sb, wprojqT_sb)
        k.putil_ctx.close()
        k.stage_ffn(k.x_in, k.x1_d, wfcqT_sb, wprojqT_sb, "f1", n_tok=cfg.t_full)
        P.flush(k.block)
    cp = k.sbuf("cp", [128, D], F32)
    for t in range(cfg.n_ttiles):
        P.op("sync", lambda e, t=t: e.dma_start(cp[:], k.x1_d[t * 128:(t + 1) * 128, :]),
             reads=["x_dstf1"], writes=["cp"], dma="g")
        P.op("gpsimd", lambda e, t=t: e.dma_start(k.y_out[t * 128:(t + 1) * 128, :], cp[:]),
             reads=["cp"], writes=["y_out"], dma="g")
    return k.finish(), k.din

def build_full(cfg: Cfg):
    k = Kern(cfg)
    P, nc = k.P, k.nc
    k.stage_consts()
    with nc.sbuf_tensor("wfcqT_sb", [128, 8 * HID], BF16) as wfcqT_sb, \
         nc.sbuf_tensor("wprojqT_sb", [128, 32 * D], BF16) as wprojqT_sb:
        k.stage_quant(wfcqT_sb, wprojqT_sb)
        k.P.retire_all()
        k.P.flush(k.block)
        k.putil_ctx.close()
        k.stage_ffn(k.x_in, k.x1_d, wfcqT_sb, wprojqT_sb, "f1", n_tok=cfg.t_full)
    k.stage_proj()
    k.stage_kv_assemble()
    k.stage_attn()
    with nc.sbuf_tensor("wfcqT_sb2", [128, 8 * HID], BF16) as w1, \
         nc.sbuf_tensor("wprojqT_sb2", [128, 32 * D], BF16) as w2:
        for dmc in range(8):
            P.op("sync", lambda e, dmc=dmc: e.dma_start(
                    w1[:, dmc * HID:(dmc + 1) * HID], k.wfcqT_d[dmc * 128:(dmc + 1) * 128, :]),
                 reads=["wfcqT_d"], writes=["wfcqT_sb"], dma="g")
        for hc in range(32):
            P.op("sync", lambda e, hc=hc: e.dma_start(
                    w2[:, hc * D:(hc + 1) * D], k.wprojqT_d[hc * 128:(hc + 1) * 128, :]),
                 reads=["wprojqT_d"], writes=["wprojqT_sb"], dma="g")
        k.stage_ffn(k.x2_d, k.y_out, w1, w2, "f2")
    return k.finish(), k.din


def build_quant_dbg(cfg: Cfg, upto="quant", sub="all"):
    k = Kern(cfg)
    P, nc = k.P, k.nc
    k.stage_consts()
    if upto == "consts":
        cp = k.sbuf("cpdbg", [128, D], F32)
        P.op("vector", lambda e: e.memset(cp[:], 1.0), writes=["cpdbg"])
        for t in range(cfg.t_loc // 128):
            P.op("gpsimd", lambda e, t=t: e.dma_start(k.y_out[t * 128:(t + 1) * 128, :], cp[:]),
                 reads=["cpdbg"], writes=["y_out"], dma="g")
        return k.finish(), k.din
    with nc.sbuf_tensor("wfcqT_sb", [128, 8 * HID], BF16) as wfcqT_sb, \
         nc.sbuf_tensor("wprojqT_sb", [128, 32 * D], BF16) as wprojqT_sb:
        k.stage_quant(wfcqT_sb, wprojqT_sb, sub=sub)
        P.flush(k.block)
    if sub != "all":
        cp = k.sbuf("cpdbg", [128, D], F32)
        P.op("vector", lambda e: e.memset(cp[:], 2.0), writes=["cpdbg"])
        for t in range(cfg.t_loc // 128):
            P.op("gpsimd", lambda e, t=t: e.dma_start(k.y_out[t * 128:(t + 1) * 128, :], cp[:]),
                 reads=["cpdbg"], writes=["y_out"], dma="g")
        return k.finish(), k.din
    cp = k.sbuf("cpdbg", [128, D], F32)
    cpb = k.sbuf("cpdbgb", [128, D], BF16)
    for t in range(8):
        P.op("sync", lambda e, t=t: e.dma_start(cpb[:], k.wfcqT_d[t * 128:(t + 1) * 128, 0:D]),
             reads=["wfcqT_d"], writes=["cpdbgb"], dma="g")
        P.op("vector", lambda e, t=t: e.tensor_copy(cp[:], cpb[:]),
             reads=["cpdbgb"], writes=["cpdbg"])
        P.op("gpsimd", lambda e, t=t: e.dma_start(k.y_out[t * 128:(t + 1) * 128, :], cp[:]),
             reads=["cpdbg"], writes=["y_out"], dma="g")
    return k.finish(), k.din


def build_dbg(cfg: Cfg, upto="proj", asub="full"):
    k = Kern(cfg)
    P, nc = k.P, k.nc
    k.stage_consts()
    with nc.sbuf_tensor("wfcqT_sb", [128, 8 * HID], BF16) as wfcqT_sb, \
         nc.sbuf_tensor("wprojqT_sb", [128, 32 * D], BF16) as wprojqT_sb:
        k.stage_quant(wfcqT_sb, wprojqT_sb)
        k.P.retire_all()
        k.P.flush(k.block)
        k.putil_ctx.close()
        k.stage_ffn(k.x_in, k.x1_d, wfcqT_sb, wprojqT_sb, "f1", n_tok=cfg.t_full)
    k.stage_proj()
    if upto == "proj":
        cp = k.sbuf("cpdbg", [128, D], F32)
        cpb = k.sbuf("cpdbgb", [128, D], BF16)
        for t in range(8):
            P.op("sync", lambda e, t=t: e.dma_start(cpb[:], k.qnT_d[t * 128:(t + 1) * 128, 0:D]),
                 reads=["Wq_d"], writes=["cpdbgb"], dma="g")
            P.op("vector", lambda e, t=t: e.tensor_copy(cp[:], cpb[:]),
                 reads=["cpdbgb"], writes=["cpdbg"])
            P.op("gpsimd", lambda e, t=t: e.dma_start(k.y_out[t * 128:(t + 1) * 128, :], cp[:]),
                 reads=["cpdbg"], writes=["y_out"], dma="g")
        return k.finish(), k.din
    k.stage_kv_assemble()
    k.stage_attn_v1(asub=asub)
    if asub != "full":
        cp = k.sbuf("cpdbg", [128, D], F32)
        P.op("vector", lambda e: e.memset(cp[:], 5.0), writes=["cpdbg"])
        for t in range(cfg.t_loc // 128):
            P.op("gpsimd", lambda e, t=t: e.dma_start(k.y_out[t * 128:(t + 1) * 128, :], cp[:]),
                 reads=["cpdbg"], writes=["y_out"], dma="g")
        return k.finish(), k.din
    cp = k.sbuf("cpdbg", [128, D], F32)
    for t in range(cfg.t_loc // 128):
        P.op("sync", lambda e, t=t: e.dma_start(cp[:], k.x2_d[t * 128:(t + 1) * 128, :]),
             reads=["x2_d"], writes=["cpdbg"], dma="g")
        P.op("gpsimd", lambda e, t=t: e.dma_start(k.y_out[t * 128:(t + 1) * 128, :], cp[:]),
             reads=["cpdbg"], writes=["y_out"], dma="g")
    return k.finish(), k.din


from concourse.bass_utils import run_bass_kernel_spmd


def kernel(**inputs):
    SEQ, TLOC, NCORES = 4096, 2048, 8
    base = {
        "Wq": inputs["L0_attn_Wq"], "Wk": inputs["L0_attn_Wk"], "Wv": inputs["L0_attn_Wv"],
        "Wg": inputs["L0_attn_Wg"], "Wo": inputs["L0_attn_Wo"],
        "bg": inputs["L0_attn_bg"],
        "Wpre": inputs["L0_attn_W_pre_th"], "Wpost": inputs["L0_attn_W_post_th"],
        "memk": inputs["L0_attn_mem_k"], "memv": inputs["L0_attn_mem_v"],
        "hs": np.ascontiguousarray(inputs["L0_attn_head_scale_params"]).reshape(16),
        "Wfc": inputs["L0_ffn_W_fc"], "Wproj": inputs["L0_ffn_W_proj"],
        "lnw": inputs["L0_ffn_ln_w"], "lnb": inputs["L0_ffn_ln_b"],
    }
    base = {k: np.ascontiguousarray(v, dtype=np.float32) for k, v in base.items()}
    x = np.ascontiguousarray(inputs["L0_attn_x"], dtype=np.float32)

    in_maps = []
    tokmaps = []
    cfg0 = None
    for c in range(NCORES):
        beta, eta = c // 2, c % 2
        blocks = [(cc * 128, 128) for cc in range(SEQ // 128) if cc % 2 == eta]
        pblocks = [(cc * 128, 128) for cc in range(SEQ // 128) if cc % 2 != eta]
        cfg = Cfg(seq=SEQ, t_loc=TLOC, tok_blocks=blocks)
        if cfg0 is None:
            cfg0 = cfg
        tokmap = np.concatenate([np.arange(gs, gs + ln) for gs, ln in blocks])
        ptokmap = np.concatenate([np.arange(gs, gs + ln) for gs, ln in pblocks])
        tokmaps.append((beta, tokmap))
        m = dict(base)
        m.update(cfg.host_consts())
        m["x_in"] = np.ascontiguousarray(
            np.concatenate([x[beta][tokmap], x[beta][ptokmap]], axis=0))
        in_maps.append(m)

    nc, _ = build_full(cfg0)
    res = run_bass_kernel_spmd(nc, in_maps, core_ids=list(range(NCORES)))
    out = np.zeros((4, SEQ, 1024), dtype=np.float32)
    for c in range(NCORES):
        beta, tokmap = tokmaps[c]
        out[beta][tokmap] = res.results[c]["y_out"]
    return out


# version 14
# speedup vs baseline: 1.1785x; 1.0307x over previous
import math
from contextlib import ExitStack

import numpy as np
import concourse.bass as bass
import concourse.mybir as mybir

F32 = mybir.dt.float32
BF16 = mybir.dt.bfloat16
AF = mybir.ActivationFunctionType
ALU = mybir.AluOpType
AX = mybir.AxisListType

D = 1024
H = 16
DH = 64
HID = 4096
NMEM = 4
TOPK = 64
LN_EPS = 1e-5
ABS_EPS = 1e-6
NEG_BIG = -3.3895314e38
VK_SHORT = -30.0
QK_SCALE = 10.0

ENGS = ("sync", "tensor", "vector", "scalar", "gpsimd")
DMA_ENGS = ("sync", "gpsimd", "scalar", "vector")


class Prog:

    def __init__(self, nc, ctx):
        self.nc = nc
        self.ctx = ctx
        self.items = {e: [] for e in ENGS}
        self.esem = {e: ctx.enter_context(nc.semaphore(f"sem_{e}")) for e in ENGS}
        self.ecnt = {e: 0 for e in ENGS}
        self.dsem = {}
        self.last_w = {}
        self.readers = {}
        self.waited = {e: {} for e in ENGS}
        self.gdep = {}

    def _dma_sem(self, name, eng):
        if name not in self.dsem:
            self.dsem[name] = [self.ctx.enter_context(self.nc.semaphore(f"d_{name}")), 0, eng]
        return self.dsem[name]

    def _wait(self, eng, dep):
        kind, a, b = dep
        if kind == "d":
            rec = self.dsem[a]
            sem, val = rec[0], rec[1]
        else:
            sem, val = a, b
            if eng == "tensor" and sem is self.esem["tensor"]:
                return
        if self.waited[eng].get(sem, -1) >= val:
            return
        self.items[eng].append(("w", sem, val))
        self.waited[eng][sem] = val

    def retire_all(self):
        for m in (self.last_w, self.readers):
            for deps in m.values():
                for dep in deps:
                    kind, a, b = dep
                    if kind == "d":
                        rec = self.dsem[a]
                        sem, val = rec[0], rec[1]
                    else:
                        sem, val = a, b
                    if self.gdep.get(sem, -1) < val:
                        self.gdep[sem] = val
        self.last_w.clear()
        self.readers.clear()

    def op(self, eng, emit, reads=(), writes=(), dma=None):
        for sem, val in self.gdep.items():
            self._wait(eng, ("e", sem, val))
        for b in reads:
            for dep in self.last_w.get(b, ()):
                self._wait(eng, dep)
        for b in writes:
            for dep in self.last_w.get(b, ()):
                self._wait(eng, dep)
            for dep in self.readers.get(b, ()):
                self._wait(eng, dep)
        if dma is not None:
            assert len(writes) >= 1
            dma = "b_" + writes[0]
            rec = self._dma_sem(dma, eng)
            rec[1] += 16
            sem, n = rec[0], 16
            done = ("d", dma, rec[1])
        else:
            sem, n = self.esem[eng], 1
            self.ecnt[eng] += 1
            done = ("e", sem, self.ecnt[eng])
        self.items[eng].append(("op", emit, sem, n))
        for b in reads:
            self.readers.setdefault(b, []).append(done)
        for b in writes:
            self.last_w[b] = [done]
            self.readers[b] = []
        return done

    def barrier_deps(self):
        deps = [(self.esem[e], self.ecnt[e]) for e in ENGS if self.ecnt[e]]
        deps += [(rec[0], rec[1]) for rec in self.dsem.values() if rec[1]]
        return deps

    def flush(self, block):
        for e in ENGS:
            items = self.items[e]
            if not items:
                continue
            self.items[e] = []

            def emit_all(eng, items=items):
                for it in items:
                    if it[0] == "w":
                        eng.wait_ge(it[1], it[2])
                    else:
                        _, emitfn, sem, n = it
                        emitfn(eng).then_inc(sem, n)
            getattr(block, e)(emit_all)


class Cfg:
    def __init__(self, seq=4096, t_loc=2048, tok_blocks=None, debug_out=None):
        self.seq = seq
        self.t_loc = t_loc
        self.t_full = seq
        self.tok_blocks = tok_blocks or [(0, t_loc)]
        assert sum(l for _, l in self.tok_blocks) == t_loc
        self.no_exchange = (t_loc == seq)
        self.pair_blocks = None
        self.debug_out = debug_out
        self.n_ttiles = t_loc // 128
        self.n_jc = 1 + seq // 128
        self.jcols = 128 + seq

    def g_tok(self, l):
        off = 0
        for gs, ln in self.tok_blocks:
            if l < off + ln:
                return gs + (l - off)
            off += ln
        raise ValueError(l)

    def rank_blocks(self, r):
        blocks = self.pair_blocks[r]
        out, off = [], 0
        for gs, ln in blocks:
            out.append((gs, off, ln))
            off += ln
        return out

    def pos_local_chunk(self, m):
        if self.no_exchange:
            return m
        nloc = self.t_loc // 128
        return (m // 2) if m % 2 == 0 else nloc + m // 2

    def pos_global_chunk(self, m):
        if self.no_exchange:
            return m
        own = [gs // 128 for gs, _ in self.tok_blocks]
        all_ = set(range(self.seq // 128))
        part = sorted(all_ - set(own))
        return own[m // 2] if m % 2 == 0 else part[m // 2]

    def sched(self):
        NG = self.t_loc // 128
        out = []
        for k in range(NG):
            if self.no_exchange:
                ntc = k + 1
                mpos = {k + 1}
            else:
                ntc = 2 * k + 2
                mpos = {2 * k + 1, 2 * k + 2}
            out.append((ntc, mpos))
        return out

    def host_consts(self):
        import ml_dtypes
        ident = np.eye(128, dtype=np.float32)
        dmask = np.zeros((128, 128), dtype=np.float32)
        dmask[:, NMEM:] = NEG_BIG
        NG = self.t_loc // 128
        ivec = np.zeros((128, NG), np.float32)
        ss = np.zeros((128, 16), np.float32)
        for k in range(NG):
            g0 = self.g_tok(k * 128)
            ivec[:, k] = g0 + np.arange(128) // 16
        for b in range(16):
            g0 = self.g_tok(0)
            ss[:, b] = ((g0 + 8 * b + np.arange(128) // 16 + 5) < TOPK).astype(np.float32)
        njc = self.n_jc
        negbase = np.zeros((128, njc), np.float32)
        for m in range(njc - 1):
            negbase[:, m + 1] = -128.0 * self.pos_global_chunk(m)
        return {"ident_c": ident.astype(ml_dtypes.bfloat16),
                "dmask_c": dmask.astype(ml_dtypes.bfloat16),
                "shortsel_c": ss.astype(np.uint32),
                "ivec_c": ivec,
                "negbase_c": negbase,
                "colio_c": np.broadcast_to(np.arange(128, dtype=np.float32)[None, :],
                                           (128, 128)).copy()}


class Kern:

    def __init__(self, cfg: Cfg):
        self.cfg = cfg
        nc = self.nc = bass.Bass("TRN2", target_bir_lowering=False, debug=False)
        T, SEQ, JCOL = cfg.t_loc, cfg.seq, cfg.jcols
        self.din = {}

        def dt_in(name, shape, dtype=F32):
            self.din[name] = nc.dram_tensor(name, shape, dtype, kind="ExternalInput").ap()
            return self.din[name]

        self.x_in = dt_in("x_in", [cfg.t_full, D])
        self.W_in = {w: dt_in(w, [D, D]) for w in ("Wq", "Wk", "Wv", "Wg", "Wo")}
        self.bg_in = dt_in("bg", [D])
        self.Wpre_in = dt_in("Wpre", [H, H])
        self.Wpost_in = dt_in("Wpost", [H, H])
        self.memk_in = dt_in("memk", [H, NMEM, DH])
        self.memv_in = dt_in("memv", [H, NMEM, DH])
        self.hs_in = dt_in("hs", [H])
        self.Wfc_in = dt_in("Wfc", [HID, D])
        self.Wproj_in = dt_in("Wproj", [D, HID])
        self.lnw_in = dt_in("lnw", [HID])
        self.lnb_in = dt_in("lnb", [HID])
        self.ident_in = dt_in("ident_c", [128, 128], BF16)
        self.dmask_in = dt_in("dmask_c", [128, 128], BF16)
        self.shortsel_in = dt_in("shortsel_c", [128, 16], mybir.dt.uint32)
        self.ivec_in = dt_in("ivec_c", [128, cfg.t_loc // 128], F32)
        self.negbase_in = dt_in("negbase_c", [128, cfg.n_jc], F32)
        self.colio_in = dt_in("colio_c", [128, 128], F32)

        self.y_out = nc.dram_tensor("y_out", [T, D], F32, kind="ExternalOutput").ap()

        dram = lambda name, shape, dtype: nc.dram_tensor(name, shape, dtype, kind="Internal").ap()
        self.x1_d = dram("x1_d", [cfg.t_full, D], F32)
        self.x2_d = dram("x2_d", [T, D], F32)
        self.wfcqT_d = dram("wfcqT_d", [D, HID], BF16)
        self.wprojqT_d = dram("wprojqT_d", [HID, D], BF16)
        self.qnT_d = dram("qnT_d", [D, T], BF16)
        self.sgT_d = dram("sgT_d", [D, T], BF16)
        self.knT_own_d = dram("knT_own_d", [D, cfg.t_full], BF16)
        self.v_own_d = dram("v_own_d", [cfg.t_full, D], BF16)
        self.kT_d = dram("kT_d", [D, JCOL], BF16)
        self.v_d = dram("v_d", [JCOL, D], BF16)

        self.ctx = ExitStack()
        sbuf = self.sbuf = lambda name, shape, dtype: self.ctx.enter_context(
            nc.sbuf_tensor(name, shape, dtype))

        self.ident = sbuf("ident", [128, 128], BF16)
        self.ident_f = sbuf("ident_f", [128, 128], F32)
        self.dmask = sbuf("dmask", [128, 128], BF16)
        self.ones128_b = sbuf("ones128_b", [128, 1], BF16)
        self.ones128_f = sbuf("ones128_f", [128, 1], F32)
        self.onesrow_b = sbuf("onesrow_b", [1, 128], BF16)
        self.onesrow_f = sbuf("onesrow_f", [1, 128], F32)
        self.onesh_b = sbuf("onesh_b", [128, 2], BF16)
        self.onesh_bq = sbuf("onesh_bq", [128, 2], BF16)
        self.cpool = sbuf("cpool", [128, 4], F32)
        self.lnw_sb = sbuf("lnw_sb", [128, 32], F32)
        self.lnb_sb = sbuf("lnb_sb", [128, 32], F32)
        self.wpre_t = sbuf("wpre_t", [128, 128], BF16)
        self.wpost_t = sbuf("wpost_t", [128, 128], BF16)

        self.putil_ctx = ExitStack()
        self.putil = self.putil_ctx.enter_context(nc.psum_tensor("putil", [128, 128], F32))
        self.P = Prog(nc, self.ctx)
        self.block = self.ctx.enter_context(nc.Block())

    def load_vec_T(self, vec_in, n, dst, key, psum_ap=None, alloc=None):
        P, nc = self.P, self.nc
        nch = n // 128
        stk = key + "_stg"
        stg = (alloc or self.sbuf)(key + "_stg", [nch, 128], F32)
        pst = (psum_ap if psum_ap is not None else self.putil)[:, 0:nch]
        P.op("sync", lambda e: e.dma_start(stg[:], vec_in.rearrange("(c p) -> c p", p=128)),
             writes=[stk], dma="g")
        P.op("tensor", lambda e: e.transpose(pst, stg[:], self.ident_f[0:nch, 0:nch]),
             reads=[stk, "ident_f"], writes=["putil"])
        P.op("vector", lambda e: e.tensor_copy(dst[:], pst), reads=["putil"], writes=[key])

    def stage_consts(self):
        P, nc = self.P, self.nc
        P.op("sync", lambda e: e.dma_start(self.ident[:], self.ident_in), writes=["ident"], dma="g")
        P.op("sync", lambda e: e.dma_start(self.dmask[:], self.dmask_in),
             writes=["dmask"], dma="g")
        P.op("vector", lambda e: e.tensor_copy(self.ident_f[:], self.ident[:]),
             reads=["ident"], writes=["ident_f"])
        self.load_vec_T(self.lnw_in, 4096, self.lnw_sb, "lnw_sb")
        self.load_vec_T(self.lnb_in, 4096, self.lnb_sb, "lnb_sb")
        P.op("vector", lambda e: e.memset(self.ones128_b[:], 1.0), writes=["ones128_b"])
        P.op("vector", lambda e: e.memset(self.ones128_f[:], 1.0), writes=["ones128_f"])
        P.op("vector", lambda e: e.memset(self.onesrow_b[:], 1.0), writes=["onesrow_b"])
        P.op("vector", lambda e: e.memset(self.onesrow_f[:], 1.0), writes=["onesrow_f"])
        P.op("vector", lambda e: e.memset(self.cpool[:, 0:1], float(ABS_EPS)), writes=["cpool"])
        P.op("vector", lambda e: e.memset(self.cpool[:, 1:2], float(LN_EPS)), writes=["cpool"])
        P.op("vector", lambda e: e.memset(self.cpool[:, 2:3], float(np.log(QK_SCALE))), writes=["cpool"])
        P.op("vector", lambda e: e.memset(self.cpool[:, 3:4], float(VK_SHORT)), writes=["cpool"])
        P.op("vector", lambda e: e.memset(self.onesh_b[:], 0.0), writes=["onesh_b"])
        P.op("vector", lambda e: e.memset(self.onesh_b[0:64, 0:1], 1.0), writes=["onesh_b"])
        P.op("vector", lambda e: e.memset(self.onesh_b[64:128, 1:2], 1.0), writes=["onesh_b"])
        P.op("vector", lambda e: e.memset(self.onesh_bq[:], 0.0), writes=["onesh_bq"])
        P.op("vector", lambda e: e.memset(self.onesh_bq[0:64, 0:1], 0.1), writes=["onesh_bq"])
        P.op("vector", lambda e: e.memset(self.onesh_bq[64:128, 1:2], 0.1), writes=["onesh_bq"])

        wsm = self.sbuf("wsm", [16, 64], F32)
        wsm_b = self.sbuf("wsm_b", [16, 32], BF16)
        wsmT = self.wsmT = self.sbuf("wsmT", [16, 32], BF16)
        pm16 = self.putil[0:16, 0:16].bitcast(BF16)
        P.op("sync", lambda e: e.dma_start(wsm[:, 0:16], self.Wpre_in), writes=["wsm"], dma="g")
        P.op("sync", lambda e: e.dma_start(wsm[:, 16:32], self.Wpost_in), writes=["wsm"], dma="g")
        P.op("sync", lambda e: e.dma_start(wsm[0:16, 32:33], self.hs_in[:, None]), writes=["wsm"], dma="g")
        P.op("vector", lambda e: e.tensor_scalar(wsm[0:16, 48:64], wsm[0:16, 16:32],
                                                 wsm[0:16, 32:33], None, ALU.mult),
             reads=["wsm"], writes=["wsm48"])
        P.op("vector", lambda e: e.tensor_copy(wsm_b[:, 0:16], wsm[0:16, 0:16]),
             reads=["wsm"], writes=["wsm_b"])
        P.op("vector", lambda e: e.tensor_copy(wsm_b[:, 16:32], wsm[0:16, 48:64]),
             reads=["wsm48"], writes=["wsm_b"])
        P.op("tensor", lambda e: e.transpose(pm16[:, 0:16], wsm_b[:, 0:16], self.ident[0:16, 0:16]),
             reads=["wsm_b", "ident"], writes=["putil"])
        P.op("tensor", lambda e: e.transpose(pm16[:, 16:32], wsm_b[:, 16:32], self.ident[0:16, 0:16]),
             reads=["wsm_b", "ident"], writes=["putil"])
        P.op("vector", lambda e: e.tensor_copy(wsmT[:], pm16[:]), reads=["putil"], writes=["wsmT"])
        P.op("vector", lambda e: e.memset(self.wpre_t[:], 0.0), writes=["wpre_t"])
        P.op("vector", lambda e: e.memset(self.wpost_t[:], 0.0), writes=["wpost_t"])
        for io in range(8):
            P.op("gpsimd", lambda e, io=io: e.dma_start(
                    self.wpre_t[16 * io:16 * io + 16, 16 * io:16 * io + 16], wsmT[:, 0:16]),
                 reads=["wsmT"], writes=["wpre_t"], dma="g")
            P.op("gpsimd", lambda e, io=io: e.dma_start(
                    self.wpost_t[16 * io:16 * io + 16, 16 * io:16 * io + 16], wsmT[:, 16:32]),
                 reads=["wsmT"], writes=["wpost_t"], dma="g")

    def stage_quant(self, wfcqT_sb, wprojqT_sb, sub="all"):
        P, nc = self.P, self.nc
        self.qsub = sub
        qctx = ExitStack()
        qsbuf = lambda name, shape, dtype: qctx.enter_context(nc.sbuf_tensor(name, shape, dtype))
        qstg = [qsbuf(f"qstg{i}", [128, HID], F32) for i in range(2)]
        qsm = qsbuf("qsm", [128, 40], F32)
        qmask = qsbuf("qmask", [128, HID], F32)
        qbf = qsbuf("qbf", [128, HID], BF16)
        pq = self.putil[:, 64:72]
        pqt = qctx.enter_context(nc.psum_tensor("pqt", [128, 4096], BF16))

        def do_quant(w_in, n_rows, n_cols, tag, out_cb):
            nrc = n_rows // 128
            for c in range(nrc):
                s = qstg[c % 2]
                sk = f"qs{c % 2}"
                P.op("sync", lambda e, c=c, s=s: e.dma_start(
                        s[:, 0:n_cols], w_in[c * 128:(c + 1) * 128, :]), writes=[sk], dma="g")
                P.op("vector", lambda e, c=c, s=s: e.tensor_reduce(
                        qsm[:, c:c + 1], s[:, 0:n_cols], AX.X, ALU.add, apply_absolute_value=True),
                     reads=[sk], writes=["qsm"])
            P.op("vector", lambda e, nrc=nrc: e.tensor_reduce(
                    qsm[:, 36:37], qsm[:, 0:nrc], AX.X, ALU.add), reads=["qsm"], writes=["qsm36"])
            P.op("tensor", lambda e: e.matmul(pq[0:1, 0:1], qsm[:, 36:37], self.ones128_f[:],
                                              start=True, stop=True),
                 reads=["qsm36", "ones128_f"], writes=["putil"])
            P.op("scalar", lambda e, n=n_rows * n_cols: e.activation(
                    qsm[0:1, 37:38], pq[0:1, 0:1], AF.Identity, bias=self.cpool[0:1, 0:1], scale=1.0 / n),
                 reads=["putil", "cpool"], writes=["qsm37"])
            P.op("tensor", lambda e: e.matmul(pq[:, 1:2], self.onesrow_f[:], qsm[0:1, 37:38],
                                              start=True, stop=True),
                 reads=["qsm37", "onesrow_f"], writes=["putil"])
            if self.qsub == "absum":
                return
            P.op("vector", lambda e: e.tensor_copy(qsm[:, 38:39], pq[:, 1:2]),
                 reads=["putil"], writes=["gam0"])
            P.op("vector", lambda e: e.reciprocal(qsm[:, 39:40], qsm[:, 38:39]),
                 reads=["gam0"], writes=["gam"])
            for c in range(nrc):
                s = qstg[c % 2]
                sk = f"qs{c % 2}"
                P.op("sync", lambda e, c=c, s=s: e.dma_start(
                        s[:, 0:n_cols], w_in[c * 128:(c + 1) * 128, :]), writes=[sk], dma="g")
                P.op("vector", lambda e, s=s, n_cols=n_cols: e.tensor_scalar(
                        s[:, 0:n_cols], s[:, 0:n_cols], qsm[:, 39:40], None, ALU.mult),
                     reads=[sk, "gam"], writes=[sk])
                P.op("vector", lambda e, s=s, n_cols=n_cols: e.tensor_scalar(
                        qmask[:, 0:n_cols], s[:, 0:n_cols], 0.5, None, ALU.is_gt),
                     reads=[sk], writes=["qmask"])
                P.op("vector", lambda e, s=s, n_cols=n_cols: e.tensor_scalar(
                        s[:, 0:n_cols], s[:, 0:n_cols], -0.5, None, ALU.is_lt),
                     reads=[sk], writes=[sk])
                P.op("vector", lambda e, s=s, n_cols=n_cols: e.tensor_sub(
                        qbf[:, 0:n_cols], qmask[:, 0:n_cols], s[:, 0:n_cols]),
                     reads=["qmask", sk], writes=["qbf"])
                if self.qsub != "noq":
                    out_cb(c)

        def fc_cb(c):
            for dmc in range(8):
                pslot = dmc % 4
                P.op("tensor", lambda e, dmc=dmc, pslot=pslot: e.transpose(
                        pqt[:, pslot * 1024:pslot * 1024 + 128],
                        qbf[:, dmc * 128:(dmc + 1) * 128], self.ident[:]),
                     reads=["qbf", "ident"], writes=[f"pqt{pslot}"])
                P.op("vector", lambda e, c=c, dmc=dmc, pslot=pslot: e.tensor_copy(
                        wfcqT_sb[:, dmc * HID + c * 128: dmc * HID + (c + 1) * 128],
                        pqt[:, pslot * 1024:pslot * 1024 + 128]),
                     reads=[f"pqt{pslot}"], writes=["wfcqT_sb"])
        do_quant(self.Wfc_in, HID, D, "fc", fc_cb)
        for dmc in range(8):
            P.op("gpsimd", lambda e, dmc=dmc: e.dma_start(
                    self.wfcqT_d[dmc * 128:(dmc + 1) * 128, :],
                    wfcqT_sb[:, dmc * HID:(dmc + 1) * HID]),
                 reads=["wfcqT_sb"], writes=["wfcqT_d"], dma="g")

        def pj_cb(c):
            for hc in range(32):
                pslot = hc % 4
                P.op("tensor", lambda e, hc=hc, pslot=pslot: e.transpose(
                        pqt[:, pslot * 1024:pslot * 1024 + 128],
                        qbf[:, hc * 128:(hc + 1) * 128], self.ident[:]),
                     reads=["qbf", "ident"], writes=[f"pqt{pslot}"])
                P.op("vector", lambda e, c=c, hc=hc, pslot=pslot: e.tensor_copy(
                        wprojqT_sb[:, hc * D + c * 128: hc * D + (c + 1) * 128],
                        pqt[:, pslot * 1024:pslot * 1024 + 128]),
                     reads=[f"pqt{pslot}"], writes=["wprojqT_sb"])
        do_quant(self.Wproj_in, D, HID, "pj", pj_cb)
        for hc in range(32):
            P.op("gpsimd", lambda e, hc=hc: e.dma_start(
                    self.wprojqT_d[hc * 128:(hc + 1) * 128, :],
                    wprojqT_sb[:, hc * D:(hc + 1) * D]),
                 reads=["wprojqT_sb"], writes=["wprojqT_d"], dma="g")
        P.retire_all()
        P.flush(self.block)
        qctx.close()

    def ln_tile(self, x_ap, xf, fslice, stat, scol, keys, out_bf, oslice):
        P = self.P
        xk, stk, outk = keys
        P.op("sync", lambda e: e.dma_start(xf[:, fslice], x_ap), writes=[xk], dma="g")
        P.op("vector", lambda e: e.tensor_scalar(
                out_bf[:, oslice], xf[:, fslice], -1.0 / D, None, ALU.mult,
                op1=ALU.add, accum_out=stat[:, scol:scol + 1]),
             reads=[xk], writes=[stk, outk])
        P.op("vector", lambda e: e.tensor_scalar(
                xf[:, fslice], xf[:, fslice], stat[:, scol:scol + 1], None, ALU.add),
             reads=[xk, stk], writes=[xk])
        P.op("scalar", lambda e: e.activation(
                out_bf[:, oslice], xf[:, fslice], AF.Square, accum_out=stat[:, scol + 1:scol + 2]),
             reads=[xk], writes=[stk, outk])
        P.op("scalar", lambda e: e.activation(
                stat[:, scol + 1:scol + 2], stat[:, scol + 1:scol + 2], AF.Ln,
                scale=1.0 / D, bias=self.cpool[:, 1:2]),
             reads=[stk, "cpool"], writes=[stk])
        P.op("scalar", lambda e: e.activation(
                stat[:, scol + 1:scol + 2], stat[:, scol + 1:scol + 2], AF.Exp, scale=-0.5),
             reads=[stk], writes=[stk])
        P.op("vector", lambda e: e.tensor_scalar(
                out_bf[:, oslice], xf[:, fslice], stat[:, scol + 1:scol + 2], None, ALU.mult),
             reads=[xk, stk], writes=[outk])

    def stage_ffn(self, x_src_d, x_dst_d, wfcqT_sb, wprojqT_sb, tag, scale_weights=True,
                  n_tok=None):
        P, nc, cfg = self.P, self.nc, self.cfg
        T = n_tok if n_tok is not None else cfg.t_loc
        TB = min(256, T)
        NTB = TB // 128
        NBLK = T // TB
        with ExitStack() as fstk:
            en = fstk.enter_context
            xf = en(nc.sbuf_tensor(f"xf_{tag}", [128, D], F32))
            xr = en(nc.sbuf_tensor(f"xr_{tag}", [128, D], F32))
            stat = en(nc.sbuf_tensor(f"st_{tag}", [128, 8], F32))
            lnxb = en(nc.sbuf_tensor(f"lnx_{tag}", [128, D], BF16))
            lnxT = en(nc.sbuf_tensor(f"lnxT_{tag}", [128, 8 * TB], BF16))
            gh = en(nc.sbuf_tensor(f"gh_{tag}", [128, 32 * TB], BF16))
            gh2 = en(nc.sbuf_tensor(f"gh2_{tag}", [128, TB], BF16))
            tstat = en(nc.sbuf_tensor(f"tstat_{tag}", [1, 2 * TB], F32))
            bcf = en(nc.sbuf_tensor(f"bcf_{tag}", [128, 2 * TB], F32))
            BC = en(nc.sbuf_tensor(f"BC_{tag}", [128, 18], F32))
            bhl = en(nc.sbuf_tensor(f"bhl_{tag}", [128, 2], BF16))
            yTs = en(nc.sbuf_tensor(f"yTs_{tag}", [128, TB], F32))
            yU = en(nc.sbuf_tensor(f"yU_{tag}", [128, TB], F32))
            ytok = en(nc.sbuf_tensor(f"ytok_{tag}", [128, NTB * D], F32))
            pfA = en(nc.psum_tensor(f"pfA_{tag}", [128, 512], F32))
            pfB = en(nc.psum_tensor(f"pfB_{tag}", [128, 512], F32))
            pstA = en(nc.psum_tensor(f"pstA_{tag}", [1, 512], F32))
            pstB = en(nc.psum_tensor(f"pstB_{tag}", [1, 512], F32))
            pxT = en(nc.psum_tensor(f"pxT_{tag}", [128, 1024], BF16))
            ptr = en(nc.psum_tensor(f"ptr_{tag}", [128, 512], F32))
            pyt = en(nc.psum_tensor(f"py_{tag}", [128, 512], F32))
            pBC = en(nc.psum_tensor(f"pBC_{tag}", [128, 512], F32))

            for dmc in range(8):
                for hc in range(32):
                    P.op("vector", lambda e, hc=hc: e.tensor_copy(bhl[:, 0:1], self.lnb_sb[:, hc:hc + 1]),
                         reads=["lnb_sb"], writes=[f"{tag}bhl"])
                    P.op("vector", lambda e, hc=hc: e.tensor_sub(stat[:, 4:5], self.lnb_sb[:, hc:hc + 1],
                                                                 bhl[:, 0:1]),
                         reads=["lnb_sb", f"{tag}bhl"], writes=[f"{tag}blo"])
                    P.op("vector", lambda e: e.tensor_copy(bhl[:, 1:2], stat[:, 4:5]),
                         reads=[f"{tag}blo"], writes=[f"{tag}bhl"])
                    P.op("tensor", lambda e, hc=hc, dmc=dmc: e.matmul(
                            pBC[:, 0:2], wprojqT_sb[:, hc * D + dmc * 128: hc * D + (dmc + 1) * 128],
                            bhl[:], start=(hc == 0), stop=(hc == 31)),
                         reads=["wprojqT_sb", f"{tag}bhl"], writes=[f"{tag}pBC"])
                P.op("vector", lambda e, dmc=dmc: e.tensor_copy(BC[:, 16:18], pBC[:, 0:2]),
                     reads=[f"{tag}pBC"], writes=[f"{tag}BCs"])
                P.op("vector", lambda e, dmc=dmc: e.tensor_add(BC[:, 8 + dmc:9 + dmc],
                                                               BC[:, 16:17], BC[:, 17:18]),
                     reads=[f"{tag}BCs"], writes=[f"{tag}BC"])
            if scale_weights:
                for hc in range(32):
                    P.op("vector", lambda e, hc=hc: e.tensor_scalar(
                            wprojqT_sb[:, hc * D:(hc + 1) * D], wprojqT_sb[:, hc * D:(hc + 1) * D],
                            self.lnw_sb[:, hc:hc + 1], None, ALU.mult),
                         reads=["wprojqT_sb", "lnw_sb"], writes=["wprojqT_sb"])
            for dmc in range(8):
                for hc in range(32):
                    P.op("tensor", lambda e, hc=hc, dmc=dmc: e.matmul(
                            pBC[:, 0:1], wprojqT_sb[:, hc * D + dmc * 128: hc * D + (dmc + 1) * 128],
                            self.ones128_b[:], start=(hc == 0), stop=(hc == 31)),
                         reads=["wprojqT_sb", "ones128_b"], writes=[f"{tag}pBC"])
                P.op("vector", lambda e, dmc=dmc: e.tensor_copy(BC[:, dmc:dmc + 1], pBC[:, 0:1]),
                     reads=[f"{tag}pBC"], writes=[f"{tag}BC"])

            for blk in range(NBLK):
                t0 = blk * TB
                for tt in range(NTB):
                    tok = t0 + tt * 128
                    self.ln_tile(x_src_d[tok:tok + 128, :], xf, slice(0, D), stat, 0,
                                 (f"{tag}xf", f"{tag}stat", f"{tag}lnxb"), lnxb, slice(0, D))
                    for dmc in range(8):
                        P.op("tensor", lambda e, dmc=dmc: e.transpose(
                                pxT[:, 0:128],
                                lnxb[:, dmc * 128:dmc * 128 + 128], self.ident[:]),
                             reads=[f"{tag}lnxb", "ident"], writes=[f"{tag}pxT"])
                        P.op("vector", lambda e, tt=tt, dmc=dmc: e.tensor_copy(
                                lnxT[:, dmc * TB + tt * 128: dmc * TB + (tt + 1) * 128],
                                pxT[:, 0:128]),
                             reads=[f"{tag}pxT"], writes=[f"{tag}lnxT"])

                for s in range(32):
                    pcurt, pk = (pfA, f"{tag}pfA") if s % 2 == 0 else (pfB, f"{tag}pfB")
                    pcur = pcurt[:, 0:TB]
                    for dmc in range(8):
                        P.op("tensor", lambda e, s=s, dmc=dmc, pcur=pcur: e.matmul(
                                pcur, wfcqT_sb[:, dmc * HID + s * 128: dmc * HID + (s + 1) * 128],
                                lnxT[:, dmc * TB:(dmc + 1) * TB],
                                start=(dmc == 0), stop=(dmc == 7)),
                             reads=["wfcqT_sb", f"{tag}lnxT"], writes=[pk])
                    P.op("scalar", lambda e, s=s, pcur=pcur: e.activation(
                            gh[:, s * TB:(s + 1) * TB], pcur, AF.Gelu),
                         reads=[pk], writes=[f"{tag}gh{s}"])
                    P.op("scalar", lambda e, s=s: e.activation(
                            gh2[:], gh[:, s * TB:(s + 1) * TB], AF.Square),
                         reads=[f"{tag}gh{s}"], writes=[f"{tag}gh2"])
                    P.op("tensor", lambda e, s=s: e.matmul(
                            pstA[0:1, 0:TB], self.ones128_b[:], gh[:, s * TB:(s + 1) * TB],
                            start=(s == 0), stop=(s == 31)),
                         reads=[f"{tag}gh{s}", "ones128_b"], writes=[f"{tag}pstA"])
                    P.op("tensor", lambda e, s=s: e.matmul(
                            pstB[0:1, 0:TB], self.ones128_b[:], gh2[:],
                            start=(s == 0), stop=(s == 31)),
                         reads=[f"{tag}gh2", "ones128_b"], writes=[f"{tag}pstB"])

                P.op("scalar", lambda e: e.activation(tstat[0:1, 0:TB], pstA[0:1, 0:TB],
                                                      AF.Identity, scale=1.0 / HID),
                     reads=[f"{tag}pstA"], writes=[f"{tag}mu"])
                P.op("scalar", lambda e: e.activation(tstat[0:1, TB:2 * TB], pstB[0:1, 0:TB],
                                                      AF.Identity, scale=1.0 / HID),
                     reads=[f"{tag}pstB"], writes=[f"{tag}ex2"])
                P.op("vector", lambda e: e.tensor_mul(yU[0:1, 0:TB], tstat[0:1, 0:TB],
                                                      tstat[0:1, 0:TB]),
                     reads=[f"{tag}mu"], writes=[f"{tag}mu2"])
                P.op("vector", lambda e: e.tensor_sub(tstat[0:1, TB:2 * TB], tstat[0:1, TB:2 * TB],
                                                      yU[0:1, 0:TB]),
                     reads=[f"{tag}ex2", f"{tag}mu2"], writes=[f"{tag}var"])
                P.op("scalar", lambda e: e.activation(tstat[0:1, TB:2 * TB], tstat[0:1, TB:2 * TB],
                                                      AF.Ln, bias=self.cpool[0:1, 1:2]),
                     reads=[f"{tag}var", "cpool"], writes=[f"{tag}var"])
                P.op("scalar", lambda e: e.activation(tstat[0:1, TB:2 * TB], tstat[0:1, TB:2 * TB],
                                                      AF.Exp, scale=-0.5),
                     reads=[f"{tag}var"], writes=[f"{tag}var"])
                P.op("vector", lambda e: e.tensor_scalar(
                        tstat[0:1, 0:TB], tstat[0:1, 0:TB], -1.0, None, ALU.mult),
                     reads=[f"{tag}mu"], writes=[f"{tag}mu"])
                P.op("vector", lambda e: e.tensor_mul(tstat[0:1, 0:TB], tstat[0:1, 0:TB],
                                                      tstat[0:1, TB:2 * TB]),
                     reads=[f"{tag}mu", f"{tag}var"], writes=[f"{tag}mrs"])
                P.op("tensor", lambda e: e.matmul(ptr[:, 0:TB], self.onesrow_f[:],
                                                  tstat[0:1, TB:2 * TB], start=True, stop=True),
                     reads=[f"{tag}var", "onesrow_f"], writes=[f"{tag}ptr"])
                P.op("vector", lambda e: e.tensor_copy(bcf[:, 0:TB], ptr[:, 0:TB]),
                     reads=[f"{tag}ptr"], writes=[f"{tag}bcA"])
                P.op("tensor", lambda e: e.matmul(ptr[:, 0:TB], self.onesrow_f[:],
                                                  tstat[0:1, 0:TB], start=True, stop=True),
                     reads=[f"{tag}mrs", "onesrow_f"], writes=[f"{tag}ptr"])
                P.op("vector", lambda e: e.tensor_copy(bcf[:, TB:2 * TB], ptr[:, 0:TB]),
                     reads=[f"{tag}ptr"], writes=[f"{tag}bcB"])

                for dmc in range(8):
                    for s in range(32):
                        P.op("tensor", lambda e, s=s, dmc=dmc: e.matmul(
                                pyt[:, 0:TB], wprojqT_sb[:, s * D + dmc * 128: s * D + (dmc + 1) * 128],
                                gh[:, s * TB:(s + 1) * TB],
                                start=(s == 0), stop=(s == 31)),
                             reads=["wprojqT_sb", f"{tag}gh{s}"], writes=[f"{tag}py"])
                    P.op("vector", lambda e, dmc=dmc: e.scalar_tensor_tensor(
                            yTs[:], pyt[:, 0:TB], 1.0, bcf[:, 0:TB], ALU.mult, ALU.mult),
                         reads=[f"{tag}py", f"{tag}bcA"], writes=[f"{tag}yTs"])
                    P.op("vector", lambda e, dmc=dmc: e.tensor_scalar(
                            yU[:], bcf[:, TB:2 * TB], BC[:, dmc:dmc + 1], BC[:, 8 + dmc:9 + dmc],
                            ALU.mult, op1=ALU.add),
                         reads=[f"{tag}bcB", f"{tag}BC"], writes=[f"{tag}yU"])
                    P.op("vector", lambda e, dmc=dmc: e.tensor_add(yTs[:], yTs[:], yU[:]),
                         reads=[f"{tag}yTs", f"{tag}yU"], writes=[f"{tag}yTs"])
                    for tt in range(NTB):
                        P.op("tensor", lambda e, tt=tt: e.transpose(
                                ptr[:, 0:128],
                                yTs[:, tt * 128:(tt + 1) * 128], self.ident_f[:]),
                             reads=[f"{tag}yTs", "ident_f"], writes=[f"{tag}ptr"])
                        P.op("vector", lambda e, tt=tt, dmc=dmc: e.tensor_copy(
                                ytok[:, tt * D + dmc * 128: tt * D + dmc * 128 + 128],
                                ptr[:, 0:128]),
                             reads=[f"{tag}ptr"], writes=[f"{tag}ytok{tt}"])
                for tt in range(NTB):
                    tok = t0 + tt * 128
                    P.op("sync", lambda e, tok=tok: e.dma_start(xr[:], x_src_d[tok:tok + 128, :]),
                         writes=[f"{tag}xr"], dma="g")
                    P.op("vector", lambda e, tt=tt: e.scalar_tensor_tensor(
                            xr[:], ytok[:, tt * D:(tt + 1) * D], 0.5, xr[:],
                            ALU.mult, ALU.add),
                         reads=[f"{tag}ytok{tt}", f"{tag}xr"], writes=[f"{tag}xr"])
                    P.op("gpsimd", lambda e, tok=tok: e.dma_start(x_dst_d[tok:tok + 128, :], xr[:]),
                         reads=[f"{tag}xr"], writes=["x_dst" + tag], dma="g")
            P.retire_all()
            P.flush(self.block)


    def stage_proj(self):
        P, nc, cfg = self.P, self.nc, self.cfg
        T = cfg.t_full
        TB = min(512, cfg.t_loc)
        NBLK = T // TB
        NBLK_OWN = cfg.t_loc // TB
        with ExitStack() as stk:
            en = stk.enter_context
            wsb = {}
            for w in ("Wq", "Wk", "Wv", "Wg"):
                wsb[w] = en(nc.sbuf_tensor(f"wsb_{w}", [128, 8 * D], BF16))
            stg = en(nc.sbuf_tensor("pj_stg", [128, D], F32))
            bgsb = en(nc.sbuf_tensor("pj_bg", [128, 8], F32))
            xf = en(nc.sbuf_tensor("pj_xf", [128, D], F32))
            stat = en(nc.sbuf_tensor("pj_stat", [128, 8], F32))
            lnxb = en(nc.sbuf_tensor("pj_lnx", [128, D], BF16))
            lnxT = en(nc.sbuf_tensor("pj_lnxT", [128, 8 * TB], BF16))
            sq = en(nc.sbuf_tensor("pj_sq", [128, 512], BF16))
            qraw = en(nc.sbuf_tensor("pj_qraw", [128, 512], F32))
            nrm = en(nc.sbuf_tensor("pj_nrm", [2, 1024], F32))
            onesh_f = en(nc.sbuf_tensor("pj_onesh_f", [2, 128], F32))
            outb = en(nc.sbuf_tensor("pj_outb", [128, 512], BF16))
            mk = en(nc.sbuf_tensor("pj_mk", [64, 64], F32))
            mkb = en(nc.sbuf_tensor("pj_mkb", [64, 64], BF16))
            mkn = en(nc.sbuf_tensor("pj_mkn", [64, 1], F32))
            zpad = en(nc.sbuf_tensor("pj_zpad", [128, 1024], BF16))
            ppj = en(nc.psum_tensor("ppjA", [128, 512], F32))
            ppj2 = en(nc.psum_tensor("ppjB", [128, 512], F32))
            pxT = en(nc.psum_tensor("pj_pxT", [128, 1024], BF16))
            pnrm = en(nc.psum_tensor("pj_pnrm", [2, 512], F32))
            pnb = en(nc.psum_tensor("pj_pnb", [128, 512], F32))
            pmk = en(nc.psum_tensor("pj_pmk", [64, 1024], BF16))

            oshT = en(nc.sbuf_tensor("pj_oshT", [128, 2], F32))
            P.op("vector", lambda e: e.memset(oshT[:], 0.0), writes=["oshT"])
            P.op("vector", lambda e: e.memset(oshT[0:64, 0:1], 1.0), writes=["oshT"])
            P.op("vector", lambda e: e.memset(oshT[64:128, 1:2], 1.0), writes=["oshT"])
            P.op("tensor", lambda e: e.transpose(pnrm[0:2, 0:128], oshT[:], self.ident_f[:]),
                 reads=["oshT", "ident_f"], writes=["pj_pnrm"])
            P.op("vector", lambda e: e.tensor_copy(onesh_f[:], pnrm[0:2, 0:128]),
                 reads=["pj_pnrm"], writes=["onesh_f"])
            P.op("vector", lambda e: e.memset(zpad[:], 0.0), writes=["zpad"])
            P.op("vector", lambda e: e.memset(self.cpool[:, 0:1], float(ABS_EPS)), writes=["cpool"])
            self.load_vec_T(self.bg_in, D, bgsb, "pj_bg", psum_ap=pnb, alloc=lambda n_, s_, d_: en(nc.sbuf_tensor(n_, s_, d_)))
            for w in ("Wq", "Wk", "Wv", "Wg"):
                for dmc in range(8):
                    P.op("sync", lambda e, w=w, dmc=dmc: e.dma_start(
                            stg[:], self.W_in[w][dmc * 128:(dmc + 1) * 128, :]),
                         writes=["pj_stg"], dma="g")
                    P.op("vector", lambda e, w=w, dmc=dmc: e.tensor_copy(
                            wsb[w][:, dmc * D:(dmc + 1) * D], stg[:]),
                         reads=["pj_stg"], writes=[f"wsb_{w}"])

            P.op("sync", lambda e: e.dma_start(mk[:], self.memk_in.rearrange("g s d -> (g s) d")),
                 writes=["pj_mk"], dma="g")
            P.op("scalar", lambda e: e.activation(mkb[:], mk[:], AF.Square,
                                                  accum_out=mkn[:]),
                 reads=["pj_mk"], writes=["pj_mkn", "pj_mkb"])
            P.op("scalar", lambda e: e.activation(mkn[:], mkn[:], AF.Ln, bias=self.cpool[0:64, 0:1],
                                                  scale=1.0),
                 reads=["pj_mkn", "cpool"], writes=["pj_mkn"])
            P.op("scalar", lambda e: e.activation(mkn[:], mkn[:], AF.Exp, scale=-0.5),
                 reads=["pj_mkn"], writes=["pj_mkn"])
            P.op("vector", lambda e: e.tensor_scalar(mkb[:], mk[:], mkn[:], None, ALU.mult),
                 reads=["pj_mk", "pj_mkn"], writes=["pj_mkb"])
            P.op("tensor", lambda e: e.transpose(pmk[:, 0:64], mkb[:], self.ident[0:64, 0:64]),
                 reads=["pj_mkb", "ident"], writes=["pj_pmk"])
            P.op("vector", lambda e: e.tensor_copy(mkb[:], pmk[:, 0:64]), reads=["pj_pmk"], writes=["pj_mkb"])
            P.op("gpsimd", lambda e: e.dma_start(
                    self.kT_d.rearrange("(g d) c -> d g c", d=DH)[:, :, 0:NMEM],
                    mkb[:].rearrange("d (g s) -> d g s", s=NMEM)),
                 reads=["pj_mkb"], writes=["kT_d"], dma="g")
            P.op("gpsimd", lambda e: e.dma_start(
                    self.v_d[0:NMEM, :].rearrange("s (g d) -> s g d", d=DH),
                    self.memv_in.rearrange("g s d -> s g d")),
                 writes=["v_d"], dma="g")
            for c in range(8):
                P.op("gpsimd", lambda e, c=c: e.dma_start(
                        self.kT_d[c * 128:(c + 1) * 128, NMEM:128], zpad[:, 0:128 - NMEM]),
                     reads=["zpad"], writes=["kT_d"], dma="g")
            P.op("gpsimd", lambda e: e.dma_start(self.v_d[NMEM:128, :], zpad[0:128 - NMEM, :]),
                 reads=["zpad"], writes=["v_d"], dma="g")

            def norm_head_slice(hds, src_key, out_dram, qscale):
                P.op("scalar", lambda e: e.activation(sq[:, 0:TB], ppj[:, 0:TB], AF.Square),
                     reads=["ppjA"], writes=["pj_sq"])
                P.op("scalar", lambda e: e.activation(qraw[:, 0:TB], ppj[:, 0:TB], AF.Identity),
                     reads=["ppjA"], writes=["pj_qraw"])
                P.op("tensor", lambda e: e.matmul(pnrm[0:2, 0:TB], self.onesh_b[:], sq[:, 0:TB],
                                                  start=True, stop=True),
                     reads=["onesh_b", "pj_sq"], writes=["pj_pnrm"])
                P.op("vector", lambda e: e.tensor_scalar(nrm[:, 0:TB], pnrm[0:2, 0:TB], 1e-24, None,
                                                         ALU.max),
                     reads=["pj_pnrm"], writes=["pj_nrm"])
                P.op("scalar", lambda e: e.activation(nrm[:, 0:TB], nrm[:, 0:TB], AF.Ln),
                     reads=["pj_nrm"], writes=["pj_nrm"])
                if qscale != 1.0:
                    P.op("scalar", lambda e: e.activation(
                            nrm[:, 0:TB], nrm[:, 0:TB], AF.Exp, scale=-0.5,
                            bias=self.cpool[0:2, 2:3]),
                         reads=["pj_nrm", "cpool"], writes=["pj_nrm"])
                else:
                    P.op("scalar", lambda e: e.activation(
                            nrm[:, 0:TB], nrm[:, 0:TB], AF.Exp, scale=-0.5),
                         reads=["pj_nrm"], writes=["pj_nrm"])
                P.op("tensor", lambda e: e.matmul(pnb[:, 0:TB], onesh_f[:], nrm[:, 0:TB],
                                                  start=True, stop=True),
                     reads=["onesh_f", "pj_nrm"], writes=["pj_pnb"])
                P.op("vector", lambda e: e.tensor_mul(outb[:, 0:TB], qraw[:, 0:TB], pnb[:, 0:TB]),
                     reads=["pj_qraw", "pj_pnb"], writes=["pj_outb"])
                P.op("gpsimd", lambda e: e.dma_start(out_dram, outb[:, 0:TB]),
                     reads=["pj_outb"], writes=[src_key + "_d"], dma="g")

            for blk in range(NBLK):
                t0 = blk * TB
                for tt in range(TB // 128):
                    tok = t0 + tt * 128
                    self.ln_tile(self.x1_d[tok:tok + 128, :], xf, slice(0, D), stat, 0,
                                 ("pj_xf", "pj_stat", "pj_lnxb"), lnxb, slice(0, D))
                    for dmc in range(8):
                        P.op("tensor", lambda e, dmc=dmc: e.transpose(
                                pxT[:, 0:128],
                                lnxb[:, dmc * 128:dmc * 128 + 128], self.ident[:]),
                             reads=["pj_lnxb", "ident"], writes=["pj_pxT"])
                        P.op("vector", lambda e, tt=tt, dmc=dmc: e.tensor_copy(
                                lnxT[:, dmc * TB + tt * 128: dmc * TB + (tt + 1) * 128],
                                pxT[:, 0:128]),
                             reads=["pj_pxT"], writes=["pj_lnxT"])

                for w, out_d, qs in (("Wq", self.qnT_d, QK_SCALE), ("Wk", self.knT_own_d, 1.0)):
                    if w == "Wq" and blk >= NBLK_OWN:
                        continue
                    for hds in range(8):
                        for dmc in range(8):
                            P.op("tensor", lambda e, w=w, hds=hds, dmc=dmc: e.matmul(
                                    ppj[:, 0:TB], wsb[w][:, dmc * D + hds * 128: dmc * D + (hds + 1) * 128],
                                    lnxT[:, dmc * TB:(dmc + 1) * TB],
                                    start=(dmc == 0), stop=(dmc == 7)),
                                 reads=[f"wsb_{w}", "pj_lnxT"], writes=["ppjA"])
                        norm_head_slice(hds, w, out_d[hds * 128:(hds + 1) * 128, t0:t0 + TB], qs)
                for tt in range(TB // 128):
                    tok = t0 + tt * 128
                    for vh in range(2):
                        for dmc in range(8):
                            P.op("tensor", lambda e, tt=tt, dmc=dmc, vh=vh: e.matmul(
                                    ppj2[:], lnxT[:, dmc * TB + tt * 128: dmc * TB + (tt + 1) * 128],
                                    wsb["Wv"][:, dmc * D + vh * 512: dmc * D + vh * 512 + 512],
                                    start=(dmc == 0), stop=(dmc == 7)),
                                 reads=["wsb_Wv", "pj_lnxT"], writes=["ppjB"])
                        P.op("vector", lambda e, vh=vh: e.tensor_copy(outb[:, 0:512], ppj2[:]),
                             reads=["ppjB"], writes=["pj_outb"])
                        P.op("gpsimd", lambda e, tok=tok, vh=vh: e.dma_start(
                                self.v_own_d[tok:tok + 128, vh * 512:vh * 512 + 512], outb[:, 0:512]),
                             reads=["pj_outb"], writes=["v_own_d"], dma="g")
                for hds in range(8):
                    if blk >= NBLK_OWN:
                        break
                    for dmc in range(8):
                        P.op("tensor", lambda e, hds=hds, dmc=dmc: e.matmul(
                                ppj[:, 0:TB], wsb["Wg"][:, dmc * D + hds * 128: dmc * D + (hds + 1) * 128],
                                lnxT[:, dmc * TB:(dmc + 1) * TB],
                                start=(dmc == 0), stop=(dmc == 7)),
                             reads=["wsb_Wg", "pj_lnxT"], writes=["ppjA"])
                    P.op("scalar", lambda e, hds=hds: e.activation(
                            outb[:, 0:TB], ppj[:, 0:TB], AF.Sigmoid, bias=bgsb[:, hds:hds + 1]),
                         reads=["ppjA", "pj_bg"], writes=["pj_outb"])
                    P.op("gpsimd", lambda e, hds=hds, t0=t0: e.dma_start(
                            self.sgT_d[hds * 128:(hds + 1) * 128, t0:t0 + TB], outb[:, 0:TB]),
                         reads=["pj_outb"], writes=["sgT_d"], dma="g")
            P.retire_all()
            P.flush(self.block)

    def stage_kv_assemble(self):
        P, nc, cfg = self.P, self.nc, self.cfg
        for m in range(cfg.seq // 128):
            lc = cfg.pos_local_chunk(m)
            P.op("gpsimd", lambda e, m=m, lc=lc: e.dma_start(
                    self.kT_d[:, 128 + m * 128:128 + (m + 1) * 128],
                    self.knT_own_d[:, lc * 128:(lc + 1) * 128]),
                 reads=["Wk_d"], writes=["kT_d"], dma="g")
            P.op("gpsimd", lambda e, m=m, lc=lc: e.dma_start(
                    self.v_d[128 + m * 128:128 + (m + 1) * 128, :],
                    self.v_own_d[lc * 128:(lc + 1) * 128, :]),
                 reads=["v_own_d"], writes=["v_d"], dma="g")

    def stage_attn(self, asub="full"):
        assert asub == "full"
        P, nc, cfg = self.P, self.nc, self.cfg
        T = cfg.t_loc
        JCOL = cfg.jcols
        NG = T // 128
        NW = 2 * NG
        sched = cfg.sched()
        with ExitStack() as stk:
            en = stk.enter_context
            db = en(nc.sbuf_tensor("db_all", [128, 16 * JCOL], BF16))
            dbv = db[:].rearrange("p (b j) -> p b j", b=16)
            sc0 = en(nc.sbuf_tensor("sc0", [128, JCOL], BF16))
            esb = en(nc.sbuf_tensor("esb", [128, JCOL], BF16))
            z_sb = en(nc.sbuf_tensor("z0", [128, 1024], BF16))
            qn_sb = [en(nc.sbuf_tensor(f"qn{i}", [64, 1024], BF16)) for i in range(2)]
            kc_sb = [en(nc.sbuf_tensor(f"kc{i}", [64, 2048], BF16)) for i in range(2)]
            vc_sb = [en(nc.sbuf_tensor(f"vc{i}", [128, 1024], BF16)) for i in range(2)]
            mr_sb = [en(nc.sbuf_tensor(f"mr{i}", [128, 1024], BF16)) for i in range(2)]
            pt2_sb = en(nc.sbuf_tensor("pt20", [128, 1024], BF16))
            m8t = en(nc.sbuf_tensor("m8t", [128, 8], BF16))
            vks = en(nc.sbuf_tensor("vks", [128, 64], F32))
            wscl = [en(nc.sbuf_tensor(f"wscl{b}", [128, 128], BF16)) for b in range(16)]
            wpost2 = en(nc.sbuf_tensor("wpost2", [128, 128], BF16))
            out2sb = en(nc.sbuf_tensor("out2sb", [128, 512], F32))
            pev = en(nc.sbuf_tensor("pev", [128, 512], F32))
            sg_sb = en(nc.sbuf_tensor("sg_sb", [128, 512], BF16))
            wo_sb = en(nc.sbuf_tensor("wo_sb", [128, 8 * D], BF16))
            yTs = en(nc.sbuf_tensor("at_yTs", [128, 64], F32))
            xr = en(nc.sbuf_tensor("at_xr", [128, D], F32))
            shortsel = en(nc.sbuf_tensor("shortsel", [128, 16], mybir.dt.uint32))
            ivec_sb = en(nc.sbuf_tensor("ivec_sb", [128, NG], F32))
            negbase_sb = en(nc.sbuf_tensor("negbase_sb", [128, cfg.n_jc], F32))
            colio_sb = en(nc.sbuf_tensor("colio_sb", [128, 128], F32))
            ivs = en(nc.sbuf_tensor("ivs", [128, 2], F32))
            msk = [en(nc.sbuf_tensor(f"msk{i}", [128, 128], BF16)) for i in range(2)]
            pbig = en(nc.psum_tensor("pbig", [128, 512], F32))
            pgat = [en(nc.psum_tensor(f"pgat{i}", [128, 1024], BF16)) for i in range(2)]
            ppre = en(nc.psum_tensor("ppre", [128, 512], F32))
            ppost = en(nc.psum_tensor("ppost", [128, 512], F32))
            pout2 = en(nc.psum_tensor("pout2", [128, 512], F32))
            pwo = en(nc.psum_tensor("pwo", [128, 512], F32))
            ptw = en(nc.psum_tensor("ptw", [128, 512], F32))

            P.op("sync", lambda e: e.dma_start(shortsel[:], self.shortsel_in),
                 writes=["shortsel"], dma="g")
            P.op("sync", lambda e: e.dma_start(ivec_sb[:], self.ivec_in),
                 writes=["ivec_sb"], dma="g")
            P.op("sync", lambda e: e.dma_start(negbase_sb[:], self.negbase_in),
                 writes=["negbase_sb"], dma="g")
            P.op("sync", lambda e: e.dma_start(colio_sb[:], self.colio_in),
                 writes=["colio_sb"], dma="g")
            P.op("vector", lambda e: e.tensor_copy(
                    wpost2[:].rearrange("p (h i) -> p h i", i=8),
                    self.wpost_t[:].rearrange("p (i h) -> p h i", h=16)),
                 reads=["wpost_t"], writes=["wpost2"])
            for hdc in range(8):
                P.op("sync", lambda e, hdc=hdc: e.dma_start(
                        xr[:], self.W_in["Wo"][hdc * 128:(hdc + 1) * 128, :]),
                     writes=["at_xr"], dma="g")
                P.op("vector", lambda e, hdc=hdc: e.tensor_copy(
                        wo_sb[:, hdc * D:(hdc + 1) * D], xr[:]),
                     reads=["at_xr"], writes=["wo_sb"])

            kT_v = self.kT_d.rearrange("(g p) j -> p g j", p=64)
            qnT_v = self.qnT_d.rearrange("(g p) t -> p g t", p=64)
            sgT_v = self.sgT_d.rearrange("(c q) t -> q c t", q=128)

            def wave_info(w):
                grp, half = divmod(w, 2)
                ntc, mpos = sched[grp]
                return grp, half, list(range(ntc + 1)), mpos

            def head_load(w):
                grp, half, jcs, _ = wave_info(w)
                tok0 = grp * 128 + half * 64
                sl = w % 2
                P.op("sync", lambda e, sl=sl, tok0=tok0: e.dma_start(
                        qn_sb[sl][:].rearrange("p (g t) -> p g t", g=16),
                        qnT_v[:, :, tok0:tok0 + 64]),
                     reads=["Wq_d"], writes=[f"qn{sl}"], dma="g")
                P.op("sync", lambda e: e.dma_start(
                        kc_sb[0][:].rearrange("p (g j) -> p g j", g=16),
                        kT_v[:, :, 0:128]),
                     reads=["kT_d"], writes=["kc0"], dma="g")

            nmsk = [0]
            pend_masks = {}

            def emit_mask(w, grp, jc, bg):
                off = float(8 * bg)
                ms = nmsk[0] % 2
                nmsk[0] += 1
                P.op("vector", lambda e, grp=grp, off=off, jc=jc: e.tensor_scalar(
                        ivs[:, 0:1], ivec_sb[:, grp:grp + 1],
                        negbase_sb[:, jc:jc + 1], off, ALU.add, op1=ALU.add),
                     reads=["ivec_sb", "negbase_sb"], writes=["ivs"])
                P.op("vector", lambda e, ms=ms: e.tensor_scalar(
                        msk[ms][:], colio_sb[:], ivs[:, 0:1], NEG_BIG,
                        ALU.is_gt, op1=ALU.mult),
                     reads=["colio_sb", "ivs"], writes=[f"msk{ms}"])
                P.op("gpsimd", lambda e, bg=bg, jc=jc, ms=ms: e.tensor_tensor(
                        dbv[:, bg:bg + 1, jc * 128:(jc + 1) * 128].rearrange("p b j -> p (b j)"),
                        dbv[:, bg:bg + 1, jc * 128:(jc + 1) * 128].rearrange("p b j -> p (b j)"),
                        msk[ms][:], ALU.add),
                     reads=[f"msk{ms}", f"dbq{bg // 4}"], writes=[f"dbq{bg // 4}"])

            def a_ops(w, jc):
                grp, half, jcs, mpos = wave_info(w)
                sl = w % 2
                qn, qk = qn_sb[sl], f"qn{sl}"
                kb, kk = kc_sb[jc % 2], f"kc{jc % 2}"
                if jc + 1 <= jcs[-1]:
                    nsl = (jc + 1) % 2
                    P.op("sync", lambda e, jc=jc, nsl=nsl: e.dma_start(
                            kc_sb[nsl][:].rearrange("p (g j) -> p g j", g=16),
                            kT_v[:, :, (jc + 1) * 128:(jc + 2) * 128]),
                         reads=["kT_d"], writes=[f"kc{nsl}"], dma="g")
                for f in range(2):
                    for gl in range(8):
                        g = f * 8 + gl
                        P.op("tensor", lambda e, g=g, gl=gl, kb=kb, qn=qn: e.matmul(
                                pbig[:, gl * 64:(gl + 1) * 64],
                                kb[0:64, g * 128:(g + 1) * 128],
                                qn[0:64, g * 64:(g + 1) * 64],
                                start=True, stop=True),
                             reads=[kk, qk], writes=["pbig"])
                    P.op("scalar", lambda e, f=f: e.activation(
                            z_sb[:].rearrange("p (i g) -> p g i", g=16)[:, f * 8:(f + 1) * 8, :],
                            pbig[:].rearrange("p (g i) -> p g i", i=64),
                            AF.Identity),
                         reads=["pbig"], writes=["z0"])
                psl = jc % 2
                for b in range(8):
                    P.op("tensor", lambda e, b=b, psl=psl: e.transpose(
                            pgat[psl][:, b * 128:(b + 1) * 128],
                            z_sb[:, b * 128:(b + 1) * 128], self.ident[:]),
                         reads=["z0", "ident"], writes=[f"pgat{psl}"])
                P.op("scalar", lambda e, psl=psl: e.activation(
                        mr_sb[psl][:], pgat[psl][:], AF.Identity),
                     reads=[f"pgat{psl}"], writes=[f"mr{psl}"])
                for bq in range(2):
                    for bl in range(4):
                        b = bq * 4 + bl
                        P.op("tensor", lambda e, b=b, bl=bl, psl=psl: e.matmul(
                                ppre[:, bl * 128:(bl + 1) * 128], self.wpre_t[:],
                                mr_sb[psl][:, b * 128:(b + 1) * 128],
                                start=True, stop=True),
                             reads=["wpre_t", f"mr{psl}"], writes=["ppre"])
                    bg0 = half * 8 + bq * 4
                    P.op("scalar", lambda e, bg0=bg0, jc=jc: e.activation(
                            dbv[:, bg0:bg0 + 4, jc * 128:(jc + 1) * 128],
                            ppre[:].rearrange("p (b j) -> p b j", b=4),
                            AF.Identity),
                         reads=["ppre"], writes=[f"dbq{bg0 // 4}"])
                    if jc == 0:
                        for bl in range(4):
                            bg = bg0 + bl
                            P.op("gpsimd", lambda e, bg=bg: e.tensor_tensor(
                                    dbv[:, bg:bg + 1, 0:128].rearrange("p b j -> p (b j)"),
                                    dbv[:, bg:bg + 1, 0:128].rearrange("p b j -> p (b j)"),
                                    self.dmask[:], ALU.add),
                                 reads=[f"dbq{bg // 4}", "dmask"], writes=[f"dbq{bg // 4}"])
                    if jc in mpos:
                        for bl in range(4):
                            bg = bg0 + bl
                            if w <= 1:
                                emit_mask(w, grp, jc, bg)
                            else:
                                pend_masks.setdefault(w, []).append((grp, jc, bg))

            def tail_ops(w, b, valid):
                grp, half = divmod(w, 2)
                bg = half * 8 + b
                P.op("scalar", lambda e, bg=bg, valid=valid: e.activation(
                        esb[:, 0:valid], dbv[:, bg, 0:valid], AF.Exp,
                        bias=vks[:, 16 + bg:17 + bg]),
                     reads=[f"dbq{bg // 4}", f"vkn{bg}"], writes=["esb"])
                P.op("vector", lambda e, bg=bg, valid=valid: e.scalar_tensor_tensor(
                        dbv[:, bg, 0:valid], dbv[:, bg, 0:valid], vks[:, bg:bg + 1],
                        esb[:, 0:valid], ALU.is_ge, ALU.mult,
                        accum_out=vks[:, 32 + bg:33 + bg]),
                     reads=[f"dbq{bg // 4}", f"vkr{bg}", "esb"],
                     writes=[f"dbq{bg // 4}", f"vkd{bg}"])
                P.op("vector", lambda e, bg=bg: e.reciprocal(vks[:, 48 + bg:49 + bg],
                                                             vks[:, 32 + bg:33 + bg]),
                     reads=[f"vkd{bg}"], writes=[f"vkc{bg}"])
                P.op("vector", lambda e, bg=bg: e.tensor_scalar(
                        wscl[bg][:], wpost2[:], vks[:, 48 + bg:49 + bg], None, ALU.mult),
                     reads=["wpost2", f"vkc{bg}"], writes=[f"wscl{bg}"])

            def tk(w):
                grp, half, jcs, _ = wave_info(w)
                valid = len(jcs) * 128
                pend = None
                for b in range(8):
                    bg = half * 8 + b
                    dsl = dbv[:, bg, 0:valid]
                    for r in range(8):
                        P.op("vector", lambda e, dsl=dsl, r=r, valid=valid: e.max(
                                m8t[:], dsl if r == 0 else sc0[:, 0:valid]),
                             reads=[f"dbq{bg // 4}" if r == 0 else "sc0"], writes=["m8t"])
                        if pend is not None and r == 1:
                            tail_ops(w, pend, valid)
                            pend = None
                        if r < 7:
                            P.op("vector", lambda e, dsl=dsl, r=r, valid=valid: e.match_replace(
                                    sc0[:, 0:valid], m8t[:],
                                    dsl if r == 0 else sc0[:, 0:valid], NEG_BIG),
                                 reads=["m8t", f"dbq{bg // 4}" if r == 0 else "sc0"],
                                 writes=["sc0"])
                    P.op("vector", lambda e, bg=bg: e.tensor_copy(vks[:, bg:bg + 1], m8t[:, 7:8]),
                         reads=["m8t"], writes=[f"vkr{bg}"])
                    if grp == 0:
                        P.op("vector", lambda e, bg=bg: e.copy_predicated(
                                vks[:, bg:bg + 1], shortsel[:, bg:bg + 1],
                                self.cpool[:, 3:4]),
                             reads=["shortsel", "cpool", f"vkr{bg}"], writes=[f"vkr{bg}"])
                    P.op("vector", lambda e, bg=bg: e.tensor_scalar(
                            vks[:, 16 + bg:17 + bg], vks[:, bg:bg + 1], -1.0, None, ALU.mult),
                         reads=[f"vkr{bg}"], writes=[f"vkn{bg}"])
                    pend = b
                tail_ops(w, pend, valid)

            def b_av(w, jc):
                grp, half, jcs, _ = wave_info(w)
                vb, vkk = vc_sb[jc % 2], f"vc{jc % 2}"
                if jc == 0:
                    P.op("scalar", lambda e: e.dma_start(
                            vc_sb[0][:], self.v_d[0:128, :]),
                         reads=["v_d"], writes=["vc0"], dma="s")
                if jc + 1 <= jcs[-1]:
                    nsl = (jc + 1) % 2
                    P.op("scalar", lambda e, jc=jc, nsl=nsl: e.dma_start(
                            vc_sb[nsl][:], self.v_d[(jc + 1) * 128:(jc + 2) * 128, :]),
                         reads=["v_d"], writes=[f"vc{nsl}"], dma="s")
                for bq in range(2):
                    for bl in range(4):
                        b = bq * 4 + bl
                        bg = half * 8 + b
                        P.op("tensor", lambda e, bg=bg, bl=bl, jc=jc: e.matmul(
                                ppost[:, bl * 128:(bl + 1) * 128],
                                dbv[:, bg, jc * 128:(jc + 1) * 128], wscl[bg][:],
                                start=True, stop=True),
                             reads=[f"dbq{bg // 4}", f"wscl{bg}"], writes=["ppost"])
                    for bl in range(4):
                        b = bq * 4 + bl
                        P.op("scalar", lambda e, b=b, bl=bl: e.activation(
                                pt2_sb[:].rearrange("p (h bi) -> p h bi", bi=64)[:, :, b * 8:(b + 1) * 8],
                                ppost[:, bl * 128:(bl + 1) * 128].rearrange(
                                    "p (h i) -> p h i", i=8),
                                AF.Identity),
                             reads=["ppost"], writes=["pt20"])
                for h in range(16):
                    ph = (h % 2) * 64
                    P.op("tensor", lambda e, h=h, ph=ph, vb=vb: e.matmul(
                            pout2[ph:ph + 64, (h // 2) * 64:(h // 2) * 64 + 64],
                            vb[:, h * 64:(h + 1) * 64],
                            pt2_sb[:, h * 64:(h + 1) * 64],
                            start=True, stop=True,
                            tile_position=(0, ph)),
                         reads=[vkk, "pt20"], writes=["pout2"])
                P.op("scalar", lambda e: e.activation(pev[:], pout2[:], AF.Identity),
                     reads=["pout2"], writes=["pev"])
                if jc == 0:
                    P.op("gpsimd", lambda e: e.tensor_copy(out2sb[:], pev[:]),
                         reads=["pev"], writes=["out2sb"])
                else:
                    P.op("gpsimd", lambda e: e.tensor_add(out2sb[:], out2sb[:], pev[:]),
                         reads=["pev", "out2sb"], writes=["out2sb"])

            def b_out(w):
                grp, half, jcs, _ = wave_info(w)
                tok0 = grp * 128 + half * 64
                P.op("sync", lambda e, tok0=tok0: e.dma_start(
                        sg_sb[:].rearrange("q (c i) -> q c i", c=8),
                        sgT_v[:, :, tok0:tok0 + 64]),
                     reads=["sgT_d"], writes=["sg_sb"], dma="g")
                P.op("gpsimd", lambda e: e.tensor_mul(sg_sb[:], out2sb[:], sg_sb[:]),
                     reads=["out2sb", "sg_sb"], writes=["sg_sb"])
                P.op("sync", lambda e, tok0=tok0: e.dma_start(
                        xr[0:64, :], self.x1_d[tok0:tok0 + 64, :]),
                     reads=["x_dstf1"], writes=["at_xr"], dma="g")
                for g4 in range(2):
                    for dq in range(4):
                        dms = g4 * 4 + dq
                        for hdc in range(8):
                            P.op("tensor", lambda e, dms=dms, hdc=hdc: e.matmul(
                                    pwo[:, dms * 64:(dms + 1) * 64],
                                    wo_sb[:, hdc * D + dms * 128: hdc * D + (dms + 1) * 128],
                                    sg_sb[:, hdc * 64:(hdc + 1) * 64],
                                    start=(hdc == 0), stop=(hdc == 7)),
                                 reads=["wo_sb", "sg_sb"], writes=["pwo"])
                        P.op("scalar", lambda e, dms=dms: e.activation(
                                yTs[:], pwo[:, dms * 64:(dms + 1) * 64], AF.Identity),
                             reads=["pwo"], writes=["at_yTs"])
                        P.op("tensor", lambda e, dq=dq: e.transpose(
                                ptw[0:64, dq * 128:(dq + 1) * 128], yTs[:], self.ident_f[:]),
                             reads=["at_yTs", "ident_f"], writes=["ptw"])
                    P.op("vector", lambda e, g4=g4: e.tensor_add(
                            xr[0:64, g4 * 512:(g4 + 1) * 512],
                            xr[0:64, g4 * 512:(g4 + 1) * 512], ptw[0:64, 0:512]),
                         reads=["ptw", "at_xr"], writes=["at_xr"])
                P.op("gpsimd", lambda e, tok0=tok0: e.dma_start(
                        self.x2_d[tok0:tok0 + 64, :], xr[0:64, :]),
                     reads=["at_xr"], writes=["x2_d"], dma="g")

            head_load(0)
            for jc in wave_info(0)[2]:
                a_ops(0, jc)
            head_load(1)
            for jc in wave_info(1)[2]:
                a_ops(1, jc)
            for w in range(NW):
                tk(w)
                for (grp_, jc_, bg_) in pend_masks.pop(w + 1, []):
                    emit_mask(w + 1, grp_, jc_, bg_)
                if w - 1 >= 0:
                    b_out(w - 1)
                jB = wave_info(w)[2]
                jA = wave_info(w + 2)[2] if w + 2 < NW else []
                if jA:
                    head_load(w + 2)
                for jc in range(max(len(jA), len(jB))):
                    if jc < len(jB):
                        b_av(w, jc)
                    if jc < len(jA):
                        a_ops(w + 2, jc)
            b_out(NW - 1)
            P.retire_all()
            P.flush(self.block)

    def stage_attn_v1(self, asub="full"):
        P, nc, cfg = self.P, self.nc, self.cfg
        self.asub = asub
        T = cfg.t_loc
        JCOL = cfg.jcols
        NG = T // 128
        with ExitStack() as stk:
            en = stk.enter_context
            db = [en(nc.sbuf_tensor(f"db{b}", [128, JCOL], BF16)) for b in range(16)]
            esb = en(nc.sbuf_tensor("esb", [128, JCOL], BF16))
            sc0 = en(nc.sbuf_tensor("sc0", [128, JCOL], BF16))
            sc1 = esb
            qn_sb = en(nc.sbuf_tensor("qn_sb", [64, 16 * 128], BF16))
            kc_sb = [en(nc.sbuf_tensor(f"kc{i}", [64, 16 * 128], BF16)) for i in range(2)]
            vc_sb = [en(nc.sbuf_tensor("vc0", [128, 1024], BF16))] * 2
            z_sb = en(nc.sbuf_tensor("z_sb", [128, 2048], BF16))
            mr_sb = [en(nc.sbuf_tensor(f"mr{i}", [128, 128], BF16)) for i in range(2)]
            pt2_sb = [en(nc.sbuf_tensor("pt20", [128, 2048], BF16))] * 2
            m8t = en(nc.sbuf_tensor("m8t", [128, 8], BF16))
            vks = en(nc.sbuf_tensor("vks", [128, 64], F32))
            wscl = [en(nc.sbuf_tensor(f"wscl{b}", [128, 128], BF16)) for b in range(16)]
            wpost2 = en(nc.sbuf_tensor("wpost2", [128, 128], BF16))
            out2sb = en(nc.sbuf_tensor("out2sb", [128, 1024], F32))
            sg_sb = en(nc.sbuf_tensor("sg_sb", [128, 1024], BF16))
            gt_sb = sg_sb
            wo_sb = en(nc.sbuf_tensor("wo_sb", [128, 8 * D], BF16))
            yTs = en(nc.sbuf_tensor("at_yTs", [128, 128], F32))
            ytok = out2sb
            xr = en(nc.sbuf_tensor("at_xr", [128, D], F32))
            shortsel = en(nc.sbuf_tensor("shortsel", [128, 16], mybir.dt.uint32))
            ivec_sb = en(nc.sbuf_tensor("ivec_sb", [128, NG], F32))
            negbase_sb = en(nc.sbuf_tensor("negbase_sb", [128, cfg.n_jc], F32))
            colio_sb = en(nc.sbuf_tensor("colio_sb", [128, 128], F32))
            ivs = en(nc.sbuf_tensor("ivs", [128, 2], F32))
            msk = en(nc.sbuf_tensor("msk", [128, 128], BF16))
            pbig = en(nc.psum_tensor("pbig", [128, 512], F32))
            pgat = en(nc.psum_tensor("pgat", [128, 1024], BF16))
            ppre = en(nc.psum_tensor("ppre", [128, 512], F32))
            ppost = en(nc.psum_tensor("ppost", [128, 512], F32))
            pout2 = en(nc.psum_tensor("pout2", [128, 1024], F32))
            pwo = en(nc.psum_tensor("pwo", [128, 512], F32))
            ptw = en(nc.psum_tensor("ptw", [128, 512], F32))

            P.op("sync", lambda e: e.dma_start(shortsel[:], self.shortsel_in),
                 writes=["shortsel"], dma="g")
            P.op("sync", lambda e: e.dma_start(ivec_sb[:], self.ivec_in),
                 writes=["ivec_sb"], dma="g")
            P.op("sync", lambda e: e.dma_start(negbase_sb[:], self.negbase_in),
                 writes=["negbase_sb"], dma="g")
            P.op("sync", lambda e: e.dma_start(colio_sb[:], self.colio_in),
                 writes=["colio_sb"], dma="g")
            P.op("vector", lambda e: e.tensor_copy(
                    wpost2[:].rearrange("p (h i) -> p h i", i=8),
                    self.wpre_t[:].rearrange("p (i h) -> p h i", h=16) if False else
                    self.wpost_t[:].rearrange("p (i h) -> p h i", h=16)),
                 reads=["wpost_t"], writes=["wpost2"])
            for hdc in range(8):
                P.op("sync", lambda e, hdc=hdc: e.dma_start(
                        xr[:], self.W_in["Wo"][hdc * 128:(hdc + 1) * 128, :]),
                     writes=["at_xr"], dma="g")
                P.op("vector", lambda e, hdc=hdc: e.tensor_copy(
                        wo_sb[:, hdc * D:(hdc + 1) * D], xr[:]),
                     reads=["at_xr"], writes=["wo_sb"])

            kT_v = self.kT_d.rearrange("(g p) j -> p g j", p=64)
            qnT_v = self.qnT_d.rearrange("(g p) t -> p g t", p=64)

            sched = cfg.sched()
            for grp in range(NG):
                ntc, mpos = sched[grp]
                jcs = list(range(0, ntc + 1))
                valid = (ntc + 1) * 128

                for hds in range(4):
                    P.op("sync", lambda e, hds=hds, grp=grp: e.dma_start(
                            qn_sb[:].rearrange("p (g t) -> p g t", g=16)[:, hds * 4:(hds + 1) * 4, :],
                            qnT_v[:, hds * 4:(hds + 1) * 4, grp * 128:(grp + 1) * 128]),
                         reads=["Wq_d"], writes=["qn_sb"], dma="g")

                for jc in jcs:
                    kb = kc_sb[jc % 2]
                    kk = f"kc{jc % 2}"
                    P.op("sync", lambda e, jc=jc, kb=kb: e.dma_start(
                            kb[:].rearrange("p (g j) -> p g j", g=16),
                            kT_v[:, :, jc * 128:(jc + 1) * 128]),
                         reads=["kT_d"], writes=[kk], dma="g")
                    for gq in range(4):
                        for gl in range(4):
                            g = gq * 4 + gl
                            P.op("tensor", lambda e, g=g, gl=gl, kb=kb: e.matmul(
                                    pbig[:, gl * 128:(gl + 1) * 128],
                                    kb[0:64, g * 128:(g + 1) * 128],
                                    qn_sb[0:64, g * 128:(g + 1) * 128],
                                    start=True, stop=True),
                                 reads=[kk, "qn_sb"], writes=["pbig"])
                        P.op("scalar", lambda e, gq=gq: e.activation(
                                z_sb[:].rearrange("p (i g) -> p g i", g=16)[:, gq * 4:(gq + 1) * 4, :],
                                pbig[:], AF.Identity),
                             reads=["pbig"], writes=["z_sb"])
                    for b in range(16):
                        if self.asub == "A1":
                            break
                        sl = b % 2
                        P.op("tensor", lambda e, b=b: e.transpose(
                                pgat[:, 0:128],
                                z_sb[:, b * 128:(b + 1) * 128], self.ident[:]),
                             reads=["z_sb", "ident"], writes=["pgat"])
                        P.op("vector", lambda e, b=b, sl=sl: e.tensor_copy(
                                mr_sb[sl][:], pgat[:, 0:128]),
                             reads=["pgat"], writes=[f"mr{sl}"])
                        if self.asub == "A2":
                            continue
                        P.op("tensor", lambda e, b=b, sl=sl: e.matmul(
                                ppre[:, 0:128], self.wpre_t[:], mr_sb[sl][:],
                                start=True, stop=True),
                             reads=["wpre_t", f"mr{sl}"], writes=["ppre"])
                        if jc == 0:
                            P.op("vector", lambda e: e.tensor_tensor(
                                    ppre[:, 0:128], ppre[:, 0:128],
                                    self.dmask[:], ALU.add),
                                 reads=["ppre", "dmask"], writes=["ppre"])
                        if jc in mpos:
                            off = float(8 * b)
                            P.op("vector", lambda e, grp=grp, off=off, jc=jc: e.tensor_scalar(
                                    ivs[:, 0:1], ivec_sb[:, grp:grp + 1],
                                    negbase_sb[:, jc:jc + 1], off, ALU.add, op1=ALU.add),
                                 reads=["ivec_sb", "negbase_sb"], writes=["ivs"])
                            P.op("vector", lambda e: e.tensor_scalar(
                                    msk[:], colio_sb[:], ivs[:, 0:1], None, ALU.is_gt),
                                 reads=["colio_sb", "ivs"], writes=["msk"])
                            P.op("vector", lambda e: e.scalar_tensor_tensor(
                                    ppre[:, 0:128], msk[:], NEG_BIG,
                                    ppre[:, 0:128], ALU.mult, ALU.add),
                                 reads=["msk", "ppre"], writes=["ppre"])
                        P.op("scalar", lambda e, b=b, jc=jc: e.activation(
                                db[b][:, jc * 128:(jc + 1) * 128],
                                ppre[:, 0:128], AF.Identity),
                             reads=["ppre"], writes=[f"db{b}"])

                if self.asub.startswith("A"):
                    continue
                for b in range(16):
                    dcur, kcur = db[b], f"db{b}"
                    for r in range(8):
                        P.op("vector", lambda e, dcur=dcur, valid=valid: e.max(
                                m8t[:], dcur[:, 0:valid]),
                             reads=[kcur], writes=["m8t"])
                        if r < 7:
                            dnew, knew = (sc0, "sc0") if r % 2 == 0 else (sc1, "esb")
                            P.op("vector", lambda e, dcur=dcur, dnew=dnew, valid=valid:
                                    e.match_replace(dnew[:, 0:valid], m8t[:],
                                                    dcur[:, 0:valid], NEG_BIG),
                                 reads=["m8t", kcur], writes=[knew])
                            dcur, kcur = dnew, knew
                    P.op("vector", lambda e, b=b: e.tensor_copy(vks[:, b:b + 1], m8t[:, 7:8]),
                         reads=["m8t"], writes=["vks"])
                    if grp == 0:
                        P.op("vector", lambda e, b=b: e.copy_predicated(
                                vks[:, b:b + 1], shortsel[:, b:b + 1],
                                self.cpool[:, 3:4]),
                             reads=["shortsel", "cpool", "vks"], writes=["vks"])
                    P.op("vector", lambda e, b=b: e.tensor_scalar(
                            vks[:, 16 + b:17 + b], vks[:, b:b + 1], -1.0, None, ALU.mult),
                         reads=["vks"], writes=["vks"])
                    P.op("scalar", lambda e, b=b, valid=valid: e.activation(
                            esb[:, 0:valid], db[b][:, 0:valid], AF.Exp,
                            bias=vks[:, 16 + b:17 + b]),
                         reads=[f"db{b}", "vks"], writes=["esb"])
                    P.op("vector", lambda e, b=b, valid=valid: e.scalar_tensor_tensor(
                            db[b][:, 0:valid], db[b][:, 0:valid], vks[:, b:b + 1],
                            esb[:, 0:valid], ALU.is_ge, ALU.mult,
                            accum_out=vks[:, 32 + b:33 + b]),
                         reads=[f"db{b}", "vks", "esb"], writes=[f"db{b}", "vks_den"])
                for b in range(16):
                    P.op("vector", lambda e, b=b: e.reciprocal(vks[:, 48 + b:49 + b],
                                                               vks[:, 32 + b:33 + b]),
                         reads=["vks_den"], writes=["vks_r"])
                    P.op("vector", lambda e, b=b: e.tensor_scalar(
                            wscl[b][:], wpost2[:], vks[:, 48 + b:49 + b], None, ALU.mult),
                         reads=["wpost2", "vks_r"], writes=[f"wscl{b}"])

                for jc in jcs:
                    vb = vc_sb[jc % 2]
                    vkk = "vc0"
                    P.op("sync", lambda e, jc=jc, vb=vb: e.dma_start(
                            vb[:], self.v_d[jc * 128:(jc + 1) * 128, :]),
                         reads=["v_d"], writes=[vkk], dma="g")
                    p2 = pt2_sb[jc % 2]
                    p2k = "pt20"
                    for b in range(16):
                        P.op("tensor", lambda e, b=b, jc=jc: e.matmul(
                                ppost[:, 0:128],
                                db[b][:, jc * 128:(jc + 1) * 128], wscl[b][:],
                                start=True, stop=True),
                             reads=[f"db{b}", f"wscl{b}"], writes=["ppost"])
                        P.op("vector", lambda e, b=b, p2=p2: e.tensor_copy(
                                p2[:].rearrange("p (h bi) -> p h bi", bi=128)[:, :, b * 8:(b + 1) * 8],
                                ppost[:, 0:128].rearrange(
                                    "p (h i) -> p h i", i=8)),
                             reads=["ppost"], writes=[p2k])
                    for h in range(16):
                        ph = (h % 2) * 64
                        P.op("tensor", lambda e, h=h, ph=ph, vb=vb, p2=p2: e.matmul(
                                pout2[ph:ph + 64, (h // 2) * 128:(h // 2) * 128 + 128],
                                vb[:, h * 64:(h + 1) * 64],
                                p2[:, h * 128:(h + 1) * 128],
                                start=True, stop=True,
                                tile_position=(0, ph)),
                             reads=[vkk, p2k], writes=["pout2"])
                    if jc == 0:
                        P.op("vector", lambda e: e.tensor_copy(out2sb[:], pout2[:]),
                             reads=["pout2"], writes=["out2sb"])
                    else:
                        P.op("vector", lambda e: e.tensor_add(out2sb[:], out2sb[:], pout2[:]),
                             reads=["pout2", "out2sb"], writes=["out2sb"])

                for hds in range(8):
                    P.op("sync", lambda e, hds=hds, grp=grp: e.dma_start(
                            sg_sb[:, hds * 128:(hds + 1) * 128],
                            self.sgT_d[hds * 128:(hds + 1) * 128, grp * 128:(grp + 1) * 128]),
                         reads=["sgT_d"], writes=["sg_sb"], dma="g")
                P.op("vector", lambda e: e.tensor_mul(gt_sb[:], out2sb[:], sg_sb[:]),
                     reads=["out2sb", "sg_sb"], writes=["sg_sb"])
                for dms in range(8):
                    for hdc in range(8):
                        P.op("tensor", lambda e, dms=dms, hdc=hdc: e.matmul(
                                pwo[:, 0:128],
                                wo_sb[:, hdc * D + dms * 128: hdc * D + (dms + 1) * 128],
                                gt_sb[:, hdc * 128:(hdc + 1) * 128],
                                start=(hdc == 0), stop=(hdc == 7)),
                             reads=["wo_sb", "sg_sb"], writes=["pwo"])
                    P.op("vector", lambda e, dms=dms: e.tensor_copy(yTs[:], pwo[:, 0:128]),
                         reads=["pwo"], writes=["at_yTs"])
                    P.op("tensor", lambda e, dms=dms: e.transpose(
                            ptw[:, 0:128], yTs[:], self.ident_f[:]),
                         reads=["at_yTs", "ident_f"], writes=["ptw"])
                    P.op("vector", lambda e, dms=dms: e.tensor_copy(
                            ytok[:, dms * 128:(dms + 1) * 128], ptw[:, 0:128]),
                         reads=["ptw"], writes=["out2sb"])
                P.op("sync", lambda e, grp=grp: e.dma_start(
                        xr[:], self.x1_d[grp * 128:(grp + 1) * 128, :]),
                     reads=["x_dstf1"], writes=["at_xr"], dma="g")
                P.op("vector", lambda e: e.tensor_add(xr[:], xr[:], ytok[:]),
                     reads=["at_xr", "out2sb"], writes=["at_xr"])
                P.op("gpsimd", lambda e, grp=grp: e.dma_start(
                        self.x2_d[grp * 128:(grp + 1) * 128, :], xr[:]),
                     reads=["at_xr"], writes=["x2_d"], dma="g")
            P.retire_all()
            P.flush(self.block)


    def finish(self):
        deps = self.P.barrier_deps()

        def fin(eng, deps=deps):
            for sem, val in deps:
                eng.wait_ge(sem, val)
        self.P.flush(self.block)
        self.block.vector(fin)
        self.ctx.close()
        return self.nc


def build_x1(cfg: Cfg):
    k = Kern(cfg)
    P, nc = k.P, k.nc
    k.stage_consts()
    with nc.sbuf_tensor("wfcqT_sb", [128, 8 * HID], BF16) as wfcqT_sb, \
         nc.sbuf_tensor("wprojqT_sb", [128, 32 * D], BF16) as wprojqT_sb:
        k.stage_quant(wfcqT_sb, wprojqT_sb)
        k.putil_ctx.close()
        k.stage_ffn(k.x_in, k.x1_d, wfcqT_sb, wprojqT_sb, "f1", n_tok=cfg.t_full)
        P.flush(k.block)
    cp = k.sbuf("cp", [128, D], F32)
    for t in range(cfg.n_ttiles):
        P.op("sync", lambda e, t=t: e.dma_start(cp[:], k.x1_d[t * 128:(t + 1) * 128, :]),
             reads=["x_dstf1"], writes=["cp"], dma="g")
        P.op("gpsimd", lambda e, t=t: e.dma_start(k.y_out[t * 128:(t + 1) * 128, :], cp[:]),
             reads=["cp"], writes=["y_out"], dma="g")
    return k.finish(), k.din

def build_full(cfg: Cfg):
    k = Kern(cfg)
    P, nc = k.P, k.nc
    k.stage_consts()
    with nc.sbuf_tensor("wfcqT_sb", [128, 8 * HID], BF16) as wfcqT_sb, \
         nc.sbuf_tensor("wprojqT_sb", [128, 32 * D], BF16) as wprojqT_sb:
        k.stage_quant(wfcqT_sb, wprojqT_sb)
        k.P.retire_all()
        k.P.flush(k.block)
        k.putil_ctx.close()
        k.stage_ffn(k.x_in, k.x1_d, wfcqT_sb, wprojqT_sb, "f1", n_tok=cfg.t_full)
    k.stage_proj()
    k.stage_kv_assemble()
    k.stage_attn()
    with nc.sbuf_tensor("wfcqT_sb2", [128, 8 * HID], BF16) as w1, \
         nc.sbuf_tensor("wprojqT_sb2", [128, 32 * D], BF16) as w2:
        for dmc in range(8):
            P.op("sync", lambda e, dmc=dmc: e.dma_start(
                    w1[:, dmc * HID:(dmc + 1) * HID], k.wfcqT_d[dmc * 128:(dmc + 1) * 128, :]),
                 reads=["wfcqT_d"], writes=["wfcqT_sb"], dma="g")
        for hc in range(32):
            P.op("sync", lambda e, hc=hc: e.dma_start(
                    w2[:, hc * D:(hc + 1) * D], k.wprojqT_d[hc * 128:(hc + 1) * 128, :]),
                 reads=["wprojqT_d"], writes=["wprojqT_sb"], dma="g")
        k.stage_ffn(k.x2_d, k.y_out, w1, w2, "f2")
    return k.finish(), k.din


def build_quant_dbg(cfg: Cfg, upto="quant", sub="all"):
    k = Kern(cfg)
    P, nc = k.P, k.nc
    k.stage_consts()
    if upto == "consts":
        cp = k.sbuf("cpdbg", [128, D], F32)
        P.op("vector", lambda e: e.memset(cp[:], 1.0), writes=["cpdbg"])
        for t in range(cfg.t_loc // 128):
            P.op("gpsimd", lambda e, t=t: e.dma_start(k.y_out[t * 128:(t + 1) * 128, :], cp[:]),
                 reads=["cpdbg"], writes=["y_out"], dma="g")
        return k.finish(), k.din
    with nc.sbuf_tensor("wfcqT_sb", [128, 8 * HID], BF16) as wfcqT_sb, \
         nc.sbuf_tensor("wprojqT_sb", [128, 32 * D], BF16) as wprojqT_sb:
        k.stage_quant(wfcqT_sb, wprojqT_sb, sub=sub)
        P.flush(k.block)
    if sub != "all":
        cp = k.sbuf("cpdbg", [128, D], F32)
        P.op("vector", lambda e: e.memset(cp[:], 2.0), writes=["cpdbg"])
        for t in range(cfg.t_loc // 128):
            P.op("gpsimd", lambda e, t=t: e.dma_start(k.y_out[t * 128:(t + 1) * 128, :], cp[:]),
                 reads=["cpdbg"], writes=["y_out"], dma="g")
        return k.finish(), k.din
    cp = k.sbuf("cpdbg", [128, D], F32)
    cpb = k.sbuf("cpdbgb", [128, D], BF16)
    for t in range(8):
        P.op("sync", lambda e, t=t: e.dma_start(cpb[:], k.wfcqT_d[t * 128:(t + 1) * 128, 0:D]),
             reads=["wfcqT_d"], writes=["cpdbgb"], dma="g")
        P.op("vector", lambda e, t=t: e.tensor_copy(cp[:], cpb[:]),
             reads=["cpdbgb"], writes=["cpdbg"])
        P.op("gpsimd", lambda e, t=t: e.dma_start(k.y_out[t * 128:(t + 1) * 128, :], cp[:]),
             reads=["cpdbg"], writes=["y_out"], dma="g")
    return k.finish(), k.din


def build_dbg(cfg: Cfg, upto="proj", asub="full"):
    k = Kern(cfg)
    P, nc = k.P, k.nc
    k.stage_consts()
    with nc.sbuf_tensor("wfcqT_sb", [128, 8 * HID], BF16) as wfcqT_sb, \
         nc.sbuf_tensor("wprojqT_sb", [128, 32 * D], BF16) as wprojqT_sb:
        k.stage_quant(wfcqT_sb, wprojqT_sb)
        k.P.retire_all()
        k.P.flush(k.block)
        k.putil_ctx.close()
        k.stage_ffn(k.x_in, k.x1_d, wfcqT_sb, wprojqT_sb, "f1", n_tok=cfg.t_full)
    k.stage_proj()
    if upto == "proj":
        cp = k.sbuf("cpdbg", [128, D], F32)
        cpb = k.sbuf("cpdbgb", [128, D], BF16)
        for t in range(8):
            P.op("sync", lambda e, t=t: e.dma_start(cpb[:], k.qnT_d[t * 128:(t + 1) * 128, 0:D]),
                 reads=["Wq_d"], writes=["cpdbgb"], dma="g")
            P.op("vector", lambda e, t=t: e.tensor_copy(cp[:], cpb[:]),
                 reads=["cpdbgb"], writes=["cpdbg"])
            P.op("gpsimd", lambda e, t=t: e.dma_start(k.y_out[t * 128:(t + 1) * 128, :], cp[:]),
                 reads=["cpdbg"], writes=["y_out"], dma="g")
        return k.finish(), k.din
    k.stage_kv_assemble()
    k.stage_attn_v1(asub=asub)
    if asub != "full":
        cp = k.sbuf("cpdbg", [128, D], F32)
        P.op("vector", lambda e: e.memset(cp[:], 5.0), writes=["cpdbg"])
        for t in range(cfg.t_loc // 128):
            P.op("gpsimd", lambda e, t=t: e.dma_start(k.y_out[t * 128:(t + 1) * 128, :], cp[:]),
                 reads=["cpdbg"], writes=["y_out"], dma="g")
        return k.finish(), k.din
    cp = k.sbuf("cpdbg", [128, D], F32)
    for t in range(cfg.t_loc // 128):
        P.op("sync", lambda e, t=t: e.dma_start(cp[:], k.x2_d[t * 128:(t + 1) * 128, :]),
             reads=["x2_d"], writes=["cpdbg"], dma="g")
        P.op("gpsimd", lambda e, t=t: e.dma_start(k.y_out[t * 128:(t + 1) * 128, :], cp[:]),
             reads=["cpdbg"], writes=["y_out"], dma="g")
    return k.finish(), k.din


from concourse.bass_utils import run_bass_kernel_spmd


def kernel(**inputs):
    SEQ, TLOC, NCORES = 4096, 2048, 8
    base = {
        "Wq": inputs["L0_attn_Wq"], "Wk": inputs["L0_attn_Wk"], "Wv": inputs["L0_attn_Wv"],
        "Wg": inputs["L0_attn_Wg"], "Wo": inputs["L0_attn_Wo"],
        "bg": inputs["L0_attn_bg"],
        "Wpre": inputs["L0_attn_W_pre_th"], "Wpost": inputs["L0_attn_W_post_th"],
        "memk": inputs["L0_attn_mem_k"], "memv": inputs["L0_attn_mem_v"],
        "hs": np.ascontiguousarray(inputs["L0_attn_head_scale_params"]).reshape(16),
        "Wfc": inputs["L0_ffn_W_fc"], "Wproj": inputs["L0_ffn_W_proj"],
        "lnw": inputs["L0_ffn_ln_w"], "lnb": inputs["L0_ffn_ln_b"],
    }
    base = {k: np.ascontiguousarray(v, dtype=np.float32) for k, v in base.items()}
    x = np.ascontiguousarray(inputs["L0_attn_x"], dtype=np.float32)

    in_maps = []
    tokmaps = []
    cfg0 = None
    for c in range(NCORES):
        beta, eta = c // 2, c % 2
        blocks = [(cc * 128, 128) for cc in range(SEQ // 128) if cc % 2 == eta]
        pblocks = [(cc * 128, 128) for cc in range(SEQ // 128) if cc % 2 != eta]
        cfg = Cfg(seq=SEQ, t_loc=TLOC, tok_blocks=blocks)
        if cfg0 is None:
            cfg0 = cfg
        tokmap = np.concatenate([np.arange(gs, gs + ln) for gs, ln in blocks])
        ptokmap = np.concatenate([np.arange(gs, gs + ln) for gs, ln in pblocks])
        tokmaps.append((beta, tokmap))
        m = dict(base)
        m.update(cfg.host_consts())
        m["x_in"] = np.ascontiguousarray(
            np.concatenate([x[beta][tokmap], x[beta][ptokmap]], axis=0))
        in_maps.append(m)

    nc, _ = build_full(cfg0)
    res = run_bass_kernel_spmd(nc, in_maps, core_ids=list(range(NCORES)))
    out = np.zeros((4, SEQ, 1024), dtype=np.float32)
    for c in range(NCORES):
        beta, tokmap = tokmaps[c]
        out[beta][tokmap] = res.results[c]["y_out"]
    return out
